# version 26
# baseline (speedup 1.0000x reference)
import numpy as np
import ml_dtypes
import concourse.bass as bass
import concourse.mybir as mybir
from concourse.bass_utils import run_bass_kernel_spmd
from contextlib import ExitStack

F32 = mybir.dt.float32
BF16 = mybir.dt.bfloat16
AF = mybir.ActivationFunctionType
ALU = mybir.AluOpType
AX = mybir.AxisListType
BFNP = ml_dtypes.bfloat16

T = 2048
E = 2048
H = 16
HD = 128
DEX = 8192
NDC = 16
NTC = 16
HPG = 2
NG = H // HPG
NQTG = 8
NEC = 64
NQ = 4
SQRT_HD = float(HD) ** 0.5
LN_EPS = 1e-5
Q_EPS = 1e-8

ENGINES = ("T", "V", "A", "S", "G")


class Ctr:
    __slots__ = ("h", "n", "name")

    def __init__(self, h, name):
        self.h = h
        self.n = 0
        self.name = name


class Prog:
    def __init__(self, nc):
        self.nc = nc
        self.ops = []

    def op(self, eng, fn, waits=(), inc=None, dma=False, r=(), w=()):
        assert eng in ENGINES
        for (c, v) in waits:
            assert v <= c.n, f"wait on {c.name}>={v} but only {c.n} emitted"
        incs = []
        if inc is not None:
            amt = 16 if dma else 1
            inc.n += amt
            incs.append((inc, amt))
        self.ops.append((eng, fn, tuple(waits), tuple(incs),
                         frozenset(r), frozenset(w)))

    def emit_engine(self, eng_obj, name):
        lastw = {}
        dirty_r = set()
        dirty_w = set()
        for (e, fn, waits, incs, r, w) in self.ops:
            if e != name:
                continue
            for (c, v) in waits:
                if v <= 0:
                    continue
                if lastw.get(id(c), -1) < v:
                    eng_obj.wait_ge(c.h, v)
                    lastw[id(c)] = v
            if name != "T" and \
                    ((r & dirty_w) or (w & dirty_w) or (w & dirty_r)):
                eng_obj.drain()
                dirty_r.clear()
                dirty_w.clear()
            inst = fn(eng_obj)
            for (c, amt) in incs:
                inst.then_inc(c.h, amt)
            dirty_r |= r
            dirty_w |= w


def _flat2d(ap_flat, off, p, cols):
    return ap_flat[off:off + p * cols].rearrange("(p c) -> p c", p=p)


def build_nc():
    nc = bass.Bass("TRN2", target_bir_lowering=False)

    x_h = nc.dram_tensor("x", [T, E], F32, kind="ExternalInput").ap()
    wqkv_h = nc.dram_tensor("wqkvT", [NG * 6 * 128 * 2048], F32, kind="ExternalInput").ap()
    wproj_h = nc.dram_tensor("wprojT", [16 * 128 * 2048], F32, kind="ExternalInput").ap()
    wu_h = nc.dram_tensor("wuT", [NEC * 128 * 2048], F32, kind="ExternalInput").ap()
    wg_h = nc.dram_tensor("wgT", [NEC * 128 * 2048], F32, kind="ExternalInput").ap()
    wd_h = nc.dram_tensor("wdT", [256 * 128 * 512], F32, kind="ExternalInput").ap()
    bq_h = nc.dram_tensor("bq", [128, H], F32, kind="ExternalInput").ap()
    bk_h = nc.dram_tensor("bk", [128, H], F32, kind="ExternalInput").ap()
    bv_h = nc.dram_tensor("bv_rep", [NG * 128 * (HPG * HD)], F32, kind="ExternalInput").ap()
    bp_h = nc.dram_tensor("bproj_rep", [128 * 2048], F32, kind="ExternalInput").ap()
    out_h = nc.dram_tensor("out", [T, E], F32, kind="ExternalOutput").ap()

    attv_scr = nc.dram_tensor("attv_scr", [H * NQTG * 128 * 256], BF16, kind="Internal").ap()
    s_scr = nc.dram_tensor("s_scr", [NTC * 128 * 2048], BF16, kind="Internal").ap()
    tu_scr = nc.dram_tensor("tu_scr", [NEC * 128 * 2048], BF16, kind="Internal").ap()
    tg_scr = nc.dram_tensor("tg_scr", [NEC * 128 * 2048], BF16, kind="Internal").ap()
    td_scr = nc.dram_tensor("td_scr", [256 * 128 * 512], BF16, kind="Internal").ap()
    tiny_scr = nc.dram_tensor("tiny_scr", [128], F32, kind="Internal").ap()
    tiny_scr2 = nc.dram_tensor("tiny_scr2", [128], F32, kind="Internal").ap()

    ident_dram = nc.inline_tensor(np.eye(128, dtype=BFNP), name="identconst")

    cursor = nc.sbuf_top

    def alloc(name, shape, dt):
        nonlocal cursor
        esz = {F32: 4, BF16: 2}[dt]
        nb = esz
        for s in shape[1:]:
            nb *= s
        nb = (nb + 31) // 32 * 32
        cursor -= nb
        return nc.alloc_sbuf_tensor_at(name, shape, dt, offset=cursor)

    xn_fm_t = alloc("xn_fm", [128, NDC * T], BF16)
    xn_fm = xn_fm_t[:, :]
    xn_v = xn_fm.rearrange("p (dc t) -> p dc t", dc=NDC)
    ident = alloc("ident", [128, 128], BF16)
    ones_row = alloc("ones_row", [1, 128], F32)
    ones_bf = alloc("ones_bf", [1, 128], BF16)
    bq_s = alloc("bq_s", [128, H], F32)
    bk_s = alloc("bk_s", [128, H], F32)
    tnames = ("st_sum0 st_sum1 st_nm0 st_nm1 st_a0 st_a1 st_b0 st_b1 "
              "st_var0 st_var1 st_rec0 st_rec1 st_rstd0 st_rstd1 "
              "ngm0 ngm1 nms0 nms1 rsum0 rsum1 rsr0 rsr1 "
              "pw_u pw_g pw_d t11u t11g t11d tmp11 tmp11b "
              "cu_p ncu_p cg_p ncg_p cd_p ncd_p sun_p sgd_p").split()
    tt_ = {nm: alloc(nm, [128, 1], F32) for nm in tnames}
    pmax_ = [alloc("pmax0", [128, 4], F32), alloc("pmax1", [128, 4], F32)]
    trow = alloc("trow", [1, 128], F32)
    brow = alloc("brow", [1, 128], F32)
    partials = alloc("partials", [128, 192], F32)

    A0 = cursor
    q_buf, k_buf, v_buf = [None, None], [None, None], [None, None]
    for i in (0, 1):
        q_buf[i] = alloc(f"q_buf{i}", [128, HPG * T], BF16)
        k_buf[i] = alloc(f"k_buf{i}", [128, HPG * T], BF16)
        v_buf[i] = alloc(f"v_buf{i}", [128, NTC * HPG * HD], BF16)
    a_p3 = cursor
    cursor = A0
    wproj_bf_t = alloc("wproj_bf", [128, NDC * E], BF16)
    wproj_bf = wproj_bf_t[:, :]
    a_p4 = cursor
    cursor = A0
    hidden_t = alloc("hidden_q", [128, NEC * 512], BF16)
    hidden_q = hidden_t[:, :]
    cursor = min(a_p3, a_p4, cursor)

    B0 = cursor
    x_tile, xn_tm = [None, None], [None, None]
    for i in (0, 1):
        x_tile[i] = alloc(f"x_tile{i}", [128, E], F32)
        xn_tm[i] = alloc(f"xn_tm{i}", [128, E], BF16)
    sq_scr = alloc("sq_scr", [128, E], BF16)
    b_p1 = cursor
    cursor = B0
    wstage, w_bf = [None, None], [None, None]
    for i in (0, 1):
        wstage[i] = alloc(f"wstage{i}", [128, 1024], F32)
    for i in (0, 1):
        w_bf[i] = alloc(f"w_bf{i}", [128, 2048], BF16)
    p_tm_b = [None, None]
    p_tm_b[0] = alloc("p_tm0", [128, 2 * T], BF16)[:, :]
    p_tm_b[1] = alloc("p_tm1", [128, 2 * T], BF16)[:, :]
    p_pt_t = alloc("p_pt", [128, NTC * 256], BF16)
    p_pt = p_pt_t[:, :]
    av_evac = [None, None]
    bv_s = [None, None]
    for i in (0, 1):
        av_evac[i] = alloc(f"av_evac{i}", [128, 256], BF16)
    for i in (0, 1):
        bv_s[i] = alloc(f"bv_s{i}", [128, HPG * HD], F32)
    b_p3 = cursor
    cursor = B0
    wstage4, attv_pair, s_tile = [None, None], [None, None], [None, None]
    for i in (0, 1):
        wstage4[i] = alloc(f"wstage4_{i}", [128, 2048], F32)
    for i in (0, 1):
        attv_pair[i] = alloc(f"attv_pair{i}", [128, H * 256], BF16)
    s_tile[0] = s_tile[1] = alloc("s_tile0", [128, 2048], BF16)
    bproj_row = alloc("bproj_row", [1, 2048], BF16)
    b_p4 = cursor
    cursor = B0
    tu_b, tg_b, td_b, silu_b = [None, None], [None, None], [None, None], [None, None]
    g_tile, xs_strip, ss_strip, out_strip = [None, None], [None, None], [None, None], [None, None]
    for i in (0, 1):
        tu_b[i] = alloc(f"tu_b{i}", [128, 2048], BF16)
        tg_b[i] = alloc(f"tg_b{i}", [128, 2048], BF16)
    for i in (0, 1):
        td_b[i] = alloc(f"td_b{i}", [128, 1024], BF16)
        silu_b[i] = alloc(f"silu_b{i}", [128, 512], BF16)
        g_tile[i] = alloc(f"g_tile{i}", [128, 512], BF16)
        xs_strip[i] = alloc(f"xs_strip{i}", [128, 512], F32)
        ss_strip[i] = alloc(f"ss_strip{i}", [128, 512], BF16)
        out_strip[i] = alloc(f"out_strip{i}", [128, 512], F32)
    b_p6 = cursor
    cursor = min(b_p1, b_p3, b_p4, b_p6)

    qstage = [None, None]
    for i in (0, 1):
        qstage[i] = alloc(f"qstage{i}", [128, 2048], F32)
    abs_scr = alloc("abs_scr", [128, 2048], BF16)
    a_gt = abs_scr
    tq_out = [None, None]
    for i in (0, 1):
        tq_out[i] = alloc(f"tq_out{i}", [128, 2048], BF16)

    assert cursor > nc.sbuf_base + 1536, \
        f"SBUF overflow: cursor {cursor} base {nc.sbuf_base}"

    ps_big = nc.alloc_psum_tensor("ps_big", [128, 2048], F32)
    ps_pt = nc.alloc_psum_tensor("ps_pt", [128, 1024], BF16)
    ps_u = nc.alloc_psum_tensor("ps_u", [128, 512], F32)
    ps_g = nc.alloc_psum_tensor("ps_g", [128, 512], F32)
    ps_m = nc.alloc_psum_tensor("ps_m", [128, 512], F32)

    P = Prog(nc)
    op = P.op

    with ExitStack() as ctx:
        def sem(nm):
            return Ctr(ctx.enter_context(nc.semaphore(nm)), nm)

        QP = {q: [sem(f"d{q}{p}") for p in range(2)] for q in "SGA"}
        qix = {q: 0 for q in "SGA"}
        cT, cV, cA = sem("cT"), sem("cV"), sem("cA")
        cG = sem("cG")
        Z = (QP["S"][0], 0)

        def qdone(q):
            return tuple((c, c.n) for c in QP[q])

        def tmax(tokens):
            best = {}
            for (c, n) in tokens:
                if n > best.get(id(c), (None, -1))[1]:
                    best[id(c)] = (c, n)
            return tuple(best.values())

        def dma(q, dst, src, waits=(), r=(), w=()):
            ctr = QP[q][qix[q] % 2]
            qix[q] += 1
            prev = ctr.n
            op(q, lambda e, d=dst, s=src: e.dma_start(out=d, in_=s),
               waits=tuple(waits) + ((ctr, prev),), inc=ctr, dma=True,
               r=r, w=w)
            return (ctr, ctr.n)

        def vop(fn, waits=(), r=(), w=()):
            op("V", fn, waits=waits, inc=cV, r=r, w=w)
            return cV.n

        def aop(fn, waits=(), r=(), w=()):
            op("A", fn, waits=waits, inc=cA, r=r, w=w)
            return cA.n

        def top(fn, waits=(), inc=False, r=(), w=()):
            op("T", fn, waits=waits, inc=cT if inc else None, r=r, w=w)
            return cT.n

        def gop(fn, waits=(), r=(), w=()):
            op("G", fn, waits=waits, inc=cG, r=r, w=w)
            return cG.n

        dma("S", ident[:, :], ident_dram.ap()[:, :])
        dma("S", bq_s[:, :], bq_h)
        dma("S", bk_s[:, :], bk_h)
        P0_S = qdone("S")
        vop(lambda e: e.memset(ones_row[:, :], 1.0), w={"ones_row"})
        vop(lambda e: e.memset(ones_bf[:, :], 1.0), w={"ones_bf"})

        vdone_xt = [0, 0]
        adone_xt = [0, 0]
        pt_free_v = 0
        for tt in range(NTC):
            sl = tt % 2
            ss = tt % 2
            xt = f"xt{sl}"
            xt_ready = dma("G", x_tile[sl][:, :],
                           x_h[tt * 128:(tt + 1) * 128, :],
                           waits=[(cV, vdone_xt[sl]), (cA, adone_xt[sl])],
                           w={xt})
            vop(lambda e, sl=sl, ss=ss: e.reduce_sum(
                out=tt_[f"st_sum{ss}"][:, :], in_=x_tile[sl][:, :], axis=AX.X),
                waits=[xt_ready], r={xt}, w={f"st_sum{ss}"})
            a_sq = aop(lambda e, sl=sl, ss=ss: e.activation(
                out=sq_scr[:, :], in_=x_tile[sl][:, :], func=AF.Square,
                accum_out=tt_[f"st_a{ss}"][:, 0:1]),
                waits=[xt_ready], r={xt}, w={"sq_scr", f"st_a{ss}"})
            vop(lambda e, ss=ss: e.tensor_scalar(
                out=tt_[f"st_nm{ss}"][:, :], in0=tt_[f"st_sum{ss}"][:, :],
                scalar1=-1.0 / E, scalar2=None, op0=ALU.mult),
                r={f"st_sum{ss}"}, w={f"st_nm{ss}"})
            vop(lambda e, ss=ss: e.tensor_scalar(
                out=tt_[f"st_a{ss}"][:, :], in0=tt_[f"st_a{ss}"][:, :],
                scalar1=1.0 / E, scalar2=LN_EPS, op0=ALU.mult, op1=ALU.add),
                waits=[(cA, a_sq)], r={f"st_a{ss}"}, w={f"st_a{ss}"})
            vop(lambda e, ss=ss: e.tensor_tensor(
                out=tt_[f"st_b{ss}"][:, :], in0=tt_[f"st_nm{ss}"][:, :],
                in1=tt_[f"st_nm{ss}"][:, :], op=ALU.mult),
                r={f"st_nm{ss}"}, w={f"st_b{ss}"})
            vop(lambda e, ss=ss: e.tensor_sub(
                out=tt_[f"st_var{ss}"][:, :], in0=tt_[f"st_a{ss}"][:, :],
                in1=tt_[f"st_b{ss}"][:, :]),
                r={f"st_a{ss}", f"st_b{ss}"}, w={f"st_var{ss}"})
            v_rec = vop(lambda e, ss=ss: e.reciprocal(
                out=tt_[f"st_rec{ss}"][:, :], in_=tt_[f"st_var{ss}"][:, :]),
                r={f"st_var{ss}"}, w={f"st_rec{ss}"})
            a_rstd = aop(lambda e, ss=ss: e.activation(
                out=tt_[f"st_rstd{ss}"][:, :], in_=tt_[f"st_rec{ss}"][:, :],
                func=AF.Sqrt),
                waits=[(cV, v_rec)], r={f"st_rec{ss}"}, w={f"st_rstd{ss}"})
            v_xn = vop(lambda e, sl=sl, ss=ss: e.tensor_scalar(
                out=xn_tm[sl][:, :], in0=x_tile[sl][:, :],
                scalar1=tt_[f"st_nm{ss}"][:, 0:1],
                scalar2=tt_[f"st_rstd{ss}"][:, 0:1],
                op0=ALU.add, op1=ALU.mult),
                waits=[(cA, a_rstd)],
                r={xt, f"st_nm{ss}", f"st_rstd{ss}"}, w={f"xn_tm{sl}"})
            vdone_xt[sl] = v_xn
            adone_xt[sl] = a_sq
            for half in (0, 1):
                for i in range(8):
                    dc = half * 8 + i
                    top(lambda e, sl=sl, dc=dc, i=i: e.transpose(
                        out=ps_pt[:, i * 128:(i + 1) * 128],
                        in_=xn_tm[sl][:, dc * 128:(dc + 1) * 128],
                        identity=ident[:, :]),
                        waits=([(cV, v_xn), (cV, pt_free_v),
                                *P0_S] if i == 0 else ()),
                        inc=(i == 7), r={f"xn_tm{sl}", "ident"},
                        w={"bank4"})
                t_half = cT.n
                for j2 in (0, 1):
                    dc0 = half * 8 + j2 * 4
                    dst = xn_v[:, dc0:dc0 + 4, tt * 128:tt * 128 + 128]
                    pt_free_v = vop(
                        lambda e, dst=dst, j2=j2: e.tensor_copy(
                            out=dst, in_=ps_pt[:, j2 * 512:(j2 + 1) * 512]),
                        waits=[(cT, t_half)], r={"bank4"}, w={"xn_fm"})
        XN_V = cV.n
        P1_A = cA.n
        P1_T = cT.n
        bar13 = [(cV, XN_V), (cA, P1_A), (cT, P1_T)]

        a_absdone = [0, 0]
        q_vdone = [0, 0]
        g_wr = [Z, Z]
        qmode = {"load": "A", "store": "S"}
        tug_piece_done = {}
        tdq_piece_done = {}
        rt_free = [Z]

        def bcast(dst_pp, val11_name, tag):
            vb = vop(lambda e, val11_name=val11_name: e.tensor_scalar(
                out=brow[:, :], in0=ones_row[:, :],
                scalar1=tt_[val11_name][0:1, 0:1], scalar2=None, op0=ALU.mult),
                waits=[rt_free[0]],
                r={"ones_row", val11_name}, w={"brow"})
            d1 = dma("A", tiny_scr2.rearrange("(o p) -> o p", o=1),
                     brow[0:1, :], waits=[(cV, vb)], r={"brow"})
            d2 = dma("A", dst_pp[:, :],
                     tiny_scr2.rearrange("(p o) -> p o", p=128),
                     waits=[d1], w={tag})
            rt_free[0] = d2

        qloads = []
        qcomps = []
        qd_in = {}
        fins = {}
        srcs = {}
        dsts = {}
        wmeta = [("u", wu_h, tu_scr, 0), ("g", wg_h, tg_scr, 64),
                 ("d", wd_h, td_scr, 128)]
        for (wn, src, dst, pbase) in wmeta:
            def scale_load(c, wn=wn, src=src):
                sl = c % 2
                qd_in[(wn, c)] = dma(
                    "A", qstage[sl][:, :],
                    _flat2d(src, c * 262144, 128, 2048),
                    waits=[(cA, a_absdone[sl]), (cV, q_vdone[0]),
                           (cV, q_vdone[1])],
                    w={f"qstage{sl}"})

            def scale_comp(c, wn=wn, pbase=pbase):
                sl = c % 2
                a_absdone[sl] = aop(
                    lambda e, sl=sl, pc=pbase + c: e.activation(
                        out=abs_scr[:, :], in_=qstage[sl][:, :], func=AF.Abs,
                        accum_out=partials[:, pc:pc + 1]),
                    waits=[qd_in[(wn, c)]], r={f"qstage{sl}"},
                    w={"abs_scr", "partials"})
            for c in range(64):
                qloads.append(lambda c=c, f=scale_load: f(c))
                qcomps.append(lambda c=c, f=scale_comp: f(c))

            def finalize(wn=wn, pbase=pbase):
                v1 = vop(lambda e: e.reduce_sum(
                    out=tt_[f"pw_{wn}"][:, :],
                    in_=partials[:, pbase:pbase + 64], axis=AX.X),
                    waits=[(cA, a_absdone[0]), (cA, a_absdone[1])],
                    r={"partials"}, w={f"pw_{wn}"})
                d1 = dma("A", tiny_scr.rearrange("(p o) -> p o", p=128),
                         tt_[f"pw_{wn}"][:, :],
                         waits=[(cV, v1), rt_free[0]], r={f"pw_{wn}"})
                d2 = dma("A", trow[0:1, :],
                         tiny_scr.rearrange("(o p) -> o p", o=1),
                         waits=[d1], w={"trow"})
                rt_free[0] = d2
                vop(lambda e: e.reduce_sum(
                    out=tt_[f"t11{wn}"][0:1, :], in_=trow[:, :], axis=AX.X),
                    waits=[d2], r={"trow"}, w={f"t11{wn}"})
                vop(lambda e: e.tensor_scalar(
                    out=tt_[f"t11{wn}"][0:1, :], in0=tt_[f"t11{wn}"][0:1, :],
                    scalar1=1.0 / float(DEX * E), scalar2=Q_EPS,
                    op0=ALU.mult, op1=ALU.add),
                    r={f"t11{wn}"}, w={f"t11{wn}"})
                vop(lambda e: e.tensor_scalar(
                    out=tt_["tmp11"][0:1, :], in0=tt_[f"t11{wn}"][0:1, :],
                    scalar1=0.5, scalar2=None, op0=ALU.mult),
                    r={f"t11{wn}"}, w={"tmp11"})
                vop(lambda e: e.tensor_scalar(
                    out=tt_["tmp11b"][0:1, :], in0=tt_["tmp11"][0:1, :],
                    scalar1=-1.0, scalar2=None, op0=ALU.mult),
                    r={"tmp11"}, w={"tmp11b"})
                bcast(tt_[f"c{wn}_p"], "tmp11", f"c{wn}_p")
                bcast(tt_[f"nc{wn}_p"], "tmp11b", f"nc{wn}_p")
                if wn == "u":
                    bcast(tt_["sun_p"], "t11u", "sun_p")
                if wn == "d":
                    vop(lambda e: e.tensor_tensor(
                        out=tt_["tmp11"][0:1, :], in0=tt_["t11g"][0:1, :],
                        in1=tt_["t11d"][0:1, :], op=ALU.mult),
                        r={"t11g", "t11d"}, w={"tmp11"})
                    bcast(tt_["sgd_p"], "tmp11", "sgd_p")
                finalize.bc_done = qdone("A")
            finalize.bc_done = ()
            qloads.append(lambda: None)
            qcomps.append(finalize)
            fins[wn] = finalize
            srcs[wn] = src
            dsts[wn] = dst

        def quant_load(wn, c, sl):
            qd_in[(wn + "q", c)] = dma(
                qmode["load"], qstage[sl][:, :],
                _flat2d(srcs[wn], c * 262144, 128, 2048),
                waits=[(cV, q_vdone[sl]), (cA, a_absdone[0]),
                       (cA, a_absdone[1])],
                w={f"qstage{sl}"})

        def quant_comp(wn, c, sl):
            fin = fins[wn]
            d_in = qd_in[(wn + "q", c)]
            vop(lambda e, sl=sl, wn=wn: e.tensor_scalar(
                out=tq_out[sl][:, :], in0=qstage[sl][:, :],
                scalar1=tt_[f"nc{wn}_p"][:, 0:1], scalar2=None,
                op0=ALU.is_lt),
                waits=[d_in, *fin.bc_done, g_wr[sl]],
                r={f"qstage{sl}", f"nc{wn}_p"}, w={f"tq_out{sl}"})
            q_vdone[sl] = vop(lambda e, sl=sl, wn=wn: e.scalar_tensor_tensor(
                out=tq_out[sl][:, :], in0=qstage[sl][:, :],
                scalar=tt_[f"c{wn}_p"][:, 0:1], in1=tq_out[sl][:, :],
                op0=ALU.is_gt, op1=ALU.subtract),
                r={f"qstage{sl}", f"c{wn}_p", f"tq_out{sl}"},
                w={f"tq_out{sl}"})
            g_wr[sl] = dma(qmode["store"],
                           _flat2d(dsts[wn], c * 262144, 128, 2048),
                           tq_out[sl][:, :],
                           waits=[(cV, q_vdone[sl])], r={f"tq_out{sl}"})
            if wn == "d":
                tdq_piece_done[c] = g_wr[sl]
            else:
                tug_piece_done[(wn, c)] = g_wr[sl]

        qorder = []
        for c in range(64):
            qorder += [("u", c), ("g", c)]
        qorder += [("d", c) for c in range(64)]
        for gi, (wn, c) in enumerate(qorder):
            qloads.append(lambda wn=wn, c=c, sl=gi % 2: quant_load(wn, c, sl))
            qcomps.append(lambda wn=wn, c=c, sl=gi % 2: quant_comp(wn, c, sl))
        qload_it = iter(qloads)
        qcomp_it = iter(qcomps)
        qstate = {"loads_done": 0, "comps_done": 0,
                  "nl": len(qloads), "ncmp": len(qcomps)}

        def run_qloads(n):
            for _ in range(n):
                if qstate["loads_done"] >= qstate["comps_done"] + 2:
                    return
                stp = next(qload_it, None)
                if stp is None:
                    return
                stp()
                qstate["loads_done"] += 1

        def run_qpairs(n):
            for _ in range(n):
                run_qloads(1)
                run_qcomps(1)

        def run_qcomps(n):
            for _ in range(n):
                if qstate["comps_done"] >= qstate["loads_done"]:
                    return
                stp = next(qcomp_it, None)
                if stp is None:
                    return
                stp()
                qstate["comps_done"] += 1

        s_wstage_free = [0, 0]
        s_wbf_free = [0, 0]
        evac_ps = {0: 0, 1: 0}
        evac_ps_v = {0: 0, 1: 0}
        bv_free = [0, 0]
        groupbuf_free = [0, 0]
        attv_block_done = {}
        qcomp_per_slot = 3
        st_ = {"ptm_free_t": [0, 0], "pt_evac_v": 0, "pt_evac_a": 0,
               "ppt_free": 0, "psm_free": 0,
               "av_i": 0, "av_dma": [Z, Z], "scores_free": 0,
               "sidx": 0, "pend": None}

        def production_steps(g):
            gset = g % 2
            base = g * 6 * 128 * 2048
            piece_v = {}
            bv_ready = [Z]

            def load_bv():
                def do():
                    bv_ready[0] = dma(
                        "S", bv_s[gset][:, :],
                        _flat2d(bv_h, g * 128 * HPG * HD, 128, HPG * HD),
                        waits=bar13 + [(cV, bv_free[gset])], w={f"bv{gset}"})
                return do

            def load_piece(pi):
                def do():
                    sl = pi % 2
                    for hf in (0, 1):
                        d_in = dma(
                            "S", wstage[hf][:, :],
                            _flat2d(wqkv_h, base + pi * 262144,
                                    128, 2048)[:, hf * 1024:(hf + 1) * 1024],
                            waits=bar13 + [(cV, s_wstage_free[hf])],
                            w={f"wstage{hf}"})
                        piece_v[pi] = vop(
                            lambda e, sl=sl, hf=hf: e.tensor_copy(
                                out=w_bf[sl][:, hf * 1024:(hf + 1) * 1024],
                                in_=wstage[hf][:, :]),
                            waits=[d_in, (cT, s_wbf_free[sl])],
                            r={f"wstage{hf}"}, w={f"w_bf{sl}"})
                        s_wstage_free[hf] = piece_v[pi]
                return do

            def qk_fill(pi, which, hh, tg):
                def do():
                    sl = pi % 2
                    par = tg % 2
                    ps = ps_u if par == 0 else ps_g
                    for dc in range(NDC):
                        top(lambda e, sl=sl, dc=dc, tg=tg, ps=ps: e.matmul(
                            out=ps[:, :],
                            lhsT=w_bf[sl][:, dc * 128:(dc + 1) * 128],
                            rhs=xn_fm[:, dc * T + tg * 512:dc * T + tg * 512 + 512],
                            start=(dc == 0), stop=(dc == NDC - 1)),
                            waits=([(cV, piece_v[pi]), (cV, XN_V),
                                    (cA, evac_ps[par]),
                                    (cV, evac_ps_v[par])] if dc == 0 else ()),
                            inc=(dc == NDC - 1),
                            r={f"w_bf{pi % 2}", "xn_fm"},
                            w={f"bank{5 + par}"})
                    t_done = cT.n
                    s_wbf_free[sl] = t_done
                    dstb = q_buf if which == "q" else k_buf
                    bias = bq_s if which == "q" else bk_s
                    h_abs = g * HPG + hh
                    evac_ps[par] = aop(
                        lambda e, hh=hh, tg=tg, ps=ps, dstb=dstb,
                        bias=bias, h_abs=h_abs: e.activation(
                            out=dstb[gset][:, hh * T + tg * 512:
                                           hh * T + tg * 512 + 512],
                            in_=ps[:, :], func=AF.Identity,
                            bias=bias[:, h_abs:h_abs + 1], scale=1.0),
                        waits=[(cT, t_done), (cT, groupbuf_free[gset]),
                               *P0_S],
                        r={f"bank{5 + par}"}, w={f"qk{gset}"})
                return do

            def v_fill(tc):
                def do():
                    par = tc % 2
                    ps = ps_u if par == 0 else ps_g
                    for dc in range(NDC):
                        pi = 4 if dc < 8 else 5
                        sl = pi % 2
                        dcl = dc % 8
                        top(lambda e, sl=sl, dcl=dcl, dc=dc, tc=tc,
                            ps=ps: e.matmul(
                                out=ps[:, 0:HPG * HD],
                                lhsT=xn_fm[:, dc * T + tc * 128:
                                           dc * T + tc * 128 + 128],
                                rhs=w_bf[sl][:, dcl * (HPG * HD):
                                             (dcl + 1) * (HPG * HD)],
                                start=(dc == 0), stop=(dc == NDC - 1)),
                            waits=([(cV, piece_v[4]), (cV, piece_v[5]),
                                    (cA, evac_ps[par]),
                                    (cV, evac_ps_v[par])] if dc == 0 else ()),
                            inc=(dc == NDC - 1),
                            r={"w_bf0", "w_bf1", "xn_fm"},
                            w={f"bank{5 + par}"})
                    t_done = cT.n
                    s_wbf_free[0] = max(s_wbf_free[0], t_done)
                    s_wbf_free[1] = max(s_wbf_free[1], t_done)
                    evac_ps_v[par] = vop(
                        lambda e, tc=tc, ps=ps: e.tensor_tensor(
                            out=v_buf[gset][:, tc * (HPG * HD):
                                            (tc + 1) * (HPG * HD)],
                            in0=ps[:, 0:HPG * HD], in1=bv_s[gset][:, :],
                            op=ALU.add),
                        waits=[(cT, t_done), (cT, groupbuf_free[gset]),
                               bv_ready[0]],
                        r={f"bv{gset}", f"bank{5 + par}"}, w={f"vb{gset}"})
                return do

            steps = [load_bv(), load_piece(0), load_piece(1)]
            for tg in range(4):
                steps.append(qk_fill(0, "q", 0, tg))
            steps.append(load_piece(2))
            for tg in range(4):
                steps.append(qk_fill(1, "q", 1, tg))
            steps.append(load_piece(3))
            for tg in range(4):
                steps.append(qk_fill(2, "k", 0, tg))
            steps.append(load_piece(4))
            for tg in range(4):
                steps.append(qk_fill(3, "k", 1, tg))
            steps.append(load_piece(5))
            for tc in range(NTC):
                steps.append(v_fill(tc))
            return steps

        def attention_softmax(g, hh, qtg, sidx):
            gset = g % 2
            pb = p_tm_b[sidx % 2]
            for qt in (0, 1):
                q0 = qtg * 256 + qt * 128
                ss = qt
                for kc4 in range(4):
                    t_sc = top(lambda e, gset=gset, hh=hh, q0=q0,
                               kc4=kc4: e.matmul(
                        out=ps_big[:, kc4 * 512:(kc4 + 1) * 512],
                        lhsT=q_buf[gset][:, hh * T + q0:hh * T + q0 + 128],
                        rhs=k_buf[gset][:, hh * T + kc4 * 512:
                                        hh * T + kc4 * 512 + 512],
                        start=True, stop=True),
                        waits=([(cA, st_["scores_free"])] if kc4 == 0 else ()),
                        inc=(kc4 == 3), r={f"qk{gset}"}, w={f"bank{kc4}"})
                vop(lambda e, ss=ss: e.reduce_max(
                    out=tt_[f"ngm{ss}"][:, :], in_=ps_big[:, :], axis=AX.X,
                    negate=True),
                    waits=[(cT, t_sc)],
                    r={"bank0", "bank1", "bank2", "bank3"}, w={f"ngm{ss}"})
                v_nms = vop(lambda e, ss=ss: e.tensor_scalar(
                    out=tt_[f"nms{ss}"][:, :], in0=tt_[f"ngm{ss}"][:, :],
                    scalar1=SQRT_HD, scalar2=None, op0=ALU.mult),
                    r={f"ngm{ss}"}, w={f"nms{ss}"})
                st_["scores_free"] = aop(lambda e, pb=pb, qt=qt,
                                         ss=ss: e.activation(
                    out=pb[:, qt * T:(qt + 1) * T], in_=ps_big[:, :],
                    func=AF.Exp, bias=tt_[f"nms{ss}"][:, 0:1], scale=SQRT_HD,
                    accum_out=tt_[f"rsum{ss}"][:, 0:1]),
                    waits=[(cV, v_nms), (cT, st_["ptm_free_t"][sidx % 2])],
                    r={f"nms{ss}", "bank0", "bank1", "bank2", "bank3"},
                    w={f"p_tm{sidx % 2}", f"rsum{ss}"})
                v_rec = vop(lambda e, ss=ss: e.reciprocal(
                    out=tt_[f"rsr{ss}"][:, :], in_=tt_[f"rsum{ss}"][:, :]),
                    waits=[(cA, st_["scores_free"])],
                    r={f"rsum{ss}"}, w={f"rsr{ss}"})
                a_norm0 = aop(lambda e, pb=pb, qt=qt, ss=ss: e.activation(
                    out=pb[:, qt * T:(qt + 1) * T],
                    in_=pb[:, qt * T:(qt + 1) * T],
                    func=AF.Identity, bias=0.0,
                    scale=tt_[f"rsr{ss}"][:, 0:1]),
                    waits=[(cV, v_rec)],
                    r={f"rsr{ss}", f"p_tm{sidx % 2}"},
                    w={f"p_tm{sidx % 2}"})
            return (g, hh, qtg, sidx, a_norm0)

        def attention_tail(ctx):
            (g, hh, qtg, sidx, a_norm) = ctx
            gset = g % 2
            h_abs = g * HPG + hh
            pb = p_tm_b[sidx % 2]
            ev_v = st_["pt_evac_v"]
            ev_a = st_["pt_evac_a"]
            for rnd in range(4):
                for s_i in range(8):
                    kc = rnd * 4 + s_i // 2
                    qt = s_i % 2
                    top(lambda e, pb=pb, kc=kc, qt=qt, s_i=s_i: e.transpose(
                        out=ps_pt[:, s_i * 128:(s_i + 1) * 128],
                        in_=pb[:, qt * T + kc * 128:qt * T + kc * 128 + 128],
                        identity=ident[:, :]),
                        waits=([(cA, a_norm), (cV, ev_v), (cA, ev_a)]
                               if s_i == 0 else ()),
                        inc=(s_i == 7), r={f"p_tm{sidx % 2}", "ident"},
                        w={"bank4"})
                t_tr = cT.n
                dstv = p_pt.rearrange("p (kc q t) -> p kc q t", kc=NTC, q=2)
                if rnd % 2 == 0:
                    ev_v = vop(lambda e, rnd=rnd, dstv=dstv: e.tensor_copy(
                        out=dstv[:, rnd * 4:(rnd + 1) * 4, :, :],
                        in_=ps_pt[:, :]),
                        waits=[(cT, t_tr), (cT, st_["ppt_free"])],
                        r={"bank4"}, w={"p_pt"})
                else:
                    ev_a = aop(lambda e, rnd=rnd, dstv=dstv: e.activation(
                        out=dstv[:, rnd * 4:(rnd + 1) * 4, :, :],
                        in_=ps_pt[:, :], func=AF.Copy),
                        waits=[(cT, t_tr), (cT, st_["ppt_free"])],
                        r={"bank4"}, w={"p_pt"})
            st_["pt_evac_v"] = ev_v
            st_["pt_evac_a"] = ev_a
            st_["ptm_free_t"][sidx % 2] = cT.n
            for kc in range(NTC):
                top(lambda e, gset=gset, hh=hh, kc=kc: e.matmul(
                    out=ps_m[:, 0:256],
                    lhsT=v_buf[gset][:, kc * 256 + hh * 128:
                                     kc * 256 + hh * 128 + 128],
                    rhs=p_pt[:, kc * 256:(kc + 1) * 256],
                    start=(kc == 0), stop=(kc == NTC - 1)),
                    waits=([(cV, ev_v), (cA, ev_a),
                            (cA, st_["psm_free"])]
                           if kc == 0 else ()),
                    inc=(kc == NTC - 1), r={f"vb{gset}", "p_pt"},
                    w={"bank7"})
            t_av = cT.n
            st_["ppt_free"] = t_av
            avsl = st_["av_i"] % 2
            v_ave = aop(lambda e, avsl=avsl: e.activation(
                out=av_evac[avsl][:, :], in_=ps_m[:, 0:256], func=AF.Copy),
                waits=[(cT, t_av), st_["av_dma"][avsl]],
                r={"bank7"}, w={f"av{avsl}"})
            st_["psm_free"] = v_ave
            avdst = attv_scr.rearrange("(q p h t) -> q p h t", q=NQTG,
                                       p=128, h=H)[qtg, :, h_abs, :]
            st_["av_dma"][avsl] = dma(
                "G", avdst,
                av_evac[avsl][:, :], waits=[(cA, v_ave)], r={f"av{avsl}"})
            attv_block_done[(h_abs, qtg)] = st_["av_dma"][avsl]
            st_["av_i"] += 1
            if (hh, qtg) == (HPG - 1, NQTG - 1):
                groupbuf_free[g % 2] = cT.n

        prev_g = None
        nslots_total = NG * HPG * NQTG
        qper_slot = 3
        for g in range(NG + 1):
            steps = production_steps(g) if g < NG else []
            if prev_g is None:
                for stp in steps:
                    stp()
            else:
                slots = [(hh, qtg) for hh in range(HPG)
                         for qtg in range(NQTG)]
                per = (len(steps) + len(slots) - 2) // max(len(slots) - 1, 1)
                si = 0
                for slot_i, (hh, qtg) in enumerate(slots):
                    if slot_i > 0:
                        for _ in range(per):
                            if si < len(steps):
                                steps[si]()
                                si += 1
                    run_qloads(2)
                    ctx = attention_softmax(prev_g, hh, qtg, st_["sidx"])
                    st_["sidx"] += 1
                    run_qcomps(2)
                    if st_["pend"] is not None:
                        attention_tail(st_["pend"])
                    st_["pend"] = ctx
                    run_qpairs(1)
                while si < len(steps):
                    steps[si]()
                    si += 1
                bv_free[prev_g % 2] = cV.n
            prev_g = g if g < NG else prev_g
        if st_["pend"] is not None:
            attention_tail(st_["pend"])
            st_["pend"] = None
        P3_T, P3_V, P3_A = cT.n, cV.n, cA.n
        P3_G = cG.n

        qmode["load"] = "S"
        qmode["store"] = "G"
        bar34 = [(cT, P3_T), (cV, P3_V), (cA, P3_A), *qdone("G")]
        bp_stage = dma("S", wstage4[0][0:1, :],
                       bp_h[0:2048].rearrange("(o c) -> o c", o=1),
                       waits=bar34, w={"wstage40"})
        bp_cast = vop(lambda e: e.tensor_copy(
            out=bproj_row[0:1, :], in_=wstage4[0][0:1, :]),
            waits=[bp_stage, (cT, P3_T)],
            r={"wstage40"}, w={"bproj_row"})
        wp_stage_free = [bp_cast, 0]
        wp_cast = 0
        for dc in range(NDC):
            sl = dc % 2
            d_in = dma("S", wstage4[sl][:, :],
                       _flat2d(wproj_h, dc * 262144, 128, 2048),
                       waits=bar34 + [(cV, wp_stage_free[sl])],
                       w={f"wstage4{sl}"})
            wp_cast = vop(lambda e, sl=sl, dc=dc: e.tensor_copy(
                out=wproj_bf[:, dc * T:(dc + 1) * T], in_=wstage4[sl][:, :]),
                waits=[d_in, (cT, P3_T)],
                r={f"wstage4{sl}"}, w={"wproj_bf"})
            wp_stage_free[sl] = wp_cast
        WP_V = wp_cast

        s_dma = [Z, Z]
        s_wr = {}
        pair_T = [P3_T, P3_T]
        psum_free_a = [P3_A, P3_A, P3_A, P3_A]
        for pair in range(NQTG):
            psl = pair % 2
            blk = tmax(attv_block_done[(h, pair)] for h in range(H))
            last_d = dma(
                "G", attv_pair[psl][:, :],
                _flat2d(attv_scr, pair * 128 * H * 256, 128, H * 256),
                waits=[*blk, (cT, pair_T[psl]), (cV, P3_V)],
                w={f"avp{psl}"})
            run_qpairs(10)
            for tcl in (0, 1):
                tc = pair * 2 + tcl
                ssl = 0
                for eg in range(4):
                    bank = (tc * 4 + eg) % 4
                    top(lambda e, eg=eg, bank=bank: e.matmul(
                        out=ps_big[:, bank * 512:(bank + 1) * 512],
                        lhsT=ones_bf[0:1, :],
                        rhs=bproj_row[0:1, eg * 512:(eg + 1) * 512],
                        start=True, stop=False),
                        waits=[last_d, (cV, WP_V),
                               (cA, psum_free_a[bank]), (cA, P3_A)],
                        r={"ones_bf", "bproj_row"}, w={f"bank{bank}"})
                    for dc in range(NDC):
                        top(lambda e, psl=psl, tcl=tcl, eg=eg, dc=dc,
                            bank=bank: e.matmul(
                                out=ps_big[:, bank * 512:(bank + 1) * 512],
                                lhsT=attv_pair[psl][:, dc * 256 + tcl * 128:
                                                    dc * 256 + tcl * 128 + 128],
                                rhs=wproj_bf[:, dc * T + eg * 512:
                                             dc * T + eg * 512 + 512],
                                start=False, stop=(dc == NDC - 1)),
                            inc=(dc == NDC - 1),
                            r={f"avp{psl}", "wproj_bf"}, w={f"bank{bank}"})
                    t_p = cT.n
                    psum_free_a[bank] = aop(
                        lambda e, ssl=ssl, eg=eg, bank=bank: e.activation(
                            out=s_tile[ssl][:, eg * 512:(eg + 1) * 512],
                            in_=ps_big[:, bank * 512:(bank + 1) * 512],
                            func=AF.Copy),
                        waits=[(cT, t_p), s_dma[ssl]],
                        r={f"bank{bank}"}, w={f"s_tile{ssl}"})
                s_dma[ssl] = dma(
                    "G", _flat2d(s_scr, tc * 262144, 128, 2048),
                    s_tile[ssl][:, :],
                    waits=[(cA, psum_free_a[(tc * 4 + 3) % 4])],
                    r={f"s_tile{ssl}"})
                s_wr[tc] = s_dma[ssl]
            pair_T[psl] = cT.n
        P4_T, P4_V, P4_A = cT.n, cV.n, cA.n
        P4_G = qdone("G")

        qmode["load"] = "G"
        qmode["store"] = "G"
        bar46 = [(cT, P4_T), (cV, WP_V), *P4_G]
        tu_dma = [Z, Z]
        tg_dma = [Z, Z]
        tu_cons = [0, 0]
        td_dma = [None, None]
        td_cons = [0, 0]
        silu_free = [0, 0]
        bankg = [(cA, psum_free_a[0]), (cA, psum_free_a[1]),
                 (cA, psum_free_a[2]), (cA, psum_free_a[3]),
                 (cA, P3_A), (cA, P3_A), (cA, P3_A), (cA, P3_A)]
        gx_dma = [Z, Z]
        gs_dma = [Z, Z]
        gt_cons = [0, 0]
        strip_v = [0, 0]
        out_dma = [Z, Z]
        psf32 = ps_pt[:, :].bitcast(F32)

        def bank(b):
            if b < 4:
                return ps_big[:, b * 512:(b + 1) * 512]
            return [psf32, ps_u, ps_g, ps_m][b - 4][:, 0:512] \
                if b > 4 else psf32[:, 0:512]

        while qstate["comps_done"] < qstate["ncmp"] and False:
            pass
        first6 = True
        for Q in range(NQ):
            for ec in range(NEC):
                sl = ec % 2
                run_qpairs(2)
                tu_dma[sl] = dma("S", tu_b[sl][:, :],
                                 _flat2d(tu_scr, ec * 262144, 128, 2048),
                                 waits=(bar46 if first6 else []) +
                                 [tug_piece_done[("u", ec)],
                                  (cT, tu_cons[sl])],
                                 w={f"tu{sl}"})
                tg_dma[sl] = dma("A", tg_b[sl][:, :],
                                 _flat2d(tg_scr, ec * 262144, 128, 2048),
                                 waits=(bar46 if first6 else []) +
                                 [tug_piece_done[("g", ec)],
                                  (cT, tu_cons[sl])],
                                 w={f"tg{sl}"})
                first6 = False
                par = ec % 2
                upb = ps_big[:, par * 512:(par + 1) * 512]
                gtb = ps_big[:, (2 + par) * 512:(3 + par) * 512]
                for dc in range(NDC):
                    top(lambda e, sl=sl, dc=dc, Q=Q, upb=upb: e.matmul(
                        out=upb,
                        lhsT=tu_b[sl][:, dc * 128:(dc + 1) * 128],
                        rhs=xn_fm[:, dc * T + Q * 512:dc * T + Q * 512 + 512],
                        start=(dc == 0), stop=(dc == NDC - 1)),
                        waits=([tu_dma[sl], bankg[par]] if dc == 0 else ()),
                        inc=(dc == NDC - 1), r={f"tu{sl}", "xn_fm"},
                        w={f"bank{par}"})
                t_up = cT.n
                for dc in range(NDC):
                    top(lambda e, sl=sl, dc=dc, Q=Q, gtb=gtb: e.matmul(
                        out=gtb,
                        lhsT=tg_b[sl][:, dc * 128:(dc + 1) * 128],
                        rhs=xn_fm[:, dc * T + Q * 512:dc * T + Q * 512 + 512],
                        start=(dc == 0), stop=(dc == NDC - 1)),
                        waits=([tg_dma[sl], bankg[2 + par]] if dc == 0
                               else ()),
                        inc=(dc == NDC - 1), r={f"tg{sl}", "xn_fm"},
                        w={f"bank{2 + par}"})
                t_gt = cT.n
                tu_cons[sl] = t_gt
                ssl = ec % 2
                a_silu = aop(lambda e, ssl=ssl, upb=upb: e.activation(
                    out=silu_b[ssl][:, :], in_=upb, func=AF.Silu,
                    bias=0.0, scale=tt_["sun_p"][:, 0:1]),
                    waits=[(cT, t_up), (cV, silu_free[ssl])],
                    r={"sun_p", f"bank{par}"}, w={f"silu{ssl}"})
                bankg[par] = (cA, a_silu)
                silu_free[ssl] = vop(
                    lambda e, ssl=ssl, ec=ec, gtb=gtb: e.scalar_tensor_tensor(
                        out=hidden_q[:, ec * 512:(ec + 1) * 512],
                        in0=gtb, scalar=0.0, in1=silu_b[ssl][:, :],
                        op0=ALU.add, op1=ALU.mult),
                    waits=[(cT, t_gt), (cA, a_silu)],
                    r={f"silu{ssl}", f"bank{2 + par}"}, w={"hidden_q"})
                bankg[2 + par] = (cV, silu_free[ssl])
            HID_V = cV.n
            for dmh in range(2):
                for ec in range(NEC):
                    sl = ec % 2
                    if ec % 8 == 0:
                        run_qloads(1)
                        run_qcomps(1)
                    qq = "S" if sl == 0 else "A"
                    td_dma[sl] = dma(
                        qq, td_b[sl][:, :],
                        _flat2d(td_scr, ec * 262144, 128, 2048)[
                            :, dmh * 1024:(dmh + 1) * 1024],
                        waits=[tdq_piece_done[ec],
                               (cT, td_cons[sl]), (cT, P4_T)],
                        w={f"td{sl}"})
                    for tc in range(4):
                        for d2 in range(2):
                            b = tc * 2 + d2
                            wl = []
                            if tc == 0 and d2 == 0:
                                wl.append(td_dma[sl])
                            if ec == 0:
                                wl += [(cV, HID_V), bankg[b]]
                            top(lambda e, sl=sl, ec=ec, tc=tc, d2=d2,
                                b=b: e.matmul(
                                    out=bank(b),
                                    lhsT=hidden_q[:, ec * 512 + tc * 128:
                                                  ec * 512 + tc * 128 + 128],
                                    rhs=td_b[sl][:, d2 * 512:(d2 + 1) * 512],
                                    start=(ec == 0), stop=(ec == NEC - 1)),
                                waits=wl,
                                inc=(b == 7), r={f"td{sl}", "hidden_q"})
                    td_cons[sl] = cT.n
                t_dn = cT.n
                for tc in range(4):
                    for d2 in range(2):
                        b = tc * 2 + d2
                        dmg = dmh * 2 + d2
                        gsl = b % 2
                        tglob = Q * 4 + tc
                        gx_dma[gsl] = dma(
                            "G", xs_strip[gsl][:, :],
                            x_h[tglob * 128:(tglob + 1) * 128,
                                dmg * 512:(dmg + 1) * 512],
                            waits=[(cV, strip_v[gsl]), (cT, P4_T)],
                            w={f"xs{gsl}"})
                        gs_dma[gsl] = dma(
                            "G", ss_strip[gsl][:, :],
                            _flat2d(s_scr, tglob * 262144, 128, 2048)[
                                :, dmg * 512:(dmg + 1) * 512],
                            waits=[(cV, strip_v[gsl]), s_wr[tglob],
                                   (cT, P4_T)],
                            w={f"ssb{gsl}"})
                        a_g = aop(lambda e, gsl=gsl, b=b: e.activation(
                            out=g_tile[gsl][:, :],
                            in_=bank(b),
                            func=AF.Silu, bias=0.0,
                            scale=tt_["sgd_p"][:, 0:1]),
                            waits=[(cT, t_dn), (cV, gt_cons[gsl])],
                            r={"sgd_p", f"bank{b}"}, w={f"g{gsl}"})
                        bankg[b] = (cA, a_g)
                        v_m = vop(lambda e, gsl=gsl: e.tensor_tensor(
                            out=out_strip[gsl][:, :],
                            in0=ss_strip[gsl][:, :],
                            in1=g_tile[gsl][:, :], op=ALU.mult),
                            waits=[(cA, a_g), gs_dma[gsl],
                                   out_dma[gsl]],
                            r={f"ssb{gsl}", f"g{gsl}"}, w={f"os{gsl}"})
                        gt_cons[gsl] = v_m
                        v_o = vop(lambda e, gsl=gsl: e.tensor_tensor(
                            out=out_strip[gsl][:, :],
                            in0=out_strip[gsl][:, :],
                            in1=xs_strip[gsl][:, :], op=ALU.add),
                            waits=[gx_dma[gsl]],
                            r={f"os{gsl}", f"xs{gsl}"}, w={f"os{gsl}"})
                        strip_v[gsl] = v_o
                        out_dma[gsl] = dma(
                            "G", out_h[tglob * 128:(tglob + 1) * 128,
                                       dmg * 512:(dmg + 1) * 512],
                            out_strip[gsl][:, :],
                            waits=[(cV, v_o)], r={f"os{gsl}"})

        op("G", lambda e: e.nop(),
           waits=list(qdone("G")) + list(qdone("S")) + list(qdone("A")))

        with nc.Block() as block:
            block.tensor(lambda e: P.emit_engine(e, "T"))
            block.vector(lambda e: P.emit_engine(e, "V"))
            block.scalar(lambda e: P.emit_engine(e, "A"))
            block.sync(lambda e: P.emit_engine(e, "S"))
            block.gpsimd(lambda e: P.emit_engine(e, "G"))

    return nc


_NC_CACHE = {}


def _prep_weights(Wqkv, bqkv, Wproj, bproj, Wu, Wg, Wd):
    Wqkv = np.ascontiguousarray(np.asarray(Wqkv, dtype=np.float32))
    bqkv = np.ascontiguousarray(np.asarray(bqkv, dtype=np.float32))
    Wproj = np.ascontiguousarray(np.asarray(Wproj, dtype=np.float32))
    bproj = np.ascontiguousarray(np.asarray(bproj, dtype=np.float32))
    Wu = np.ascontiguousarray(np.asarray(Wu, dtype=np.float32))
    Wg = np.ascontiguousarray(np.asarray(Wg, dtype=np.float32))
    Wd = np.ascontiguousarray(np.asarray(Wd, dtype=np.float32))

    pieces = []
    for g in range(NG):
        hs = [g * HPG + i for i in range(HPG)]
        for h in hs:
            blk = Wqkv[h * 384:h * 384 + 128, :]
            pieces.append(blk.reshape(128, 16, 128).transpose(2, 1, 0)
                          .reshape(128, 2048))
        for h in hs:
            blk = Wqkv[h * 384 + 128:h * 384 + 256, :]
            pieces.append(blk.reshape(128, 16, 128).transpose(2, 1, 0)
                          .reshape(128, 2048))
        vrows = np.concatenate([Wqkv[h * 384 + 256:h * 384 + 384, :]
                                for h in hs], axis=0)
        vt = vrows.reshape(HPG * 128, 16, 128).transpose(2, 1, 0)
        pieces.append(np.ascontiguousarray(vt[:, :8, :]).reshape(128, 2048))
        pieces.append(np.ascontiguousarray(vt[:, 8:, :]).reshape(128, 2048))
    wqkvT = np.ascontiguousarray(np.stack(pieces)).reshape(-1)

    wprojT = np.ascontiguousarray(Wproj.T).reshape(-1)
    wuT = np.ascontiguousarray(
        Wu.reshape(NEC, 128, 16, 128).transpose(0, 3, 2, 1)).reshape(-1)
    wgT = np.ascontiguousarray(
        Wg.reshape(NEC, 128, 16, 128).transpose(0, 3, 2, 1)).reshape(-1)
    wdT = np.ascontiguousarray(
        Wd.reshape(2048, 64, 128).transpose(1, 2, 0)).reshape(-1)

    bq = np.ascontiguousarray(
        np.stack([bqkv[h * 384:h * 384 + 128] for h in range(H)], axis=1))
    bk = np.ascontiguousarray(
        np.stack([bqkv[h * 384 + 128:h * 384 + 256] for h in range(H)],
                 axis=1))
    bv_row = np.concatenate([bqkv[h * 384 + 256:h * 384 + 384]
                             for h in range(H)])
    bv_rep = np.ascontiguousarray(
        np.broadcast_to(bv_row.reshape(NG, HPG * 128)[:, None, :],
                        (NG, 128, HPG * 128))).reshape(-1)
    bproj_rep = np.ascontiguousarray(
        np.broadcast_to(bproj[None, :], (128, E))).reshape(-1)
    return dict(wqkvT=wqkvT, wprojT=wprojT, wuT=wuT, wgT=wgT, wdT=wdT,
                bq=bq, bk=bk, bv_rep=bv_rep, bproj_rep=bproj_rep)


def kernel(L0_attn_query, L0_attn_key, L0_attn_value, L0_attn_Wqkv,
           L0_attn_bqkv, L0_attn_Wproj, L0_attn_bproj, L0_ffn_W_up,
           L0_ffn_W_gate, L0_ffn_W_down):
    x = np.asarray(L0_attn_query, dtype=np.float32)
    assert x.shape == (8, T, E)
    wdict = _prep_weights(L0_attn_Wqkv, L0_attn_bqkv, L0_attn_Wproj,
                          L0_attn_bproj, L0_ffn_W_up, L0_ffn_W_gate,
                          L0_ffn_W_down)

    if "nc" not in _NC_CACHE:
        _NC_CACHE["nc"] = build_nc()
    nc = _NC_CACHE["nc"]

    in_maps = []
    for c in range(8):
        m = dict(wdict)
        m["x"] = np.ascontiguousarray(x[c])
        in_maps.append(m)
    res = run_bass_kernel_spmd(nc, in_maps, core_ids=list(range(8)))
    out = np.stack([np.asarray(r["out"], dtype=np.float32)
                    for r in res.results], axis=0)
    return out


# version 27
# speedup vs baseline: 1.1061x; 1.1061x over previous
import numpy as np
import ml_dtypes
import concourse.bass as bass
import concourse.mybir as mybir
from concourse.bass_utils import run_bass_kernel_spmd
from contextlib import ExitStack

F32 = mybir.dt.float32
BF16 = mybir.dt.bfloat16
AF = mybir.ActivationFunctionType
ALU = mybir.AluOpType
AX = mybir.AxisListType
BFNP = ml_dtypes.bfloat16

T = 2048
E = 2048
H = 16
HD = 128
DEX = 8192
NDC = 16
NTC = 16
HPG = 2
NG = H // HPG
NQTG = 8
NEC = 64
NQ = 4
SQRT_HD = float(HD) ** 0.5
LN_EPS = 1e-5
Q_EPS = 1e-8

ENGINES = ("T", "V", "A", "S", "G")


class Ctr:
    __slots__ = ("h", "n", "name")

    def __init__(self, h, name):
        self.h = h
        self.n = 0
        self.name = name


class Prog:
    def __init__(self, nc):
        self.nc = nc
        self.ops = []

    def op(self, eng, fn, waits=(), inc=None, dma=False, r=(), w=()):
        assert eng in ENGINES
        for (c, v) in waits:
            assert v <= c.n, f"wait on {c.name}>={v} but only {c.n} emitted"
        incs = []
        if inc is not None:
            amt = 16 if dma else 1
            inc.n += amt
            incs.append((inc, amt))
        self.ops.append((eng, fn, tuple(waits), tuple(incs),
                         frozenset(r), frozenset(w)))

    def emit_engine(self, eng_obj, name):
        lastw = {}
        dirty_r = set()
        dirty_w = set()
        for (e, fn, waits, incs, r, w) in self.ops:
            if e != name:
                continue
            for (c, v) in waits:
                if v <= 0:
                    continue
                if lastw.get(id(c), -1) < v:
                    eng_obj.wait_ge(c.h, v)
                    lastw[id(c)] = v
            if name != "T" and \
                    ((r & dirty_w) or (w & dirty_w) or (w & dirty_r)):
                eng_obj.drain()
                dirty_r.clear()
                dirty_w.clear()
            inst = fn(eng_obj)
            for (c, amt) in incs:
                inst.then_inc(c.h, amt)
            dirty_r |= r
            dirty_w |= w


def _flat2d(ap_flat, off, p, cols):
    return ap_flat[off:off + p * cols].rearrange("(p c) -> p c", p=p)


def build_nc():
    nc = bass.Bass("TRN2", target_bir_lowering=False)

    x_h = nc.dram_tensor("x", [T, E], F32, kind="ExternalInput").ap()
    wqkv_h = nc.dram_tensor("wqkvT", [NG * 6 * 128 * 2048], F32, kind="ExternalInput").ap()
    wproj_h = nc.dram_tensor("wprojT", [16 * 128 * 2048], F32, kind="ExternalInput").ap()
    wu_h = nc.dram_tensor("wuT", [NEC * 128 * 2048], F32, kind="ExternalInput").ap()
    wg_h = nc.dram_tensor("wgT", [NEC * 128 * 2048], F32, kind="ExternalInput").ap()
    wd_h = nc.dram_tensor("wdT", [256 * 128 * 512], F32, kind="ExternalInput").ap()
    bq_h = nc.dram_tensor("bq", [128, H], F32, kind="ExternalInput").ap()
    bk_h = nc.dram_tensor("bk", [128, H], F32, kind="ExternalInput").ap()
    bv_h = nc.dram_tensor("bv_rep", [NG * 128 * (HPG * HD)], F32, kind="ExternalInput").ap()
    bp_h = nc.dram_tensor("bproj_rep", [128 * 2048], F32, kind="ExternalInput").ap()
    out_h = nc.dram_tensor("out", [T, E], F32, kind="ExternalOutput").ap()

    attv_scr = nc.dram_tensor("attv_scr", [H * NQTG * 128 * 256], BF16, kind="Internal").ap()
    s_scr = nc.dram_tensor("s_scr", [NTC * 128 * 2048], BF16, kind="Internal").ap()
    tu_scr = nc.dram_tensor("tu_scr", [NEC * 128 * 2048], BF16, kind="Internal").ap()
    tg_scr = nc.dram_tensor("tg_scr", [NEC * 128 * 2048], BF16, kind="Internal").ap()
    td_scr = nc.dram_tensor("td_scr", [256 * 128 * 512], BF16, kind="Internal").ap()
    tiny_scr = nc.dram_tensor("tiny_scr", [128], F32, kind="Internal").ap()
    tiny_scr2 = nc.dram_tensor("tiny_scr2", [128], F32, kind="Internal").ap()

    ident_dram = nc.inline_tensor(np.eye(128, dtype=BFNP), name="identconst")

    cursor = nc.sbuf_top

    def alloc(name, shape, dt):
        nonlocal cursor
        esz = {F32: 4, BF16: 2}[dt]
        nb = esz
        for s in shape[1:]:
            nb *= s
        nb = (nb + 31) // 32 * 32
        cursor -= nb
        return nc.alloc_sbuf_tensor_at(name, shape, dt, offset=cursor)

    xn_fm_t = alloc("xn_fm", [128, NDC * T], BF16)
    xn_fm = xn_fm_t[:, :]
    xn_v = xn_fm.rearrange("p (dc t) -> p dc t", dc=NDC)
    ident = alloc("ident", [128, 128], BF16)
    ones_row = alloc("ones_row", [1, 128], F32)
    ones_bf = alloc("ones_bf", [1, 128], BF16)
    bq_s = alloc("bq_s", [128, H], F32)
    bk_s = alloc("bk_s", [128, H], F32)
    tnames = ("st_sum0 st_sum1 st_nm0 st_nm1 st_a0 st_a1 st_b0 st_b1 "
              "st_var0 st_var1 st_rec0 st_rec1 st_rstd0 st_rstd1 "
              "ngm0 ngm1 nms0 nms1 rsum0 rsum1 rsr0 rsr1 "
              "pw_u pw_g pw_d t11u t11g t11d tmp11 tmp11b "
              "cu_p ncu_p cg_p ncg_p cd_p ncd_p sun_p sgd_p").split()
    tt_ = {nm: alloc(nm, [128, 1], F32) for nm in tnames}
    pmax_ = [alloc("pmax0", [128, 4], F32), alloc("pmax1", [128, 4], F32)]
    trow = alloc("trow", [1, 128], F32)
    brow = alloc("brow", [1, 128], F32)
    partials = alloc("partials", [128, 192], F32)

    A0 = cursor
    q_buf, k_buf, v_buf = [None, None], [None, None], [None, None]
    for i in (0, 1):
        q_buf[i] = alloc(f"q_buf{i}", [128, HPG * T], BF16)
        k_buf[i] = alloc(f"k_buf{i}", [128, HPG * T], BF16)
        v_buf[i] = alloc(f"v_buf{i}", [128, NTC * HPG * HD], BF16)
    a_p3 = cursor
    cursor = A0
    wproj_bf_t = alloc("wproj_bf", [128, NDC * E], BF16)
    wproj_bf = wproj_bf_t[:, :]
    a_p4 = cursor
    cursor = A0
    hidden_t = alloc("hidden_q", [128, NEC * 512], BF16)
    hidden_q = hidden_t[:, :]
    cursor = min(a_p3, a_p4, cursor)

    B0 = cursor
    x_tile, xn_tm = [None, None], [None, None]
    for i in (0, 1):
        x_tile[i] = alloc(f"x_tile{i}", [128, E], F32)
        xn_tm[i] = alloc(f"xn_tm{i}", [128, E], BF16)
    sq_scr = alloc("sq_scr", [128, E], BF16)
    b_p1 = cursor
    cursor = B0
    wstage, w_bf = [None, None], [None, None]
    for i in (0, 1):
        wstage[i] = alloc(f"wstage{i}", [128, 1024], F32)
    for i in (0, 1):
        w_bf[i] = alloc(f"w_bf{i}", [128, 2048], BF16)
    p_tm_b = [None, None]
    p_tm_b[0] = alloc("p_tm0", [128, 2 * T], BF16)[:, :]
    p_tm_b[1] = alloc("p_tm1", [128, 2 * T], BF16)[:, :]
    p_pt_t = alloc("p_pt", [128, NTC * 256], BF16)
    p_pt = p_pt_t[:, :]
    av_evac = [None, None]
    bv_s = [None, None]
    for i in (0, 1):
        av_evac[i] = alloc(f"av_evac{i}", [128, 256], BF16)
    for i in (0, 1):
        bv_s[i] = alloc(f"bv_s{i}", [128, HPG * HD], F32)
    b_p3 = cursor
    cursor = B0
    wstage4, attv_pair, s_tile = [None, None], [None, None], [None, None]
    for i in (0, 1):
        wstage4[i] = alloc(f"wstage4_{i}", [128, 2048], F32)
    for i in (0, 1):
        attv_pair[i] = alloc(f"attv_pair{i}", [128, H * 256], BF16)
    s_tile[0] = s_tile[1] = alloc("s_tile0", [128, 2048], BF16)
    bproj_row = alloc("bproj_row", [1, 2048], BF16)
    b_p4 = cursor
    cursor = B0
    tu_b, tg_b, td_b, silu_b = [None, None], [None, None], [None, None], [None, None]
    g_tile, xs_strip, ss_strip, out_strip = [None, None], [None, None], [None, None], [None, None]
    for i in (0, 1):
        tu_b[i] = alloc(f"tu_b{i}", [128, 2048], BF16)
        tg_b[i] = alloc(f"tg_b{i}", [128, 2048], BF16)
    for i in (0, 1):
        td_b[i] = alloc(f"td_b{i}", [128, 1024], BF16)
        silu_b[i] = alloc(f"silu_b{i}", [128, 512], BF16)
        g_tile[i] = alloc(f"g_tile{i}", [128, 512], BF16)
        xs_strip[i] = alloc(f"xs_strip{i}", [128, 512], F32)
        ss_strip[i] = alloc(f"ss_strip{i}", [128, 512], BF16)
        out_strip[i] = alloc(f"out_strip{i}", [128, 512], F32)
    b_p6 = cursor
    cursor = min(b_p1, b_p3, b_p4, b_p6)

    qstage = [None, None]
    for i in (0, 1):
        qstage[i] = alloc(f"qstage{i}", [128, 2048], F32)
    abs_scr = alloc("abs_scr", [128, 2048], BF16)
    a_gt = abs_scr
    tq_out = [None, None]
    for i in (0, 1):
        tq_out[i] = alloc(f"tq_out{i}", [128, 2048], BF16)

    assert cursor > nc.sbuf_base + 1536, \
        f"SBUF overflow: cursor {cursor} base {nc.sbuf_base}"

    ps_big = nc.alloc_psum_tensor("ps_big", [128, 2048], F32)
    ps_pt = nc.alloc_psum_tensor("ps_pt", [128, 1024], BF16)
    ps_u = nc.alloc_psum_tensor("ps_u", [128, 512], F32)
    ps_g = nc.alloc_psum_tensor("ps_g", [128, 512], F32)
    ps_m = nc.alloc_psum_tensor("ps_m", [128, 512], F32)

    P = Prog(nc)
    op = P.op

    with ExitStack() as ctx:
        def sem(nm):
            return Ctr(ctx.enter_context(nc.semaphore(nm)), nm)

        QP = {q: [sem(f"d{q}{p}") for p in range(2)] for q in "SGA"}
        qix = {q: 0 for q in "SGA"}
        cT, cV, cA = sem("cT"), sem("cV"), sem("cA")
        cG = sem("cG")
        Z = (QP["S"][0], 0)

        def qdone(q):
            return tuple((c, c.n) for c in QP[q])

        def tmax(tokens):
            best = {}
            for (c, n) in tokens:
                if n > best.get(id(c), (None, -1))[1]:
                    best[id(c)] = (c, n)
            return tuple(best.values())

        def dma(q, dst, src, waits=(), r=(), w=()):
            ctr = QP[q][qix[q] % 2]
            qix[q] += 1
            prev = ctr.n
            op(q, lambda e, d=dst, s=src: e.dma_start(out=d, in_=s),
               waits=tuple(waits) + ((ctr, prev),), inc=ctr, dma=True,
               r=r, w=w)
            return (ctr, ctr.n)

        def vop(fn, waits=(), r=(), w=()):
            op("V", fn, waits=waits, inc=cV, r=r, w=w)
            return cV.n

        def aop(fn, waits=(), r=(), w=()):
            op("A", fn, waits=waits, inc=cA, r=r, w=w)
            return cA.n

        def top(fn, waits=(), inc=False, r=(), w=()):
            op("T", fn, waits=waits, inc=cT if inc else None, r=r, w=w)
            return cT.n

        def gop(fn, waits=(), r=(), w=()):
            op("G", fn, waits=waits, inc=cG, r=r, w=w)
            return cG.n

        dma("S", ident[:, :], ident_dram.ap()[:, :])
        dma("S", bq_s[:, :], bq_h)
        dma("S", bk_s[:, :], bk_h)
        P0_S = qdone("S")
        vop(lambda e: e.memset(ones_row[:, :], 1.0), w={"ones_row"})
        vop(lambda e: e.memset(ones_bf[:, :], 1.0), w={"ones_bf"})

        vdone_xt = [0, 0]
        adone_xt = [0, 0]
        pt_free_v = 0
        for tt in range(NTC):
            sl = tt % 2
            ss = tt % 2
            xt = f"xt{sl}"
            xt_ready = dma("G", x_tile[sl][:, :],
                           x_h[tt * 128:(tt + 1) * 128, :],
                           waits=[(cV, vdone_xt[sl]), (cA, adone_xt[sl])],
                           w={xt})
            vop(lambda e, sl=sl, ss=ss: e.reduce_sum(
                out=tt_[f"st_sum{ss}"][:, :], in_=x_tile[sl][:, :], axis=AX.X),
                waits=[xt_ready], r={xt}, w={f"st_sum{ss}"})
            a_sq = aop(lambda e, sl=sl, ss=ss: e.activation(
                out=sq_scr[:, :], in_=x_tile[sl][:, :], func=AF.Square,
                accum_out=tt_[f"st_a{ss}"][:, 0:1]),
                waits=[xt_ready], r={xt}, w={"sq_scr", f"st_a{ss}"})
            vop(lambda e, ss=ss: e.tensor_scalar(
                out=tt_[f"st_nm{ss}"][:, :], in0=tt_[f"st_sum{ss}"][:, :],
                scalar1=-1.0 / E, scalar2=None, op0=ALU.mult),
                r={f"st_sum{ss}"}, w={f"st_nm{ss}"})
            vop(lambda e, ss=ss: e.tensor_scalar(
                out=tt_[f"st_a{ss}"][:, :], in0=tt_[f"st_a{ss}"][:, :],
                scalar1=1.0 / E, scalar2=LN_EPS, op0=ALU.mult, op1=ALU.add),
                waits=[(cA, a_sq)], r={f"st_a{ss}"}, w={f"st_a{ss}"})
            vop(lambda e, ss=ss: e.tensor_tensor(
                out=tt_[f"st_b{ss}"][:, :], in0=tt_[f"st_nm{ss}"][:, :],
                in1=tt_[f"st_nm{ss}"][:, :], op=ALU.mult),
                r={f"st_nm{ss}"}, w={f"st_b{ss}"})
            vop(lambda e, ss=ss: e.tensor_sub(
                out=tt_[f"st_var{ss}"][:, :], in0=tt_[f"st_a{ss}"][:, :],
                in1=tt_[f"st_b{ss}"][:, :]),
                r={f"st_a{ss}", f"st_b{ss}"}, w={f"st_var{ss}"})
            v_rec = vop(lambda e, ss=ss: e.reciprocal(
                out=tt_[f"st_rec{ss}"][:, :], in_=tt_[f"st_var{ss}"][:, :]),
                r={f"st_var{ss}"}, w={f"st_rec{ss}"})
            a_rstd = aop(lambda e, ss=ss: e.activation(
                out=tt_[f"st_rstd{ss}"][:, :], in_=tt_[f"st_rec{ss}"][:, :],
                func=AF.Sqrt),
                waits=[(cV, v_rec)], r={f"st_rec{ss}"}, w={f"st_rstd{ss}"})
            v_xn = vop(lambda e, sl=sl, ss=ss: e.tensor_scalar(
                out=xn_tm[sl][:, :], in0=x_tile[sl][:, :],
                scalar1=tt_[f"st_nm{ss}"][:, 0:1],
                scalar2=tt_[f"st_rstd{ss}"][:, 0:1],
                op0=ALU.add, op1=ALU.mult),
                waits=[(cA, a_rstd)],
                r={xt, f"st_nm{ss}", f"st_rstd{ss}"}, w={f"xn_tm{sl}"})
            vdone_xt[sl] = v_xn
            adone_xt[sl] = a_sq
            for half in (0, 1):
                for i in range(8):
                    dc = half * 8 + i
                    top(lambda e, sl=sl, dc=dc, i=i: e.transpose(
                        out=ps_pt[:, i * 128:(i + 1) * 128],
                        in_=xn_tm[sl][:, dc * 128:(dc + 1) * 128],
                        identity=ident[:, :]),
                        waits=([(cV, v_xn), (cV, pt_free_v),
                                *P0_S] if i == 0 else ()),
                        inc=(i == 7), r={f"xn_tm{sl}", "ident"},
                        w={"bank4"})
                t_half = cT.n
                for j2 in (0, 1):
                    dc0 = half * 8 + j2 * 4
                    dst = xn_v[:, dc0:dc0 + 4, tt * 128:tt * 128 + 128]
                    pt_free_v = vop(
                        lambda e, dst=dst, j2=j2: e.tensor_copy(
                            out=dst, in_=ps_pt[:, j2 * 512:(j2 + 1) * 512]),
                        waits=[(cT, t_half)], r={"bank4"}, w={"xn_fm"})
        XN_V = cV.n
        P1_A = cA.n
        P1_T = cT.n
        bar13 = [(cV, XN_V), (cA, P1_A), (cT, P1_T)]

        a_absdone = [0, 0]
        q_vdone = [0, 0]
        g_wr = [Z, Z]
        qmode = {"load": "A", "store": "S"}
        tug_piece_done = {}
        tdq_piece_done = {}
        rt_free = [Z]

        def bcast(dst_pp, val11_name, tag):
            vb = vop(lambda e, val11_name=val11_name: e.tensor_scalar(
                out=brow[:, :], in0=ones_row[:, :],
                scalar1=tt_[val11_name][0:1, 0:1], scalar2=None, op0=ALU.mult),
                waits=[rt_free[0]],
                r={"ones_row", val11_name}, w={"brow"})
            d1 = dma("A", tiny_scr2.rearrange("(o p) -> o p", o=1),
                     brow[0:1, :], waits=[(cV, vb)], r={"brow"})
            d2 = dma("A", dst_pp[:, :],
                     tiny_scr2.rearrange("(p o) -> p o", p=128),
                     waits=[d1], w={tag})
            rt_free[0] = d2

        qloads = []
        qcomps = []
        qd_in = {}
        fins = {}
        srcs = {}
        dsts = {}
        wmeta = [("u", wu_h, tu_scr, 0), ("g", wg_h, tg_scr, 64),
                 ("d", wd_h, td_scr, 128)]
        for (wn, src, dst, pbase) in wmeta:
            def scale_load(c, wn=wn, src=src):
                sl = c % 2
                qd_in[(wn, c)] = dma(
                    "A", qstage[sl][:, :],
                    _flat2d(src, c * 262144, 128, 2048),
                    waits=[(cA, a_absdone[sl]), (cV, q_vdone[0]),
                           (cV, q_vdone[1])],
                    w={f"qstage{sl}"})

            def scale_comp(c, wn=wn, pbase=pbase):
                sl = c % 2
                a_absdone[sl] = aop(
                    lambda e, sl=sl, pc=pbase + c: e.activation(
                        out=abs_scr[:, :], in_=qstage[sl][:, :], func=AF.Abs,
                        accum_out=partials[:, pc:pc + 1]),
                    waits=[qd_in[(wn, c)]], r={f"qstage{sl}"},
                    w={"abs_scr", "partials"})
            for c in range(64):
                qloads.append(lambda c=c, f=scale_load: f(c))
                qcomps.append(lambda c=c, f=scale_comp: f(c))

            def finalize(wn=wn, pbase=pbase):
                v1 = vop(lambda e: e.reduce_sum(
                    out=tt_[f"pw_{wn}"][:, :],
                    in_=partials[:, pbase:pbase + 64], axis=AX.X),
                    waits=[(cA, a_absdone[0]), (cA, a_absdone[1])],
                    r={"partials"}, w={f"pw_{wn}"})
                d1 = dma("A", tiny_scr.rearrange("(p o) -> p o", p=128),
                         tt_[f"pw_{wn}"][:, :],
                         waits=[(cV, v1), rt_free[0]], r={f"pw_{wn}"})
                d2 = dma("A", trow[0:1, :],
                         tiny_scr.rearrange("(o p) -> o p", o=1),
                         waits=[d1], w={"trow"})
                rt_free[0] = d2
                vop(lambda e: e.reduce_sum(
                    out=tt_[f"t11{wn}"][0:1, :], in_=trow[:, :], axis=AX.X),
                    waits=[d2], r={"trow"}, w={f"t11{wn}"})
                vop(lambda e: e.tensor_scalar(
                    out=tt_[f"t11{wn}"][0:1, :], in0=tt_[f"t11{wn}"][0:1, :],
                    scalar1=1.0 / float(DEX * E), scalar2=Q_EPS,
                    op0=ALU.mult, op1=ALU.add),
                    r={f"t11{wn}"}, w={f"t11{wn}"})
                vop(lambda e: e.tensor_scalar(
                    out=tt_["tmp11"][0:1, :], in0=tt_[f"t11{wn}"][0:1, :],
                    scalar1=0.5, scalar2=None, op0=ALU.mult),
                    r={f"t11{wn}"}, w={"tmp11"})
                vop(lambda e: e.tensor_scalar(
                    out=tt_["tmp11b"][0:1, :], in0=tt_["tmp11"][0:1, :],
                    scalar1=-1.0, scalar2=None, op0=ALU.mult),
                    r={"tmp11"}, w={"tmp11b"})
                bcast(tt_[f"c{wn}_p"], "tmp11", f"c{wn}_p")
                bcast(tt_[f"nc{wn}_p"], "tmp11b", f"nc{wn}_p")
                if wn == "u":
                    bcast(tt_["sun_p"], "t11u", "sun_p")
                if wn == "d":
                    vop(lambda e: e.tensor_tensor(
                        out=tt_["tmp11"][0:1, :], in0=tt_["t11g"][0:1, :],
                        in1=tt_["t11d"][0:1, :], op=ALU.mult),
                        r={"t11g", "t11d"}, w={"tmp11"})
                    bcast(tt_["sgd_p"], "tmp11", "sgd_p")
                finalize.bc_done = qdone("A")
            finalize.bc_done = ()
            qloads.append(lambda: None)
            qcomps.append(finalize)
            fins[wn] = finalize
            srcs[wn] = src
            dsts[wn] = dst

        def quant_load(wn, c, sl):
            qd_in[(wn + "q", c)] = dma(
                qmode["load"], qstage[sl][:, :],
                _flat2d(srcs[wn], c * 262144, 128, 2048),
                waits=[(cV, q_vdone[sl]), (cA, a_absdone[0]),
                       (cA, a_absdone[1])],
                w={f"qstage{sl}"})

        def quant_comp(wn, c, sl):
            fin = fins[wn]
            d_in = qd_in[(wn + "q", c)]
            vop(lambda e, sl=sl, wn=wn: e.tensor_scalar(
                out=tq_out[sl][:, :], in0=qstage[sl][:, :],
                scalar1=tt_[f"nc{wn}_p"][:, 0:1], scalar2=None,
                op0=ALU.is_lt),
                waits=[d_in, *fin.bc_done, g_wr[sl]],
                r={f"qstage{sl}", f"nc{wn}_p"}, w={f"tq_out{sl}"})
            q_vdone[sl] = vop(lambda e, sl=sl, wn=wn: e.scalar_tensor_tensor(
                out=tq_out[sl][:, :], in0=qstage[sl][:, :],
                scalar=tt_[f"c{wn}_p"][:, 0:1], in1=tq_out[sl][:, :],
                op0=ALU.is_gt, op1=ALU.subtract),
                r={f"qstage{sl}", f"c{wn}_p", f"tq_out{sl}"},
                w={f"tq_out{sl}"})
            g_wr[sl] = dma(qmode["store"],
                           _flat2d(dsts[wn], c * 262144, 128, 2048),
                           tq_out[sl][:, :],
                           waits=[(cV, q_vdone[sl])], r={f"tq_out{sl}"})
            if wn == "d":
                tdq_piece_done[c] = g_wr[sl]
            else:
                tug_piece_done[(wn, c)] = g_wr[sl]

        qorder = []
        for c in range(64):
            qorder += [("u", c), ("g", c)]
        qorder += [("d", c) for c in range(64)]
        for gi, (wn, c) in enumerate(qorder):
            qloads.append(lambda wn=wn, c=c, sl=gi % 2: quant_load(wn, c, sl))
            qcomps.append(lambda wn=wn, c=c, sl=gi % 2: quant_comp(wn, c, sl))
        qload_it = iter(qloads)
        qcomp_it = iter(qcomps)
        qstate = {"loads_done": 0, "comps_done": 0,
                  "nl": len(qloads), "ncmp": len(qcomps)}

        def run_qloads(n):
            for _ in range(n):
                if qstate["loads_done"] >= qstate["comps_done"] + 2:
                    return
                stp = next(qload_it, None)
                if stp is None:
                    return
                stp()
                qstate["loads_done"] += 1

        def run_qpairs(n):
            for _ in range(n):
                run_qloads(1)
                run_qcomps(1)

        def run_qcomps(n):
            for _ in range(n):
                if qstate["comps_done"] >= qstate["loads_done"]:
                    return
                stp = next(qcomp_it, None)
                if stp is None:
                    return
                stp()
                qstate["comps_done"] += 1

        s_wstage_free = [0, 0]
        s_wbf_free = [0, 0]
        evac_ps = {0: 0, 1: 0}
        evac_ps_v = {0: 0, 1: 0}
        bv_free = [0, 0]
        groupbuf_free = [0, 0]
        attv_block_done = {}
        qcomp_per_slot = 3
        st_ = {"ptm_free_t": [0, 0], "pt_evac_v": 0, "pt_evac_a": 0,
               "ppt_free": 0, "psm_free": 0,
               "av_i": 0, "av_dma": [Z, Z], "scores_free": 0,
               "sidx": 0, "pend": None}

        def production_steps(g):
            gset = g % 2
            base = g * 6 * 128 * 2048
            piece_v = {}
            bv_ready = [Z]

            def load_bv():
                def do():
                    bv_ready[0] = dma(
                        "S", bv_s[gset][:, :],
                        _flat2d(bv_h, g * 128 * HPG * HD, 128, HPG * HD),
                        waits=bar13 + [(cV, bv_free[gset])], w={f"bv{gset}"})
                return do

            def load_piece(pi):
                def do():
                    sl = pi % 2
                    for hf in (0, 1):
                        d_in = dma(
                            "S", wstage[hf][:, :],
                            _flat2d(wqkv_h, base + pi * 262144,
                                    128, 2048)[:, hf * 1024:(hf + 1) * 1024],
                            waits=bar13 + [(cV, s_wstage_free[hf])],
                            w={f"wstage{hf}"})
                        piece_v[pi] = vop(
                            lambda e, sl=sl, hf=hf: e.tensor_copy(
                                out=w_bf[sl][:, hf * 1024:(hf + 1) * 1024],
                                in_=wstage[hf][:, :]),
                            waits=[d_in, (cT, s_wbf_free[sl])],
                            r={f"wstage{hf}"}, w={f"w_bf{sl}"})
                        s_wstage_free[hf] = piece_v[pi]
                return do

            def qk_fill(pi, which, hh, tg):
                def do():
                    sl = pi % 2
                    par = tg % 2
                    ps = ps_u if par == 0 else ps_g
                    for dc in range(NDC):
                        top(lambda e, sl=sl, dc=dc, tg=tg, ps=ps: e.matmul(
                            out=ps[:, :],
                            lhsT=w_bf[sl][:, dc * 128:(dc + 1) * 128],
                            rhs=xn_fm[:, dc * T + tg * 512:dc * T + tg * 512 + 512],
                            start=(dc == 0), stop=(dc == NDC - 1)),
                            waits=([(cV, piece_v[pi]), (cV, XN_V),
                                    (cA, evac_ps[par]),
                                    (cV, evac_ps_v[par])] if dc == 0 else ()),
                            inc=(dc == NDC - 1),
                            r={f"w_bf{pi % 2}", "xn_fm"},
                            w={f"bank{5 + par}"})
                    t_done = cT.n
                    s_wbf_free[sl] = t_done
                    dstb = q_buf if which == "q" else k_buf
                    bias = bq_s if which == "q" else bk_s
                    h_abs = g * HPG + hh
                    evac_ps[par] = aop(
                        lambda e, hh=hh, tg=tg, ps=ps, dstb=dstb,
                        bias=bias, h_abs=h_abs: e.activation(
                            out=dstb[gset][:, hh * T + tg * 512:
                                           hh * T + tg * 512 + 512],
                            in_=ps[:, :], func=AF.Identity,
                            bias=bias[:, h_abs:h_abs + 1], scale=1.0),
                        waits=[(cT, t_done), (cT, groupbuf_free[gset]),
                               *P0_S],
                        r={f"bank{5 + par}"}, w={f"qk{gset}"})
                return do

            def v_fill(tc):
                def do():
                    par = tc % 2
                    ps = ps_u if par == 0 else ps_g
                    for dc in range(NDC):
                        pi = 4 if dc < 8 else 5
                        sl = pi % 2
                        dcl = dc % 8
                        top(lambda e, sl=sl, dcl=dcl, dc=dc, tc=tc,
                            ps=ps: e.matmul(
                                out=ps[:, 0:HPG * HD],
                                lhsT=xn_fm[:, dc * T + tc * 128:
                                           dc * T + tc * 128 + 128],
                                rhs=w_bf[sl][:, dcl * (HPG * HD):
                                             (dcl + 1) * (HPG * HD)],
                                start=(dc == 0), stop=(dc == NDC - 1)),
                            waits=([(cV, piece_v[4]), (cV, piece_v[5]),
                                    (cA, evac_ps[par]),
                                    (cV, evac_ps_v[par])] if dc == 0 else ()),
                            inc=(dc == NDC - 1),
                            r={"w_bf0", "w_bf1", "xn_fm"},
                            w={f"bank{5 + par}"})
                    t_done = cT.n
                    s_wbf_free[0] = max(s_wbf_free[0], t_done)
                    s_wbf_free[1] = max(s_wbf_free[1], t_done)
                    evac_ps_v[par] = vop(
                        lambda e, tc=tc, ps=ps: e.tensor_tensor(
                            out=v_buf[gset][:, tc * (HPG * HD):
                                            (tc + 1) * (HPG * HD)],
                            in0=ps[:, 0:HPG * HD], in1=bv_s[gset][:, :],
                            op=ALU.add),
                        waits=[(cT, t_done), (cT, groupbuf_free[gset]),
                               bv_ready[0]],
                        r={f"bv{gset}", f"bank{5 + par}"}, w={f"vb{gset}"})
                return do

            steps = [load_bv(), load_piece(0), load_piece(1)]
            for tg in range(4):
                steps.append(qk_fill(0, "q", 0, tg))
            steps.append(load_piece(2))
            for tg in range(4):
                steps.append(qk_fill(1, "q", 1, tg))
            steps.append(load_piece(3))
            for tg in range(4):
                steps.append(qk_fill(2, "k", 0, tg))
            steps.append(load_piece(4))
            for tg in range(4):
                steps.append(qk_fill(3, "k", 1, tg))
            steps.append(load_piece(5))
            for tc in range(NTC):
                steps.append(v_fill(tc))
            return steps

        def softmax_qt(g, hh, qtg, sidx, qt):
            gset = g % 2
            pb = p_tm_b[sidx % 2]
            if True:
                q0 = qtg * 256 + qt * 128
                ss = qt
                for kc4 in range(4):
                    t_sc = top(lambda e, gset=gset, hh=hh, q0=q0,
                               kc4=kc4: e.matmul(
                        out=ps_big[:, kc4 * 512:(kc4 + 1) * 512],
                        lhsT=q_buf[gset][:, hh * T + q0:hh * T + q0 + 128],
                        rhs=k_buf[gset][:, hh * T + kc4 * 512:
                                        hh * T + kc4 * 512 + 512],
                        start=True, stop=True),
                        waits=([(cA, st_["scores_free"])] if kc4 == 0 else ()),
                        inc=(kc4 == 3), r={f"qk{gset}"}, w={f"bank{kc4}"})
                vop(lambda e, ss=ss: e.reduce_max(
                    out=tt_[f"ngm{ss}"][:, :], in_=ps_big[:, :], axis=AX.X,
                    negate=True),
                    waits=[(cT, t_sc)], r={"bank0", "bank1", "bank2", "bank3"}, w={f"ngm{ss}"})
                v_nms = vop(lambda e, ss=ss: e.tensor_scalar(
                    out=tt_[f"nms{ss}"][:, :], in0=tt_[f"ngm{ss}"][:, :],
                    scalar1=SQRT_HD, scalar2=None, op0=ALU.mult),
                    r={f"ngm{ss}"}, w={f"nms{ss}"})
                st_["scores_free"] = aop(lambda e, pb=pb, qt=qt,
                                         ss=ss: e.activation(
                    out=pb[:, qt * T:(qt + 1) * T], in_=ps_big[:, :],
                    func=AF.Exp, bias=tt_[f"nms{ss}"][:, 0:1], scale=SQRT_HD,
                    accum_out=tt_[f"rsum{ss}"][:, 0:1]),
                    waits=[(cV, v_nms), (cT, st_["ptm_free_t"][sidx % 2])],
                    r={f"nms{ss}", "bank0", "bank1", "bank2", "bank3"},
                    w={f"p_tm{sidx % 2}", f"rsum{ss}"})
                vop(lambda e, ss=ss: e.reciprocal(
                    out=tt_[f"rsr{ss}"][:, :], in_=tt_[f"rsum{ss}"][:, :]),
                    waits=[(cA, st_["scores_free"])],
                    r={f"rsum{ss}"}, w={f"rsr{ss}"})
                v_norm0 = vop(lambda e, pb=pb, qt=qt, ss=ss: e.tensor_scalar(
                    out=pb[:, qt * T:(qt + 1) * T],
                    in0=pb[:, qt * T:(qt + 1) * T],
                    scalar1=tt_[f"rsr{ss}"][:, 0:1], scalar2=None,
                    op0=ALU.mult),
                    r={f"rsr{ss}", f"p_tm{sidx % 2}"},
                    w={f"p_tm{sidx % 2}"})
            return v_norm0

        def attention_tail_head(ctx):
            (g, hh, qtg, sidx, v_norm) = ctx
            pb = p_tm_b[sidx % 2]
            ev_v = st_["pt_evac_v"]
            ev_a = st_["pt_evac_a"]
            for rnd in range(4):
                for s_i in range(8):
                    kc = rnd * 4 + s_i // 2
                    qt = s_i % 2
                    top(lambda e, pb=pb, kc=kc, qt=qt, s_i=s_i: e.transpose(
                        out=ps_pt[:, s_i * 128:(s_i + 1) * 128],
                        in_=pb[:, qt * T + kc * 128:qt * T + kc * 128 + 128],
                        identity=ident[:, :]),
                        waits=([(cV, v_norm), (cV, ev_v), (cA, ev_a)]
                               if s_i == 0 else ()),
                        inc=(s_i == 7), r={f"p_tm{sidx % 2}", "ident"},
                        w={"bank4"})
                t_tr = cT.n
                dstv = p_pt.rearrange("p (kc q t) -> p kc q t", kc=NTC, q=2)
                if rnd % 2 == 0:
                    ev_v = vop(lambda e, rnd=rnd, dstv=dstv: e.tensor_copy(
                        out=dstv[:, rnd * 4:(rnd + 1) * 4, :, :],
                        in_=ps_pt[:, :]),
                        waits=[(cT, t_tr), (cT, st_["ppt_free"])],
                        r={"bank4"}, w={"p_pt"})
                else:
                    ev_a = aop(lambda e, rnd=rnd, dstv=dstv: e.activation(
                        out=dstv[:, rnd * 4:(rnd + 1) * 4, :, :],
                        in_=ps_pt[:, :], func=AF.Copy),
                        waits=[(cT, t_tr), (cT, st_["ppt_free"])],
                        r={"bank4"}, w={"p_pt"})
            st_["pt_evac_v"] = ev_v
            st_["pt_evac_a"] = ev_a
            st_["ptm_free_t"][sidx % 2] = cT.n

        def attention_tail_attv(ctx):
            (g, hh, qtg, sidx, v_norm) = ctx
            gset = g % 2
            h_abs = g * HPG + hh
            ev_v = st_["pt_evac_v"]
            ev_a = st_["pt_evac_a"]
            for kc in range(NTC):
                top(lambda e, gset=gset, hh=hh, kc=kc: e.matmul(
                    out=ps_m[:, 0:256],
                    lhsT=v_buf[gset][:, kc * 256 + hh * 128:
                                     kc * 256 + hh * 128 + 128],
                    rhs=p_pt[:, kc * 256:(kc + 1) * 256],
                    start=(kc == 0), stop=(kc == NTC - 1)),
                    waits=([(cV, ev_v), (cA, ev_a),
                            (cA, st_["psm_free"])]
                           if kc == 0 else ()),
                    inc=(kc == NTC - 1), r={f"vb{gset}", "p_pt"},
                    w={"bank7"})
            t_av = cT.n
            st_["ppt_free"] = t_av
            avsl = st_["av_i"] % 2
            v_ave = aop(lambda e, avsl=avsl: e.activation(
                out=av_evac[avsl][:, :], in_=ps_m[:, 0:256], func=AF.Copy),
                waits=[(cT, t_av), st_["av_dma"][avsl]],
                r={"bank7"}, w={f"av{avsl}"})
            st_["psm_free"] = v_ave
            avdst = attv_scr.rearrange("(q p h t) -> q p h t", q=NQTG,
                                       p=128, h=H)[qtg, :, h_abs, :]
            st_["av_dma"][avsl] = dma(
                "G", avdst,
                av_evac[avsl][:, :], waits=[(cA, v_ave)], r={f"av{avsl}"})
            attv_block_done[(h_abs, qtg)] = st_["av_dma"][avsl]
            st_["av_i"] += 1
            if (hh, qtg) == (HPG - 1, NQTG - 1):
                groupbuf_free[g % 2] = cT.n

        prev_g = None
        nslots_total = NG * HPG * NQTG
        qper_slot = 3
        for g in range(NG + 1):
            steps = production_steps(g) if g < NG else []
            if prev_g is None:
                for stp in steps:
                    stp()
            else:
                slots = [(hh, qtg) for hh in range(HPG)
                         for qtg in range(NQTG)]
                per = (len(steps) + len(slots) - 2) // max(len(slots) - 1, 1)
                si = 0
                for slot_i, (hh, qtg) in enumerate(slots):
                    if slot_i > 0:
                        for _ in range(per):
                            if si < len(steps):
                                steps[si]()
                                si += 1
                    run_qloads(2)
                    softmax_qt(prev_g, hh, qtg, st_["sidx"], 0)
                    if st_["pend"] is not None:
                        attention_tail_head(st_["pend"])
                    vn1 = softmax_qt(prev_g, hh, qtg, st_["sidx"], 1)
                    ctx = (prev_g, hh, qtg, st_["sidx"], vn1)
                    st_["sidx"] += 1
                    run_qcomps(2)
                    if st_["pend"] is not None:
                        attention_tail_attv(st_["pend"])
                    st_["pend"] = ctx
                    run_qpairs(1)
                while si < len(steps):
                    steps[si]()
                    si += 1
                bv_free[prev_g % 2] = cV.n
            prev_g = g if g < NG else prev_g
        if st_["pend"] is not None:
            attention_tail_head(st_["pend"])
            attention_tail_attv(st_["pend"])
            st_["pend"] = None
        P3_T, P3_V, P3_A = cT.n, cV.n, cA.n
        P3_G = cG.n

        qmode["load"] = "S"
        qmode["store"] = "G"
        bar34 = [(cT, P3_T), (cV, P3_V), (cA, P3_A), *qdone("G")]
        bp_stage = dma("S", wstage4[0][0:1, :],
                       bp_h[0:2048].rearrange("(o c) -> o c", o=1),
                       waits=bar34, w={"wstage40"})
        bp_cast = vop(lambda e: e.tensor_copy(
            out=bproj_row[0:1, :], in_=wstage4[0][0:1, :]),
            waits=[bp_stage, (cT, P3_T)],
            r={"wstage40"}, w={"bproj_row"})
        wp_stage_free = [bp_cast, 0]
        wp_cast = 0
        for dc in range(NDC):
            sl = dc % 2
            d_in = dma("S", wstage4[sl][:, :],
                       _flat2d(wproj_h, dc * 262144, 128, 2048),
                       waits=bar34 + [(cV, wp_stage_free[sl])],
                       w={f"wstage4{sl}"})
            wp_cast = vop(lambda e, sl=sl, dc=dc: e.tensor_copy(
                out=wproj_bf[:, dc * T:(dc + 1) * T], in_=wstage4[sl][:, :]),
                waits=[d_in, (cT, P3_T)],
                r={f"wstage4{sl}"}, w={"wproj_bf"})
            wp_stage_free[sl] = wp_cast
        WP_V = wp_cast

        s_dma = [Z, Z]
        s_wr = {}
        pair_T = [P3_T, P3_T]
        psum_free_a = [P3_A, P3_A, P3_A, P3_A]
        for pair in range(NQTG):
            psl = pair % 2
            blk = tmax(attv_block_done[(h, pair)] for h in range(H))
            last_d = dma(
                "G", attv_pair[psl][:, :],
                _flat2d(attv_scr, pair * 128 * H * 256, 128, H * 256),
                waits=[*blk, (cT, pair_T[psl]), (cV, P3_V)],
                w={f"avp{psl}"})
            run_qpairs(10)
            for tcl in (0, 1):
                tc = pair * 2 + tcl
                ssl = 0
                for eg in range(4):
                    bank = (tc * 4 + eg) % 4
                    top(lambda e, eg=eg, bank=bank: e.matmul(
                        out=ps_big[:, bank * 512:(bank + 1) * 512],
                        lhsT=ones_bf[0:1, :],
                        rhs=bproj_row[0:1, eg * 512:(eg + 1) * 512],
                        start=True, stop=False),
                        waits=[last_d, (cV, WP_V),
                               (cA, psum_free_a[bank]), (cA, P3_A)],
                        r={"ones_bf", "bproj_row"}, w={f"bank{bank}"})
                    for dc in range(NDC):
                        top(lambda e, psl=psl, tcl=tcl, eg=eg, dc=dc,
                            bank=bank: e.matmul(
                                out=ps_big[:, bank * 512:(bank + 1) * 512],
                                lhsT=attv_pair[psl][:, dc * 256 + tcl * 128:
                                                    dc * 256 + tcl * 128 + 128],
                                rhs=wproj_bf[:, dc * T + eg * 512:
                                             dc * T + eg * 512 + 512],
                                start=False, stop=(dc == NDC - 1)),
                            inc=(dc == NDC - 1),
                            r={f"avp{psl}", "wproj_bf"}, w={f"bank{bank}"})
                    t_p = cT.n
                    psum_free_a[bank] = aop(
                        lambda e, ssl=ssl, eg=eg, bank=bank: e.activation(
                            out=s_tile[ssl][:, eg * 512:(eg + 1) * 512],
                            in_=ps_big[:, bank * 512:(bank + 1) * 512],
                            func=AF.Copy),
                        waits=[(cT, t_p), s_dma[ssl]],
                        r={f"bank{bank}"}, w={f"s_tile{ssl}"})
                s_dma[ssl] = dma(
                    "G", _flat2d(s_scr, tc * 262144, 128, 2048),
                    s_tile[ssl][:, :],
                    waits=[(cA, psum_free_a[(tc * 4 + 3) % 4])],
                    r={f"s_tile{ssl}"})
                s_wr[tc] = s_dma[ssl]
            pair_T[psl] = cT.n
        P4_T, P4_V, P4_A = cT.n, cV.n, cA.n
        P4_G = qdone("G")

        qmode["load"] = "G"
        qmode["store"] = "G"
        bar46 = [(cT, P4_T), (cV, WP_V), *P4_G]
        tu_dma = [Z, Z]
        tg_dma = [Z, Z]
        tu_cons = [0, 0]
        td_dma = [None, None]
        td_cons = [0, 0]
        silu_free = [0, 0]
        bankg = [(cA, psum_free_a[0]), (cA, psum_free_a[1]),
                 (cA, psum_free_a[2]), (cA, psum_free_a[3]),
                 (cA, P3_A), (cA, P3_A), (cA, P3_A), (cA, P3_A)]
        gx_dma = [Z, Z]
        gs_dma = [Z, Z]
        gt_cons = [0, 0]
        strip_v = [0, 0]
        out_dma = [Z, Z]
        psf32 = ps_pt[:, :].bitcast(F32)

        def bank(b):
            if b < 4:
                return ps_big[:, b * 512:(b + 1) * 512]
            return [psf32, ps_u, ps_g, ps_m][b - 4][:, 0:512] \
                if b > 4 else psf32[:, 0:512]

        while qstate["comps_done"] < qstate["ncmp"] and False:
            pass
        first6 = True
        for Q in range(NQ):
            for ec in range(NEC):
                sl = ec % 2
                run_qpairs(2)
                tu_dma[sl] = dma("S", tu_b[sl][:, :],
                                 _flat2d(tu_scr, ec * 262144, 128, 2048),
                                 waits=(bar46 if first6 else []) +
                                 [tug_piece_done[("u", ec)],
                                  (cT, tu_cons[sl])],
                                 w={f"tu{sl}"})
                tg_dma[sl] = dma("A", tg_b[sl][:, :],
                                 _flat2d(tg_scr, ec * 262144, 128, 2048),
                                 waits=(bar46 if first6 else []) +
                                 [tug_piece_done[("g", ec)],
                                  (cT, tu_cons[sl])],
                                 w={f"tg{sl}"})
                first6 = False
                par = ec % 2
                upb = ps_big[:, par * 512:(par + 1) * 512]
                gtb = ps_big[:, (2 + par) * 512:(3 + par) * 512]
                for dc in range(NDC):
                    top(lambda e, sl=sl, dc=dc, Q=Q, upb=upb: e.matmul(
                        out=upb,
                        lhsT=tu_b[sl][:, dc * 128:(dc + 1) * 128],
                        rhs=xn_fm[:, dc * T + Q * 512:dc * T + Q * 512 + 512],
                        start=(dc == 0), stop=(dc == NDC - 1)),
                        waits=([tu_dma[sl], bankg[par]] if dc == 0 else ()),
                        inc=(dc == NDC - 1), r={f"tu{sl}", "xn_fm"},
                        w={f"bank{par}"})
                t_up = cT.n
                for dc in range(NDC):
                    top(lambda e, sl=sl, dc=dc, Q=Q, gtb=gtb: e.matmul(
                        out=gtb,
                        lhsT=tg_b[sl][:, dc * 128:(dc + 1) * 128],
                        rhs=xn_fm[:, dc * T + Q * 512:dc * T + Q * 512 + 512],
                        start=(dc == 0), stop=(dc == NDC - 1)),
                        waits=([tg_dma[sl], bankg[2 + par]] if dc == 0
                               else ()),
                        inc=(dc == NDC - 1), r={f"tg{sl}", "xn_fm"},
                        w={f"bank{2 + par}"})
                t_gt = cT.n
                tu_cons[sl] = t_gt
                ssl = ec % 2
                a_silu = aop(lambda e, ssl=ssl, upb=upb: e.activation(
                    out=silu_b[ssl][:, :], in_=upb, func=AF.Silu,
                    bias=0.0, scale=tt_["sun_p"][:, 0:1]),
                    waits=[(cT, t_up), (cV, silu_free[ssl])],
                    r={"sun_p", f"bank{par}"}, w={f"silu{ssl}"})
                bankg[par] = (cA, a_silu)
                silu_free[ssl] = vop(
                    lambda e, ssl=ssl, ec=ec, gtb=gtb: e.scalar_tensor_tensor(
                        out=hidden_q[:, ec * 512:(ec + 1) * 512],
                        in0=gtb, scalar=0.0, in1=silu_b[ssl][:, :],
                        op0=ALU.add, op1=ALU.mult),
                    waits=[(cT, t_gt), (cA, a_silu)],
                    r={f"silu{ssl}", f"bank{2 + par}"}, w={"hidden_q"})
                bankg[2 + par] = (cV, silu_free[ssl])
            HID_V = cV.n
            for dmh in range(2):
                for ec in range(NEC):
                    sl = ec % 2
                    if ec % 8 == 0:
                        run_qloads(1)
                        run_qcomps(1)
                    qq = "S" if sl == 0 else "A"
                    td_dma[sl] = dma(
                        qq, td_b[sl][:, :],
                        _flat2d(td_scr, ec * 262144, 128, 2048)[
                            :, dmh * 1024:(dmh + 1) * 1024],
                        waits=[tdq_piece_done[ec],
                               (cT, td_cons[sl]), (cT, P4_T)],
                        w={f"td{sl}"})
                    for tc in range(4):
                        for d2 in range(2):
                            b = tc * 2 + d2
                            wl = []
                            if tc == 0 and d2 == 0:
                                wl.append(td_dma[sl])
                            if ec == 0:
                                wl += [(cV, HID_V), bankg[b]]
                            top(lambda e, sl=sl, ec=ec, tc=tc, d2=d2,
                                b=b: e.matmul(
                                    out=bank(b),
                                    lhsT=hidden_q[:, ec * 512 + tc * 128:
                                                  ec * 512 + tc * 128 + 128],
                                    rhs=td_b[sl][:, d2 * 512:(d2 + 1) * 512],
                                    start=(ec == 0), stop=(ec == NEC - 1)),
                                waits=wl,
                                inc=(b == 7), r={f"td{sl}", "hidden_q"})
                    td_cons[sl] = cT.n
                t_dn = cT.n
                for tc in range(4):
                    for d2 in range(2):
                        b = tc * 2 + d2
                        dmg = dmh * 2 + d2
                        gsl = b % 2
                        tglob = Q * 4 + tc
                        gx_dma[gsl] = dma(
                            "G", xs_strip[gsl][:, :],
                            x_h[tglob * 128:(tglob + 1) * 128,
                                dmg * 512:(dmg + 1) * 512],
                            waits=[(cV, strip_v[gsl]), (cT, P4_T)],
                            w={f"xs{gsl}"})
                        gs_dma[gsl] = dma(
                            "G", ss_strip[gsl][:, :],
                            _flat2d(s_scr, tglob * 262144, 128, 2048)[
                                :, dmg * 512:(dmg + 1) * 512],
                            waits=[(cV, strip_v[gsl]), s_wr[tglob],
                                   (cT, P4_T)],
                            w={f"ssb{gsl}"})
                        a_g = aop(lambda e, gsl=gsl, b=b: e.activation(
                            out=g_tile[gsl][:, :],
                            in_=bank(b),
                            func=AF.Silu, bias=0.0,
                            scale=tt_["sgd_p"][:, 0:1]),
                            waits=[(cT, t_dn), (cV, gt_cons[gsl])],
                            r={"sgd_p", f"bank{b}"}, w={f"g{gsl}"})
                        bankg[b] = (cA, a_g)
                        v_m = vop(lambda e, gsl=gsl: e.tensor_tensor(
                            out=out_strip[gsl][:, :],
                            in0=ss_strip[gsl][:, :],
                            in1=g_tile[gsl][:, :], op=ALU.mult),
                            waits=[(cA, a_g), gs_dma[gsl],
                                   out_dma[gsl]],
                            r={f"ssb{gsl}", f"g{gsl}"}, w={f"os{gsl}"})
                        gt_cons[gsl] = v_m
                        v_o = vop(lambda e, gsl=gsl: e.tensor_tensor(
                            out=out_strip[gsl][:, :],
                            in0=out_strip[gsl][:, :],
                            in1=xs_strip[gsl][:, :], op=ALU.add),
                            waits=[gx_dma[gsl]],
                            r={f"os{gsl}", f"xs{gsl}"}, w={f"os{gsl}"})
                        strip_v[gsl] = v_o
                        out_dma[gsl] = dma(
                            "G", out_h[tglob * 128:(tglob + 1) * 128,
                                       dmg * 512:(dmg + 1) * 512],
                            out_strip[gsl][:, :],
                            waits=[(cV, v_o)], r={f"os{gsl}"})

        op("G", lambda e: e.nop(),
           waits=list(qdone("G")) + list(qdone("S")) + list(qdone("A")))

        with nc.Block() as block:
            block.tensor(lambda e: P.emit_engine(e, "T"))
            block.vector(lambda e: P.emit_engine(e, "V"))
            block.scalar(lambda e: P.emit_engine(e, "A"))
            block.sync(lambda e: P.emit_engine(e, "S"))
            block.gpsimd(lambda e: P.emit_engine(e, "G"))

    return nc


_NC_CACHE = {}


def _prep_weights(Wqkv, bqkv, Wproj, bproj, Wu, Wg, Wd):
    Wqkv = np.ascontiguousarray(np.asarray(Wqkv, dtype=np.float32))
    bqkv = np.ascontiguousarray(np.asarray(bqkv, dtype=np.float32))
    Wproj = np.ascontiguousarray(np.asarray(Wproj, dtype=np.float32))
    bproj = np.ascontiguousarray(np.asarray(bproj, dtype=np.float32))
    Wu = np.ascontiguousarray(np.asarray(Wu, dtype=np.float32))
    Wg = np.ascontiguousarray(np.asarray(Wg, dtype=np.float32))
    Wd = np.ascontiguousarray(np.asarray(Wd, dtype=np.float32))

    pieces = []
    for g in range(NG):
        hs = [g * HPG + i for i in range(HPG)]
        for h in hs:
            blk = Wqkv[h * 384:h * 384 + 128, :]
            pieces.append(blk.reshape(128, 16, 128).transpose(2, 1, 0)
                          .reshape(128, 2048))
        for h in hs:
            blk = Wqkv[h * 384 + 128:h * 384 + 256, :]
            pieces.append(blk.reshape(128, 16, 128).transpose(2, 1, 0)
                          .reshape(128, 2048))
        vrows = np.concatenate([Wqkv[h * 384 + 256:h * 384 + 384, :]
                                for h in hs], axis=0)
        vt = vrows.reshape(HPG * 128, 16, 128).transpose(2, 1, 0)
        pieces.append(np.ascontiguousarray(vt[:, :8, :]).reshape(128, 2048))
        pieces.append(np.ascontiguousarray(vt[:, 8:, :]).reshape(128, 2048))
    wqkvT = np.ascontiguousarray(np.stack(pieces)).reshape(-1)

    wprojT = np.ascontiguousarray(Wproj.T).reshape(-1)
    wuT = np.ascontiguousarray(
        Wu.reshape(NEC, 128, 16, 128).transpose(0, 3, 2, 1)).reshape(-1)
    wgT = np.ascontiguousarray(
        Wg.reshape(NEC, 128, 16, 128).transpose(0, 3, 2, 1)).reshape(-1)
    wdT = np.ascontiguousarray(
        Wd.reshape(2048, 64, 128).transpose(1, 2, 0)).reshape(-1)

    bq = np.ascontiguousarray(
        np.stack([bqkv[h * 384:h * 384 + 128] for h in range(H)], axis=1))
    bk = np.ascontiguousarray(
        np.stack([bqkv[h * 384 + 128:h * 384 + 256] for h in range(H)],
                 axis=1))
    bv_row = np.concatenate([bqkv[h * 384 + 256:h * 384 + 384]
                             for h in range(H)])
    bv_rep = np.ascontiguousarray(
        np.broadcast_to(bv_row.reshape(NG, HPG * 128)[:, None, :],
                        (NG, 128, HPG * 128))).reshape(-1)
    bproj_rep = np.ascontiguousarray(
        np.broadcast_to(bproj[None, :], (128, E))).reshape(-1)
    return dict(wqkvT=wqkvT, wprojT=wprojT, wuT=wuT, wgT=wgT, wdT=wdT,
                bq=bq, bk=bk, bv_rep=bv_rep, bproj_rep=bproj_rep)


def kernel(L0_attn_query, L0_attn_key, L0_attn_value, L0_attn_Wqkv,
           L0_attn_bqkv, L0_attn_Wproj, L0_attn_bproj, L0_ffn_W_up,
           L0_ffn_W_gate, L0_ffn_W_down):
    x = np.asarray(L0_attn_query, dtype=np.float32)
    assert x.shape == (8, T, E)
    wdict = _prep_weights(L0_attn_Wqkv, L0_attn_bqkv, L0_attn_Wproj,
                          L0_attn_bproj, L0_ffn_W_up, L0_ffn_W_gate,
                          L0_ffn_W_down)

    if "nc" not in _NC_CACHE:
        _NC_CACHE["nc"] = build_nc()
    nc = _NC_CACHE["nc"]

    in_maps = []
    for c in range(8):
        m = dict(wdict)
        m["x"] = np.ascontiguousarray(x[c])
        in_maps.append(m)
    res = run_bass_kernel_spmd(nc, in_maps, core_ids=list(range(8)))
    out = np.stack([np.asarray(r["out"], dtype=np.float32)
                    for r in res.results], axis=0)
    return out


# version 28
# speedup vs baseline: 1.1829x; 1.0695x over previous
import numpy as np
import ml_dtypes
import concourse.bass as bass
import concourse.mybir as mybir
from concourse.bass_utils import run_bass_kernel_spmd
from contextlib import ExitStack

F32 = mybir.dt.float32
BF16 = mybir.dt.bfloat16
AF = mybir.ActivationFunctionType
ALU = mybir.AluOpType
AX = mybir.AxisListType
BFNP = ml_dtypes.bfloat16

T = 2048
E = 2048
H = 16
HD = 128
DEX = 8192
NDC = 16
NTC = 16
HPG = 2
NG = H // HPG
NQTG = 8
NEC = 64
NQ = 4
SQRT_HD = float(HD) ** 0.5
LN_EPS = 1e-5
Q_EPS = 1e-8

ENGINES = ("T", "V", "A", "S", "G")


class Ctr:
    __slots__ = ("h", "n", "name")

    def __init__(self, h, name):
        self.h = h
        self.n = 0
        self.name = name


class Prog:
    def __init__(self, nc):
        self.nc = nc
        self.ops = []

    def op(self, eng, fn, waits=(), inc=None, dma=False, r=(), w=()):
        assert eng in ENGINES
        for (c, v) in waits:
            assert v <= c.n, f"wait on {c.name}>={v} but only {c.n} emitted"
        incs = []
        if inc is not None:
            amt = 16 if dma else 1
            inc.n += amt
            incs.append((inc, amt))
        self.ops.append((eng, fn, tuple(waits), tuple(incs),
                         frozenset(r), frozenset(w)))

    def emit_engine(self, eng_obj, name):
        lastw = {}
        dirty_r = set()
        dirty_w = set()
        for (e, fn, waits, incs, r, w) in self.ops:
            if e != name:
                continue
            for (c, v) in waits:
                if v <= 0:
                    continue
                if lastw.get(id(c), -1) < v:
                    eng_obj.wait_ge(c.h, v)
                    lastw[id(c)] = v
            if name != "T" and \
                    ((r & dirty_w) or (w & dirty_w) or (w & dirty_r)):
                eng_obj.drain()
                dirty_r.clear()
                dirty_w.clear()
            inst = fn(eng_obj)
            for (c, amt) in incs:
                inst.then_inc(c.h, amt)
            dirty_r |= r
            dirty_w |= w


def _flat2d(ap_flat, off, p, cols):
    return ap_flat[off:off + p * cols].rearrange("(p c) -> p c", p=p)


def build_nc():
    nc = bass.Bass("TRN2", target_bir_lowering=False)

    x_h = nc.dram_tensor("x", [T, E], F32, kind="ExternalInput").ap()
    wqkv_h = nc.dram_tensor("wqkvT", [NG * 6 * 128 * 2048], F32, kind="ExternalInput").ap()
    wproj_h = nc.dram_tensor("wprojT", [16 * 128 * 2048], F32, kind="ExternalInput").ap()
    wu_h = nc.dram_tensor("wuT", [NEC * 128 * 2048], F32, kind="ExternalInput").ap()
    wg_h = nc.dram_tensor("wgT", [NEC * 128 * 2048], F32, kind="ExternalInput").ap()
    wd_h = nc.dram_tensor("wdT", [256 * 128 * 512], F32, kind="ExternalInput").ap()
    bq_h = nc.dram_tensor("bq", [128, H], F32, kind="ExternalInput").ap()
    bk_h = nc.dram_tensor("bk", [128, H], F32, kind="ExternalInput").ap()
    bv_h = nc.dram_tensor("bv_rep", [NG * 128 * (HPG * HD)], F32, kind="ExternalInput").ap()
    bp_h = nc.dram_tensor("bproj_rep", [128 * 2048], F32, kind="ExternalInput").ap()
    out_h = nc.dram_tensor("out", [T, E], F32, kind="ExternalOutput").ap()

    attv_scr = nc.dram_tensor("attv_scr", [H * NQTG * 128 * 256], BF16, kind="Internal").ap()
    s_scr = nc.dram_tensor("s_scr", [NTC * 128 * 2048], BF16, kind="Internal").ap()
    tu_scr = nc.dram_tensor("tu_scr", [NEC * 128 * 2048], BF16, kind="Internal").ap()
    tg_scr = nc.dram_tensor("tg_scr", [NEC * 128 * 2048], BF16, kind="Internal").ap()
    td_scr = nc.dram_tensor("td_scr", [256 * 128 * 512], BF16, kind="Internal").ap()
    tiny_scr = nc.dram_tensor("tiny_scr", [128], F32, kind="Internal").ap()
    tiny_scr2 = nc.dram_tensor("tiny_scr2", [128], F32, kind="Internal").ap()

    ident_dram = nc.inline_tensor(np.eye(128, dtype=BFNP), name="identconst")

    cursor = nc.sbuf_top

    def alloc(name, shape, dt):
        nonlocal cursor
        esz = {F32: 4, BF16: 2}[dt]
        nb = esz
        for s in shape[1:]:
            nb *= s
        nb = (nb + 31) // 32 * 32
        cursor -= nb
        return nc.alloc_sbuf_tensor_at(name, shape, dt, offset=cursor)

    xn_fm_t = alloc("xn_fm", [128, NDC * T], BF16)
    xn_fm = xn_fm_t[:, :]
    xn_v = xn_fm.rearrange("p (dc t) -> p dc t", dc=NDC)
    ident = alloc("ident", [128, 128], BF16)
    ones_row = alloc("ones_row", [1, 128], F32)
    ones_bf = alloc("ones_bf", [1, 128], BF16)
    bq_s = alloc("bq_s", [128, H], F32)
    bk_s = alloc("bk_s", [128, H], F32)
    tnames = ("st_sum0 st_sum1 st_nm0 st_nm1 st_a0 st_a1 st_b0 st_b1 "
              "st_var0 st_var1 st_rec0 st_rec1 st_rstd0 st_rstd1 "
              "ngm0 ngm1 nms0 nms1 rsum0 rsum1 rsr0 rsr1 "
              "pw_u pw_g pw_d t11u t11g t11d tmp11 tmp11b "
              "cu_p ncu_p cg_p ncg_p cd_p ncd_p sun_p sgd_p").split()
    tt_ = {nm: alloc(nm, [128, 1], F32) for nm in tnames}
    pmax_ = [alloc("pmax0", [128, 4], F32), alloc("pmax1", [128, 4], F32)]
    trow = alloc("trow", [1, 128], F32)
    brow = alloc("brow", [1, 128], F32)
    partials = alloc("partials", [128, 192], F32)

    A0 = cursor
    q_buf, k_buf, v_buf = [None, None], [None, None], [None, None]
    for i in (0, 1):
        q_buf[i] = alloc(f"q_buf{i}", [128, HPG * T], BF16)
        k_buf[i] = alloc(f"k_buf{i}", [128, HPG * T], BF16)
        v_buf[i] = alloc(f"v_buf{i}", [128, NTC * HPG * HD], BF16)
    a_p3 = cursor
    cursor = A0
    wproj_bf_t = alloc("wproj_bf", [128, NDC * E], BF16)
    wproj_bf = wproj_bf_t[:, :]
    a_p4 = cursor
    cursor = A0
    hidden_t = alloc("hidden_q", [128, NEC * 512], BF16)
    hidden_q = hidden_t[:, :]
    cursor = min(a_p3, a_p4, cursor)

    B0 = cursor
    x_tile, xn_tm = [None, None], [None, None]
    for i in (0, 1):
        x_tile[i] = alloc(f"x_tile{i}", [128, E], F32)
        xn_tm[i] = alloc(f"xn_tm{i}", [128, E], BF16)
    sq_scr = alloc("sq_scr", [128, E], BF16)
    b_p1 = cursor
    cursor = B0
    wstage, w_bf = [None, None], [None, None]
    for i in (0, 1):
        wstage[i] = alloc(f"wstage{i}", [128, 1024], F32)
    for i in (0, 1):
        w_bf[i] = alloc(f"w_bf{i}", [128, 2048], BF16)
    p_tm_b = [None, None]
    p_tm_b[0] = alloc("p_tm0", [128, 2 * T], BF16)[:, :]
    p_tm_b[1] = alloc("p_tm1", [128, 2 * T], BF16)[:, :]
    p_pt_t = alloc("p_pt", [128, NTC * 256], BF16)
    p_pt = p_pt_t[:, :]
    av_evac = [None, None]
    bv_s = [None, None]
    for i in (0, 1):
        av_evac[i] = alloc(f"av_evac{i}", [128, 256], BF16)
    for i in (0, 1):
        bv_s[i] = alloc(f"bv_s{i}", [128, HPG * HD], F32)
    b_p3 = cursor
    cursor = B0
    wstage4, attv_pair, s_tile = [None, None], [None, None], [None, None]
    for i in (0, 1):
        wstage4[i] = alloc(f"wstage4_{i}", [128, 2048], F32)
    for i in (0, 1):
        attv_pair[i] = alloc(f"attv_pair{i}", [128, H * 256], BF16)
    s_tile[0] = s_tile[1] = alloc("s_tile0", [128, 2048], BF16)
    bproj_row = alloc("bproj_row", [1, 2048], BF16)
    b_p4 = cursor
    cursor = B0
    tu_b, tg_b, td_b, silu_b = [None, None], [None, None], [None, None], [None, None]
    g_tile, xs_strip, ss_strip, out_strip = [None, None], [None, None], [None, None], [None, None]
    for i in (0, 1):
        tu_b[i] = alloc(f"tu_b{i}", [128, 2048], BF16)
        tg_b[i] = alloc(f"tg_b{i}", [128, 2048], BF16)
    td_b = [None, None, None, None]
    for i in (0, 1, 2, 3):
        td_b[i] = alloc(f"td_b{i}", [128, 1024], BF16)
    for i in (0, 1):
        silu_b[i] = alloc(f"silu_b{i}", [128, 512], BF16)
        g_tile[i] = alloc(f"g_tile{i}", [128, 512], BF16)
        xs_strip[i] = alloc(f"xs_strip{i}", [128, 512], F32)
        ss_strip[i] = alloc(f"ss_strip{i}", [128, 512], BF16)
        out_strip[i] = alloc(f"out_strip{i}", [128, 512], F32)
    b_p6 = cursor
    cursor = min(b_p1, b_p3, b_p4, b_p6)

    qstage = [None, None]
    for i in (0, 1):
        qstage[i] = alloc(f"qstage{i}", [128, 2048], F32)
    abs_scr = alloc("abs_scr", [128, 2048], BF16)
    a_gt = abs_scr
    tq_out = [None, None]
    for i in (0, 1):
        tq_out[i] = alloc(f"tq_out{i}", [128, 2048], BF16)

    assert cursor > nc.sbuf_base + 1536, \
        f"SBUF overflow: cursor {cursor} base {nc.sbuf_base}"

    ps_big = nc.alloc_psum_tensor("ps_big", [128, 2048], F32)
    ps_pt = nc.alloc_psum_tensor("ps_pt", [128, 1024], BF16)
    ps_u = nc.alloc_psum_tensor("ps_u", [128, 512], F32)
    ps_g = nc.alloc_psum_tensor("ps_g", [128, 512], F32)
    ps_m = nc.alloc_psum_tensor("ps_m", [128, 512], F32)

    P = Prog(nc)
    op = P.op

    with ExitStack() as ctx:
        def sem(nm):
            return Ctr(ctx.enter_context(nc.semaphore(nm)), nm)

        QP = {q: [sem(f"d{q}{p}") for p in range(2)] for q in "SGA"}
        qix = {q: 0 for q in "SGA"}
        cT, cV, cA = sem("cT"), sem("cV"), sem("cA")
        cG = sem("cG")
        Z = (QP["S"][0], 0)

        def qdone(q):
            return tuple((c, c.n) for c in QP[q])

        def tmax(tokens):
            best = {}
            for (c, n) in tokens:
                if n > best.get(id(c), (None, -1))[1]:
                    best[id(c)] = (c, n)
            return tuple(best.values())

        def dma(q, dst, src, waits=(), r=(), w=()):
            ctr = QP[q][qix[q] % 2]
            qix[q] += 1
            prev = ctr.n
            op(q, lambda e, d=dst, s=src: e.dma_start(out=d, in_=s),
               waits=tuple(waits) + ((ctr, prev),), inc=ctr, dma=True,
               r=r, w=w)
            return (ctr, ctr.n)

        def vop(fn, waits=(), r=(), w=()):
            op("V", fn, waits=waits, inc=cV, r=r, w=w)
            return cV.n

        def aop(fn, waits=(), r=(), w=()):
            op("A", fn, waits=waits, inc=cA, r=r, w=w)
            return cA.n

        def top(fn, waits=(), inc=False, r=(), w=()):
            op("T", fn, waits=waits, inc=cT if inc else None, r=r, w=w)
            return cT.n

        def gop(fn, waits=(), r=(), w=()):
            op("G", fn, waits=waits, inc=cG, r=r, w=w)
            return cG.n

        dma("S", ident[:, :], ident_dram.ap()[:, :])
        dma("S", bq_s[:, :], bq_h)
        dma("S", bk_s[:, :], bk_h)
        P0_S = qdone("S")
        vop(lambda e: e.memset(ones_row[:, :], 1.0), w={"ones_row"})
        vop(lambda e: e.memset(ones_bf[:, :], 1.0), w={"ones_bf"})

        vdone_xt = [0, 0]
        adone_xt = [0, 0]
        pt_free_v = 0
        for tt in range(NTC):
            sl = tt % 2
            ss = tt % 2
            xt = f"xt{sl}"
            xt_ready = dma("G", x_tile[sl][:, :],
                           x_h[tt * 128:(tt + 1) * 128, :],
                           waits=[(cV, vdone_xt[sl]), (cA, adone_xt[sl])],
                           w={xt})
            vop(lambda e, sl=sl, ss=ss: e.reduce_sum(
                out=tt_[f"st_sum{ss}"][:, :], in_=x_tile[sl][:, :], axis=AX.X),
                waits=[xt_ready], r={xt}, w={f"st_sum{ss}"})
            a_sq = aop(lambda e, sl=sl, ss=ss: e.activation(
                out=sq_scr[:, :], in_=x_tile[sl][:, :], func=AF.Square,
                accum_out=tt_[f"st_a{ss}"][:, 0:1]),
                waits=[xt_ready], r={xt}, w={"sq_scr", f"st_a{ss}"})
            vop(lambda e, ss=ss: e.tensor_scalar(
                out=tt_[f"st_nm{ss}"][:, :], in0=tt_[f"st_sum{ss}"][:, :],
                scalar1=-1.0 / E, scalar2=None, op0=ALU.mult),
                r={f"st_sum{ss}"}, w={f"st_nm{ss}"})
            vop(lambda e, ss=ss: e.tensor_scalar(
                out=tt_[f"st_a{ss}"][:, :], in0=tt_[f"st_a{ss}"][:, :],
                scalar1=1.0 / E, scalar2=LN_EPS, op0=ALU.mult, op1=ALU.add),
                waits=[(cA, a_sq)], r={f"st_a{ss}"}, w={f"st_a{ss}"})
            vop(lambda e, ss=ss: e.tensor_tensor(
                out=tt_[f"st_b{ss}"][:, :], in0=tt_[f"st_nm{ss}"][:, :],
                in1=tt_[f"st_nm{ss}"][:, :], op=ALU.mult),
                r={f"st_nm{ss}"}, w={f"st_b{ss}"})
            vop(lambda e, ss=ss: e.tensor_sub(
                out=tt_[f"st_var{ss}"][:, :], in0=tt_[f"st_a{ss}"][:, :],
                in1=tt_[f"st_b{ss}"][:, :]),
                r={f"st_a{ss}", f"st_b{ss}"}, w={f"st_var{ss}"})
            v_rec = vop(lambda e, ss=ss: e.reciprocal(
                out=tt_[f"st_rec{ss}"][:, :], in_=tt_[f"st_var{ss}"][:, :]),
                r={f"st_var{ss}"}, w={f"st_rec{ss}"})
            a_rstd = aop(lambda e, ss=ss: e.activation(
                out=tt_[f"st_rstd{ss}"][:, :], in_=tt_[f"st_rec{ss}"][:, :],
                func=AF.Sqrt),
                waits=[(cV, v_rec)], r={f"st_rec{ss}"}, w={f"st_rstd{ss}"})
            v_xn = vop(lambda e, sl=sl, ss=ss: e.tensor_scalar(
                out=xn_tm[sl][:, :], in0=x_tile[sl][:, :],
                scalar1=tt_[f"st_nm{ss}"][:, 0:1],
                scalar2=tt_[f"st_rstd{ss}"][:, 0:1],
                op0=ALU.add, op1=ALU.mult),
                waits=[(cA, a_rstd)],
                r={xt, f"st_nm{ss}", f"st_rstd{ss}"}, w={f"xn_tm{sl}"})
            vdone_xt[sl] = v_xn
            adone_xt[sl] = a_sq
            for half in (0, 1):
                for i in range(8):
                    dc = half * 8 + i
                    top(lambda e, sl=sl, dc=dc, i=i: e.transpose(
                        out=ps_pt[:, i * 128:(i + 1) * 128],
                        in_=xn_tm[sl][:, dc * 128:(dc + 1) * 128],
                        identity=ident[:, :]),
                        waits=([(cV, v_xn), (cV, pt_free_v),
                                *P0_S] if i == 0 else ()),
                        inc=(i == 7), r={f"xn_tm{sl}", "ident"},
                        w={"bank4"})
                t_half = cT.n
                for j2 in (0, 1):
                    dc0 = half * 8 + j2 * 4
                    dst = xn_v[:, dc0:dc0 + 4, tt * 128:tt * 128 + 128]
                    pt_free_v = vop(
                        lambda e, dst=dst, j2=j2: e.tensor_copy(
                            out=dst, in_=ps_pt[:, j2 * 512:(j2 + 1) * 512]),
                        waits=[(cT, t_half)], r={"bank4"}, w={"xn_fm"})
        XN_V = cV.n
        P1_A = cA.n
        P1_T = cT.n
        bar13 = [(cV, XN_V), (cA, P1_A), (cT, P1_T)]

        a_absdone = [0, 0]
        q_vdone = [0, 0]
        g_wr = [Z, Z]
        qmode = {"load": "A", "store": "S"}
        tug_piece_done = {}
        tdq_piece_done = {}
        rt_free = [Z]

        def bcast(dst_pp, val11_name, tag):
            vb = vop(lambda e, val11_name=val11_name: e.tensor_scalar(
                out=brow[:, :], in0=ones_row[:, :],
                scalar1=tt_[val11_name][0:1, 0:1], scalar2=None, op0=ALU.mult),
                waits=[rt_free[0]],
                r={"ones_row", val11_name}, w={"brow"})
            d1 = dma("A", tiny_scr2.rearrange("(o p) -> o p", o=1),
                     brow[0:1, :], waits=[(cV, vb)], r={"brow"})
            d2 = dma("A", dst_pp[:, :],
                     tiny_scr2.rearrange("(p o) -> p o", p=128),
                     waits=[d1], w={tag})
            rt_free[0] = d2

        qloads = []
        qcomps = []
        qd_in = {}
        fins = {}
        srcs = {}
        dsts = {}
        wmeta = [("u", wu_h, tu_scr, 0), ("g", wg_h, tg_scr, 64),
                 ("d", wd_h, td_scr, 128)]
        for (wn, src, dst, pbase) in wmeta:
            def scale_load(c, wn=wn, src=src):
                sl = c % 2
                qd_in[(wn, c)] = dma(
                    "A", qstage[sl][:, :],
                    _flat2d(src, c * 262144, 128, 2048),
                    waits=[(cA, a_absdone[sl]), (cV, q_vdone[0]),
                           (cV, q_vdone[1])],
                    w={f"qstage{sl}"})

            def scale_comp(c, wn=wn, pbase=pbase):
                sl = c % 2
                a_absdone[sl] = aop(
                    lambda e, sl=sl, pc=pbase + c: e.activation(
                        out=abs_scr[:, :], in_=qstage[sl][:, :], func=AF.Abs,
                        accum_out=partials[:, pc:pc + 1]),
                    waits=[qd_in[(wn, c)]], r={f"qstage{sl}"},
                    w={"abs_scr", "partials"})
            for c in range(64):
                qloads.append(lambda c=c, f=scale_load: f(c))
                qcomps.append(lambda c=c, f=scale_comp: f(c))

            def finalize(wn=wn, pbase=pbase):
                v1 = vop(lambda e: e.reduce_sum(
                    out=tt_[f"pw_{wn}"][:, :],
                    in_=partials[:, pbase:pbase + 64], axis=AX.X),
                    waits=[(cA, a_absdone[0]), (cA, a_absdone[1])],
                    r={"partials"}, w={f"pw_{wn}"})
                d1 = dma("A", tiny_scr.rearrange("(p o) -> p o", p=128),
                         tt_[f"pw_{wn}"][:, :],
                         waits=[(cV, v1), rt_free[0]], r={f"pw_{wn}"})
                d2 = dma("A", trow[0:1, :],
                         tiny_scr.rearrange("(o p) -> o p", o=1),
                         waits=[d1], w={"trow"})
                rt_free[0] = d2
                vop(lambda e: e.reduce_sum(
                    out=tt_[f"t11{wn}"][0:1, :], in_=trow[:, :], axis=AX.X),
                    waits=[d2], r={"trow"}, w={f"t11{wn}"})
                vop(lambda e: e.tensor_scalar(
                    out=tt_[f"t11{wn}"][0:1, :], in0=tt_[f"t11{wn}"][0:1, :],
                    scalar1=1.0 / float(DEX * E), scalar2=Q_EPS,
                    op0=ALU.mult, op1=ALU.add),
                    r={f"t11{wn}"}, w={f"t11{wn}"})
                vop(lambda e: e.tensor_scalar(
                    out=tt_["tmp11"][0:1, :], in0=tt_[f"t11{wn}"][0:1, :],
                    scalar1=0.5, scalar2=None, op0=ALU.mult),
                    r={f"t11{wn}"}, w={"tmp11"})
                vop(lambda e: e.tensor_scalar(
                    out=tt_["tmp11b"][0:1, :], in0=tt_["tmp11"][0:1, :],
                    scalar1=-1.0, scalar2=None, op0=ALU.mult),
                    r={"tmp11"}, w={"tmp11b"})
                bcast(tt_[f"c{wn}_p"], "tmp11", f"c{wn}_p")
                bcast(tt_[f"nc{wn}_p"], "tmp11b", f"nc{wn}_p")
                if wn == "u":
                    bcast(tt_["sun_p"], "t11u", "sun_p")
                if wn == "d":
                    vop(lambda e: e.tensor_tensor(
                        out=tt_["tmp11"][0:1, :], in0=tt_["t11g"][0:1, :],
                        in1=tt_["t11d"][0:1, :], op=ALU.mult),
                        r={"t11g", "t11d"}, w={"tmp11"})
                    bcast(tt_["sgd_p"], "tmp11", "sgd_p")
                finalize.bc_done = qdone("A")
            finalize.bc_done = ()
            qloads.append(lambda: None)
            qcomps.append(finalize)
            fins[wn] = finalize
            srcs[wn] = src
            dsts[wn] = dst

        def quant_load(wn, c, sl):
            qd_in[(wn + "q", c)] = dma(
                qmode["load"], qstage[sl][:, :],
                _flat2d(srcs[wn], c * 262144, 128, 2048),
                waits=[(cV, q_vdone[sl]), (cA, a_absdone[0]),
                       (cA, a_absdone[1])],
                w={f"qstage{sl}"})

        def quant_comp(wn, c, sl):
            fin = fins[wn]
            d_in = qd_in[(wn + "q", c)]
            vop(lambda e, sl=sl, wn=wn: e.tensor_scalar(
                out=tq_out[sl][:, :], in0=qstage[sl][:, :],
                scalar1=tt_[f"nc{wn}_p"][:, 0:1], scalar2=None,
                op0=ALU.is_lt),
                waits=[d_in, *fin.bc_done, g_wr[sl]],
                r={f"qstage{sl}", f"nc{wn}_p"}, w={f"tq_out{sl}"})
            q_vdone[sl] = vop(lambda e, sl=sl, wn=wn: e.scalar_tensor_tensor(
                out=tq_out[sl][:, :], in0=qstage[sl][:, :],
                scalar=tt_[f"c{wn}_p"][:, 0:1], in1=tq_out[sl][:, :],
                op0=ALU.is_gt, op1=ALU.subtract),
                r={f"qstage{sl}", f"c{wn}_p", f"tq_out{sl}"},
                w={f"tq_out{sl}"})
            g_wr[sl] = dma(qmode["store"],
                           _flat2d(dsts[wn], c * 262144, 128, 2048),
                           tq_out[sl][:, :],
                           waits=[(cV, q_vdone[sl])], r={f"tq_out{sl}"})
            if wn == "d":
                tdq_piece_done[c] = g_wr[sl]
            else:
                tug_piece_done[(wn, c)] = g_wr[sl]

        qorder = []
        for c in range(64):
            qorder += [("u", c), ("g", c)]
        qorder += [("d", c) for c in range(64)]
        for gi, (wn, c) in enumerate(qorder):
            qloads.append(lambda wn=wn, c=c, sl=gi % 2: quant_load(wn, c, sl))
            qcomps.append(lambda wn=wn, c=c, sl=gi % 2: quant_comp(wn, c, sl))
        qload_it = iter(qloads)
        qcomp_it = iter(qcomps)
        qstate = {"loads_done": 0, "comps_done": 0,
                  "nl": len(qloads), "ncmp": len(qcomps)}

        def run_qloads(n):
            for _ in range(n):
                if qstate["loads_done"] >= qstate["comps_done"] + 2:
                    return
                stp = next(qload_it, None)
                if stp is None:
                    return
                stp()
                qstate["loads_done"] += 1

        def run_qpairs(n):
            for _ in range(n):
                run_qloads(1)
                run_qcomps(1)

        def run_qcomps(n):
            for _ in range(n):
                if qstate["comps_done"] >= qstate["loads_done"]:
                    return
                stp = next(qcomp_it, None)
                if stp is None:
                    return
                stp()
                qstate["comps_done"] += 1

        s_wstage_free = [0, 0]
        s_wbf_free = [0, 0]
        evac_ps = {0: 0, 1: 0}
        evac_ps_v = {0: 0, 1: 0}
        bv_free = [0, 0]
        groupbuf_free = [0, 0]
        attv_block_done = {}
        qcomp_per_slot = 3
        st_ = {"ptm_free_t": [0, 0], "pt_evac_v": 0, "pt_evac_a": 0,
               "ppt_free": 0, "psm_free": 0,
               "av_i": 0, "av_dma": [Z, Z], "scores_free": 0,
               "sidx": 0, "pend": None}

        def production_steps(g):
            gset = g % 2
            base = g * 6 * 128 * 2048
            piece_v = {}
            bv_ready = [Z]

            def load_bv():
                def do():
                    bv_ready[0] = dma(
                        "S", bv_s[gset][:, :],
                        _flat2d(bv_h, g * 128 * HPG * HD, 128, HPG * HD),
                        waits=bar13 + [(cV, bv_free[gset])], w={f"bv{gset}"})
                return do

            def load_piece(pi):
                def do():
                    sl = pi % 2
                    for hf in (0, 1):
                        d_in = dma(
                            "S", wstage[hf][:, :],
                            _flat2d(wqkv_h, base + pi * 262144,
                                    128, 2048)[:, hf * 1024:(hf + 1) * 1024],
                            waits=bar13 + [(cV, s_wstage_free[hf])],
                            w={f"wstage{hf}"})
                        piece_v[pi] = vop(
                            lambda e, sl=sl, hf=hf: e.tensor_copy(
                                out=w_bf[sl][:, hf * 1024:(hf + 1) * 1024],
                                in_=wstage[hf][:, :]),
                            waits=[d_in, (cT, s_wbf_free[sl])],
                            r={f"wstage{hf}"}, w={f"w_bf{sl}"})
                        s_wstage_free[hf] = piece_v[pi]
                return do

            def qk_fill(pi, which, hh, tg):
                def do():
                    sl = pi % 2
                    par = tg % 2
                    ps = ps_u if par == 0 else ps_g
                    for dc in range(NDC):
                        top(lambda e, sl=sl, dc=dc, tg=tg, ps=ps: e.matmul(
                            out=ps[:, :],
                            lhsT=w_bf[sl][:, dc * 128:(dc + 1) * 128],
                            rhs=xn_fm[:, dc * T + tg * 512:dc * T + tg * 512 + 512],
                            start=(dc == 0), stop=(dc == NDC - 1)),
                            waits=([(cV, piece_v[pi]), (cV, XN_V),
                                    (cA, evac_ps[par]),
                                    (cV, evac_ps_v[par])] if dc == 0 else ()),
                            inc=(dc == NDC - 1),
                            r={f"w_bf{pi % 2}", "xn_fm"},
                            w={f"bank{5 + par}"})
                    t_done = cT.n
                    s_wbf_free[sl] = t_done
                    dstb = q_buf if which == "q" else k_buf
                    bias = bq_s if which == "q" else bk_s
                    h_abs = g * HPG + hh
                    evac_ps[par] = aop(
                        lambda e, hh=hh, tg=tg, ps=ps, dstb=dstb,
                        bias=bias, h_abs=h_abs: e.activation(
                            out=dstb[gset][:, hh * T + tg * 512:
                                           hh * T + tg * 512 + 512],
                            in_=ps[:, :], func=AF.Identity,
                            bias=bias[:, h_abs:h_abs + 1], scale=1.0),
                        waits=[(cT, t_done), (cT, groupbuf_free[gset]),
                               *P0_S],
                        r={f"bank{5 + par}"}, w={f"qk{gset}"})
                return do

            def v_fill(tc):
                def do():
                    par = tc % 2
                    ps = ps_u if par == 0 else ps_g
                    for dc in range(NDC):
                        pi = 4 if dc < 8 else 5
                        sl = pi % 2
                        dcl = dc % 8
                        top(lambda e, sl=sl, dcl=dcl, dc=dc, tc=tc,
                            ps=ps: e.matmul(
                                out=ps[:, 0:HPG * HD],
                                lhsT=xn_fm[:, dc * T + tc * 128:
                                           dc * T + tc * 128 + 128],
                                rhs=w_bf[sl][:, dcl * (HPG * HD):
                                             (dcl + 1) * (HPG * HD)],
                                start=(dc == 0), stop=(dc == NDC - 1)),
                            waits=([(cV, piece_v[4]), (cV, piece_v[5]),
                                    (cA, evac_ps[par]),
                                    (cV, evac_ps_v[par])] if dc == 0 else ()),
                            inc=(dc == NDC - 1),
                            r={"w_bf0", "w_bf1", "xn_fm"},
                            w={f"bank{5 + par}"})
                    t_done = cT.n
                    s_wbf_free[0] = max(s_wbf_free[0], t_done)
                    s_wbf_free[1] = max(s_wbf_free[1], t_done)
                    evac_ps_v[par] = vop(
                        lambda e, tc=tc, ps=ps: e.tensor_tensor(
                            out=v_buf[gset][:, tc * (HPG * HD):
                                            (tc + 1) * (HPG * HD)],
                            in0=ps[:, 0:HPG * HD], in1=bv_s[gset][:, :],
                            op=ALU.add),
                        waits=[(cT, t_done), (cT, groupbuf_free[gset]),
                               bv_ready[0]],
                        r={f"bv{gset}", f"bank{5 + par}"}, w={f"vb{gset}"})
                return do

            steps = [load_bv(), load_piece(0), load_piece(1)]
            for tg in range(4):
                steps.append(qk_fill(0, "q", 0, tg))
            steps.append(load_piece(2))
            for tg in range(4):
                steps.append(qk_fill(1, "q", 1, tg))
            steps.append(load_piece(3))
            for tg in range(4):
                steps.append(qk_fill(2, "k", 0, tg))
            steps.append(load_piece(4))
            for tg in range(4):
                steps.append(qk_fill(3, "k", 1, tg))
            steps.append(load_piece(5))
            for tc in range(NTC):
                steps.append(v_fill(tc))
            return steps

        def softmax_qt(g, hh, qtg, sidx, qt):
            gset = g % 2
            pb = p_tm_b[sidx % 2]
            if True:
                q0 = qtg * 256 + qt * 128
                ss = qt
                for kc4 in range(4):
                    t_sc = top(lambda e, gset=gset, hh=hh, q0=q0,
                               kc4=kc4: e.matmul(
                        out=ps_big[:, kc4 * 512:(kc4 + 1) * 512],
                        lhsT=q_buf[gset][:, hh * T + q0:hh * T + q0 + 128],
                        rhs=k_buf[gset][:, hh * T + kc4 * 512:
                                        hh * T + kc4 * 512 + 512],
                        start=True, stop=True),
                        waits=([(cA, st_["scores_free"])] if kc4 == 0 else ()),
                        inc=(kc4 == 3), r={f"qk{gset}"}, w={f"bank{kc4}"})
                vop(lambda e, ss=ss: e.reduce_max(
                    out=tt_[f"ngm{ss}"][:, :], in_=ps_big[:, :], axis=AX.X,
                    negate=True),
                    waits=[(cT, t_sc)], r={"bank0", "bank1", "bank2", "bank3"}, w={f"ngm{ss}"})
                v_nms = vop(lambda e, ss=ss: e.tensor_scalar(
                    out=tt_[f"nms{ss}"][:, :], in0=tt_[f"ngm{ss}"][:, :],
                    scalar1=SQRT_HD, scalar2=None, op0=ALU.mult),
                    r={f"ngm{ss}"}, w={f"nms{ss}"})
                st_["scores_free"] = aop(lambda e, pb=pb, qt=qt,
                                         ss=ss: e.activation(
                    out=pb[:, qt * T:(qt + 1) * T], in_=ps_big[:, :],
                    func=AF.Exp, bias=tt_[f"nms{ss}"][:, 0:1], scale=SQRT_HD,
                    accum_out=tt_[f"rsum{ss}"][:, 0:1]),
                    waits=[(cV, v_nms), (cT, st_["ptm_free_t"][sidx % 2])],
                    r={f"nms{ss}", "bank0", "bank1", "bank2", "bank3"},
                    w={f"p_tm{sidx % 2}", f"rsum{ss}"})
                vop(lambda e, ss=ss: e.reciprocal(
                    out=tt_[f"rsr{ss}"][:, :], in_=tt_[f"rsum{ss}"][:, :]),
                    waits=[(cA, st_["scores_free"])],
                    r={f"rsum{ss}"}, w={f"rsr{ss}"})
                v_norm0 = vop(lambda e, pb=pb, qt=qt, ss=ss: e.tensor_scalar(
                    out=pb[:, qt * T:(qt + 1) * T],
                    in0=pb[:, qt * T:(qt + 1) * T],
                    scalar1=tt_[f"rsr{ss}"][:, 0:1], scalar2=None,
                    op0=ALU.mult),
                    r={f"rsr{ss}", f"p_tm{sidx % 2}"},
                    w={f"p_tm{sidx % 2}"})
            return v_norm0

        def attention_tail_head(ctx):
            (g, hh, qtg, sidx, v_norm) = ctx
            pb = p_tm_b[sidx % 2]
            ev_v = st_["pt_evac_v"]
            ev_a = st_["pt_evac_a"]
            for rnd in range(4):
                for s_i in range(8):
                    kc = rnd * 4 + s_i // 2
                    qt = s_i % 2
                    top(lambda e, pb=pb, kc=kc, qt=qt, s_i=s_i: e.transpose(
                        out=ps_pt[:, s_i * 128:(s_i + 1) * 128],
                        in_=pb[:, qt * T + kc * 128:qt * T + kc * 128 + 128],
                        identity=ident[:, :]),
                        waits=([(cV, v_norm), (cV, ev_v), (cA, ev_a)]
                               if s_i == 0 else ()),
                        inc=(s_i == 7), r={f"p_tm{sidx % 2}", "ident"},
                        w={"bank4"})
                t_tr = cT.n
                dstv = p_pt.rearrange("p (kc q t) -> p kc q t", kc=NTC, q=2)
                if rnd % 2 == 0:
                    ev_v = vop(lambda e, rnd=rnd, dstv=dstv: e.tensor_copy(
                        out=dstv[:, rnd * 4:(rnd + 1) * 4, :, :],
                        in_=ps_pt[:, :]),
                        waits=[(cT, t_tr), (cT, st_["ppt_free"])],
                        r={"bank4"}, w={"p_pt"})
                else:
                    ev_a = aop(lambda e, rnd=rnd, dstv=dstv: e.activation(
                        out=dstv[:, rnd * 4:(rnd + 1) * 4, :, :],
                        in_=ps_pt[:, :], func=AF.Copy),
                        waits=[(cT, t_tr), (cT, st_["ppt_free"])],
                        r={"bank4"}, w={"p_pt"})
            st_["pt_evac_v"] = ev_v
            st_["pt_evac_a"] = ev_a
            st_["ptm_free_t"][sidx % 2] = cT.n

        def attention_tail_attv(ctx):
            (g, hh, qtg, sidx, v_norm) = ctx
            gset = g % 2
            h_abs = g * HPG + hh
            ev_v = st_["pt_evac_v"]
            ev_a = st_["pt_evac_a"]
            for kc in range(NTC):
                top(lambda e, gset=gset, hh=hh, kc=kc: e.matmul(
                    out=ps_m[:, 0:256],
                    lhsT=v_buf[gset][:, kc * 256 + hh * 128:
                                     kc * 256 + hh * 128 + 128],
                    rhs=p_pt[:, kc * 256:(kc + 1) * 256],
                    start=(kc == 0), stop=(kc == NTC - 1)),
                    waits=([(cV, ev_v), (cA, ev_a),
                            (cA, st_["psm_free"])]
                           if kc == 0 else ()),
                    inc=(kc == NTC - 1), r={f"vb{gset}", "p_pt"},
                    w={"bank7"})
            t_av = cT.n
            st_["ppt_free"] = t_av
            avsl = st_["av_i"] % 2
            v_ave = aop(lambda e, avsl=avsl: e.activation(
                out=av_evac[avsl][:, :], in_=ps_m[:, 0:256], func=AF.Copy),
                waits=[(cT, t_av), st_["av_dma"][avsl]],
                r={"bank7"}, w={f"av{avsl}"})
            st_["psm_free"] = v_ave
            avdst = attv_scr.rearrange("(q p h t) -> q p h t", q=NQTG,
                                       p=128, h=H)[qtg, :, h_abs, :]
            st_["av_dma"][avsl] = dma(
                "G", avdst,
                av_evac[avsl][:, :], waits=[(cA, v_ave)], r={f"av{avsl}"})
            attv_block_done[(h_abs, qtg)] = st_["av_dma"][avsl]
            st_["av_i"] += 1
            if (hh, qtg) == (HPG - 1, NQTG - 1):
                groupbuf_free[g % 2] = cT.n

        prev_g = None
        nslots_total = NG * HPG * NQTG
        qper_slot = 3
        for g in range(NG + 1):
            steps = production_steps(g) if g < NG else []
            if prev_g is None:
                for stp in steps:
                    stp()
            else:
                slots = [(hh, qtg) for hh in range(HPG)
                         for qtg in range(NQTG)]
                per = (len(steps) + len(slots) - 2) // max(len(slots) - 1, 1)
                si = 0
                for slot_i, (hh, qtg) in enumerate(slots):
                    if slot_i > 0:
                        for _ in range(per):
                            if si < len(steps):
                                steps[si]()
                                si += 1
                    run_qloads(2)
                    softmax_qt(prev_g, hh, qtg, st_["sidx"], 0)
                    if st_["pend"] is not None:
                        attention_tail_head(st_["pend"])
                    vn1 = softmax_qt(prev_g, hh, qtg, st_["sidx"], 1)
                    ctx = (prev_g, hh, qtg, st_["sidx"], vn1)
                    st_["sidx"] += 1
                    run_qcomps(2)
                    if st_["pend"] is not None:
                        attention_tail_attv(st_["pend"])
                    st_["pend"] = ctx
                    run_qpairs(1)
                while si < len(steps):
                    steps[si]()
                    si += 1
                bv_free[prev_g % 2] = cV.n
            prev_g = g if g < NG else prev_g
        if st_["pend"] is not None:
            attention_tail_head(st_["pend"])
            attention_tail_attv(st_["pend"])
            st_["pend"] = None
        P3_T, P3_V, P3_A = cT.n, cV.n, cA.n
        P3_G = cG.n

        qmode["load"] = "S"
        qmode["store"] = "G"
        bar34 = [(cT, P3_T), (cV, P3_V), (cA, P3_A), *qdone("G")]
        bp_stage = dma("S", wstage4[0][0:1, :],
                       bp_h[0:2048].rearrange("(o c) -> o c", o=1),
                       waits=bar34, w={"wp0", "wp1", "wp2", "wp3"})
        bp_cast = vop(lambda e: e.tensor_copy(
            out=bproj_row[0:1, :], in_=wstage4[0][0:1, :]),
            waits=[bp_stage, (cT, P3_T)],
            r={"wp0", "wp1", "wp2", "wp3"}, w={"bproj_row"})
        wp_free = [bp_cast] * 4 + [0] * 4
        wp_cast = 0
        wpv_eg = []
        wpi = 0
        for eg in range(4):
            for dc in range(NDC):
                s = wpi % 8
                wpi += 1
                buf = wstage4[s // 4][:, (s % 4) * 512:(s % 4 + 1) * 512]
                d_in = dma("S", buf,
                           _flat2d(wproj_h, dc * 262144, 128, 2048)[
                               :, eg * 512:(eg + 1) * 512],
                           waits=bar34 + [(cV, wp_free[s])],
                           w={f"wp{s}"})
                wp_cast = vop(lambda e, buf=buf, dc=dc, eg=eg: e.tensor_copy(
                    out=wproj_bf[:, dc * T + eg * 512:dc * T + eg * 512 + 512],
                    in_=buf),
                    waits=[d_in, (cT, P3_T)],
                    r={f"wp{s}"}, w={f"wproj_q{eg}"})
                wp_free[s] = wp_cast
            wpv_eg.append(wp_cast)
        WP_V = wp_cast

        s_dma = [Z, Z]
        s_wr = {}
        pair_T = [P3_T, P3_T]
        psum_free_a = [P3_A, P3_A, P3_A, P3_A]
        for pair in range(NQTG):
            psl = pair % 2
            blk = tmax(attv_block_done[(h, pair)] for h in range(H))
            last_d = dma(
                "G", attv_pair[psl][:, :],
                _flat2d(attv_scr, pair * 128 * H * 256, 128, H * 256),
                waits=[*blk, (cT, pair_T[psl]), (cV, P3_V)],
                w={f"avp{psl}"})
            run_qpairs(10)
            for tcl in (0, 1):
                tc = pair * 2 + tcl
                ssl = 0
                for eg in range(4):
                    bank = (tc * 4 + eg) % 4
                    top(lambda e, eg=eg, bank=bank: e.matmul(
                        out=ps_big[:, bank * 512:(bank + 1) * 512],
                        lhsT=ones_bf[0:1, :],
                        rhs=bproj_row[0:1, eg * 512:(eg + 1) * 512],
                        start=True, stop=False),
                        waits=[last_d, (cV, wpv_eg[eg]),
                               (cA, psum_free_a[bank]), (cA, P3_A)],
                        r={"ones_bf", "bproj_row"}, w={f"bank{bank}"})
                    for dc in range(NDC):
                        top(lambda e, psl=psl, tcl=tcl, eg=eg, dc=dc,
                            bank=bank: e.matmul(
                                out=ps_big[:, bank * 512:(bank + 1) * 512],
                                lhsT=attv_pair[psl][:, dc * 256 + tcl * 128:
                                                    dc * 256 + tcl * 128 + 128],
                                rhs=wproj_bf[:, dc * T + eg * 512:
                                             dc * T + eg * 512 + 512],
                                start=False, stop=(dc == NDC - 1)),
                            inc=(dc == NDC - 1),
                            r={f"avp{psl}", f"wproj_q{eg}"}, w={f"bank{bank}"})
                    t_p = cT.n
                    psum_free_a[bank] = aop(
                        lambda e, ssl=ssl, eg=eg, bank=bank: e.activation(
                            out=s_tile[ssl][:, eg * 512:(eg + 1) * 512],
                            in_=ps_big[:, bank * 512:(bank + 1) * 512],
                            func=AF.Copy),
                        waits=[(cT, t_p), s_dma[ssl]],
                        r={f"bank{bank}"}, w={f"s_tile{ssl}"})
                s_dma[ssl] = dma(
                    "G", _flat2d(s_scr, tc * 262144, 128, 2048),
                    s_tile[ssl][:, :],
                    waits=[(cA, psum_free_a[(tc * 4 + 3) % 4])],
                    r={f"s_tile{ssl}"})
                s_wr[tc] = s_dma[ssl]
            pair_T[psl] = cT.n
        P4_T, P4_V, P4_A = cT.n, cV.n, cA.n
        P4_G = qdone("G")

        qmode["load"] = "G"
        qmode["store"] = "G"
        bar46 = [(cT, P4_T), (cV, WP_V), *P4_G]
        tu_dma = [Z, Z]
        tg_dma = [Z, Z]
        tu_cons = [0, 0]
        td_dma = [None, None, None, None]
        td_cons = [0, 0, 0, 0]
        silu_free = [0, 0]
        bankg = [(cA, psum_free_a[0]), (cA, psum_free_a[1]),
                 (cA, psum_free_a[2]), (cA, psum_free_a[3]),
                 (cA, P3_A), (cA, P3_A), (cA, P3_A), (cA, P3_A)]
        gx_dma = [Z, Z]
        gs_dma = [Z, Z]
        gt_cons = [0, 0]
        strip_v = [0, 0]
        out_dma = [Z, Z]
        psf32 = ps_pt[:, :].bitcast(F32)

        def bank(b):
            if b < 4:
                return ps_big[:, b * 512:(b + 1) * 512]
            return [psf32, ps_u, ps_g, ps_m][b - 4][:, 0:512] \
                if b > 4 else psf32[:, 0:512]

        while qstate["comps_done"] < qstate["ncmp"] and False:
            pass
        first6 = True
        for Q in range(NQ):
            for ec in range(NEC):
                sl = ec % 2
                run_qpairs(2)
                tu_dma[sl] = dma("S", tu_b[sl][:, :],
                                 _flat2d(tu_scr, ec * 262144, 128, 2048),
                                 waits=(bar46 if first6 else []) +
                                 [tug_piece_done[("u", ec)],
                                  (cT, tu_cons[sl])],
                                 w={f"tu{sl}"})
                tg_dma[sl] = dma("A", tg_b[sl][:, :],
                                 _flat2d(tg_scr, ec * 262144, 128, 2048),
                                 waits=(bar46 if first6 else []) +
                                 [tug_piece_done[("g", ec)],
                                  (cT, tu_cons[sl])],
                                 w={f"tg{sl}"})
                first6 = False
                par = ec % 2
                upb = ps_big[:, par * 512:(par + 1) * 512]
                gtb = ps_big[:, (2 + par) * 512:(3 + par) * 512]
                for dc in range(NDC):
                    top(lambda e, sl=sl, dc=dc, Q=Q, upb=upb: e.matmul(
                        out=upb,
                        lhsT=tu_b[sl][:, dc * 128:(dc + 1) * 128],
                        rhs=xn_fm[:, dc * T + Q * 512:dc * T + Q * 512 + 512],
                        start=(dc == 0), stop=(dc == NDC - 1)),
                        waits=([tu_dma[sl], bankg[par]] if dc == 0 else ()),
                        inc=(dc == NDC - 1), r={f"tu{sl}", "xn_fm"},
                        w={f"bank{par}"})
                t_up = cT.n
                for dc in range(NDC):
                    top(lambda e, sl=sl, dc=dc, Q=Q, gtb=gtb: e.matmul(
                        out=gtb,
                        lhsT=tg_b[sl][:, dc * 128:(dc + 1) * 128],
                        rhs=xn_fm[:, dc * T + Q * 512:dc * T + Q * 512 + 512],
                        start=(dc == 0), stop=(dc == NDC - 1)),
                        waits=([tg_dma[sl], bankg[2 + par]] if dc == 0
                               else ()),
                        inc=(dc == NDC - 1), r={f"tg{sl}", "xn_fm"},
                        w={f"bank{2 + par}"})
                t_gt = cT.n
                tu_cons[sl] = t_gt
                ssl = ec % 2
                a_silu = aop(lambda e, ssl=ssl, upb=upb: e.activation(
                    out=silu_b[ssl][:, :], in_=upb, func=AF.Silu,
                    bias=0.0, scale=tt_["sun_p"][:, 0:1]),
                    waits=[(cT, t_up), (cV, silu_free[ssl])],
                    r={"sun_p", f"bank{par}"}, w={f"silu{ssl}"})
                bankg[par] = (cA, a_silu)
                silu_free[ssl] = vop(
                    lambda e, ssl=ssl, ec=ec, gtb=gtb: e.scalar_tensor_tensor(
                        out=hidden_q[:, ec * 512:(ec + 1) * 512],
                        in0=gtb, scalar=0.0, in1=silu_b[ssl][:, :],
                        op0=ALU.add, op1=ALU.mult),
                    waits=[(cT, t_gt), (cA, a_silu)],
                    r={f"silu{ssl}", f"bank{2 + par}"}, w={"hidden_q"})
                bankg[2 + par] = (cV, silu_free[ssl])
            HID_V = cV.n
            for dmh in range(2):
                for ec in range(NEC):
                    sl = ec % 4
                    if ec % 8 == 0:
                        run_qloads(1)
                        run_qcomps(1)
                    qq = "S" if sl % 2 == 0 else "A"
                    td_dma[sl] = dma(
                        qq, td_b[sl][:, :],
                        _flat2d(td_scr, ec * 262144, 128, 2048)[
                            :, dmh * 1024:(dmh + 1) * 1024],
                        waits=[tdq_piece_done[ec],
                               (cT, td_cons[sl]), (cT, P4_T)],
                        w={f"td{sl}"})
                    for tc in range(4):
                        for d2 in range(2):
                            b = tc * 2 + d2
                            wl = []
                            if tc == 0 and d2 == 0:
                                wl.append(td_dma[sl])
                            if ec == 0:
                                wl += [(cV, HID_V), bankg[b]]
                            top(lambda e, sl=sl, ec=ec, tc=tc, d2=d2,
                                b=b: e.matmul(
                                    out=bank(b),
                                    lhsT=hidden_q[:, ec * 512 + tc * 128:
                                                  ec * 512 + tc * 128 + 128],
                                    rhs=td_b[sl][:, d2 * 512:(d2 + 1) * 512],
                                    start=(ec == 0), stop=(ec == NEC - 1)),
                                waits=wl,
                                inc=(b == 7), r={f"td{sl}", "hidden_q"})
                    td_cons[sl] = cT.n
                t_dn = cT.n
                for tc in range(4):
                    for d2 in range(2):
                        b = tc * 2 + d2
                        dmg = dmh * 2 + d2
                        gsl = b % 2
                        tglob = Q * 4 + tc
                        gx_dma[gsl] = dma(
                            "G", xs_strip[gsl][:, :],
                            x_h[tglob * 128:(tglob + 1) * 128,
                                dmg * 512:(dmg + 1) * 512],
                            waits=[(cV, strip_v[gsl]), (cT, P4_T)],
                            w={f"xs{gsl}"})
                        gs_dma[gsl] = dma(
                            "G", ss_strip[gsl][:, :],
                            _flat2d(s_scr, tglob * 262144, 128, 2048)[
                                :, dmg * 512:(dmg + 1) * 512],
                            waits=[(cV, strip_v[gsl]), s_wr[tglob],
                                   (cT, P4_T)],
                            w={f"ssb{gsl}"})
                        a_g = aop(lambda e, gsl=gsl, b=b: e.activation(
                            out=g_tile[gsl][:, :],
                            in_=bank(b),
                            func=AF.Silu, bias=0.0,
                            scale=tt_["sgd_p"][:, 0:1]),
                            waits=[(cT, t_dn), (cV, gt_cons[gsl])],
                            r={"sgd_p", f"bank{b}"}, w={f"g{gsl}"})
                        bankg[b] = (cA, a_g)
                        v_m = vop(lambda e, gsl=gsl: e.tensor_tensor(
                            out=out_strip[gsl][:, :],
                            in0=ss_strip[gsl][:, :],
                            in1=g_tile[gsl][:, :], op=ALU.mult),
                            waits=[(cA, a_g), gs_dma[gsl],
                                   out_dma[gsl]],
                            r={f"ssb{gsl}", f"g{gsl}"}, w={f"os{gsl}"})
                        gt_cons[gsl] = v_m
                        v_o = vop(lambda e, gsl=gsl: e.tensor_tensor(
                            out=out_strip[gsl][:, :],
                            in0=out_strip[gsl][:, :],
                            in1=xs_strip[gsl][:, :], op=ALU.add),
                            waits=[gx_dma[gsl]],
                            r={f"os{gsl}", f"xs{gsl}"}, w={f"os{gsl}"})
                        strip_v[gsl] = v_o
                        out_dma[gsl] = dma(
                            "G", out_h[tglob * 128:(tglob + 1) * 128,
                                       dmg * 512:(dmg + 1) * 512],
                            out_strip[gsl][:, :],
                            waits=[(cV, v_o)], r={f"os{gsl}"})

        op("G", lambda e: e.nop(),
           waits=list(qdone("G")) + list(qdone("S")) + list(qdone("A")))

        with nc.Block() as block:
            block.tensor(lambda e: P.emit_engine(e, "T"))
            block.vector(lambda e: P.emit_engine(e, "V"))
            block.scalar(lambda e: P.emit_engine(e, "A"))
            block.sync(lambda e: P.emit_engine(e, "S"))
            block.gpsimd(lambda e: P.emit_engine(e, "G"))

    return nc


_NC_CACHE = {}


def _prep_weights(Wqkv, bqkv, Wproj, bproj, Wu, Wg, Wd):
    Wqkv = np.ascontiguousarray(np.asarray(Wqkv, dtype=np.float32))
    bqkv = np.ascontiguousarray(np.asarray(bqkv, dtype=np.float32))
    Wproj = np.ascontiguousarray(np.asarray(Wproj, dtype=np.float32))
    bproj = np.ascontiguousarray(np.asarray(bproj, dtype=np.float32))
    Wu = np.ascontiguousarray(np.asarray(Wu, dtype=np.float32))
    Wg = np.ascontiguousarray(np.asarray(Wg, dtype=np.float32))
    Wd = np.ascontiguousarray(np.asarray(Wd, dtype=np.float32))

    pieces = []
    for g in range(NG):
        hs = [g * HPG + i for i in range(HPG)]
        for h in hs:
            blk = Wqkv[h * 384:h * 384 + 128, :]
            pieces.append(blk.reshape(128, 16, 128).transpose(2, 1, 0)
                          .reshape(128, 2048))
        for h in hs:
            blk = Wqkv[h * 384 + 128:h * 384 + 256, :]
            pieces.append(blk.reshape(128, 16, 128).transpose(2, 1, 0)
                          .reshape(128, 2048))
        vrows = np.concatenate([Wqkv[h * 384 + 256:h * 384 + 384, :]
                                for h in hs], axis=0)
        vt = vrows.reshape(HPG * 128, 16, 128).transpose(2, 1, 0)
        pieces.append(np.ascontiguousarray(vt[:, :8, :]).reshape(128, 2048))
        pieces.append(np.ascontiguousarray(vt[:, 8:, :]).reshape(128, 2048))
    wqkvT = np.ascontiguousarray(np.stack(pieces)).reshape(-1)

    wprojT = np.ascontiguousarray(Wproj.T).reshape(-1)
    wuT = np.ascontiguousarray(
        Wu.reshape(NEC, 128, 16, 128).transpose(0, 3, 2, 1)).reshape(-1)
    wgT = np.ascontiguousarray(
        Wg.reshape(NEC, 128, 16, 128).transpose(0, 3, 2, 1)).reshape(-1)
    wdT = np.ascontiguousarray(
        Wd.reshape(2048, 64, 128).transpose(1, 2, 0)).reshape(-1)

    bq = np.ascontiguousarray(
        np.stack([bqkv[h * 384:h * 384 + 128] for h in range(H)], axis=1))
    bk = np.ascontiguousarray(
        np.stack([bqkv[h * 384 + 128:h * 384 + 256] for h in range(H)],
                 axis=1))
    bv_row = np.concatenate([bqkv[h * 384 + 256:h * 384 + 384]
                             for h in range(H)])
    bv_rep = np.ascontiguousarray(
        np.broadcast_to(bv_row.reshape(NG, HPG * 128)[:, None, :],
                        (NG, 128, HPG * 128))).reshape(-1)
    bproj_rep = np.ascontiguousarray(
        np.broadcast_to(bproj[None, :], (128, E))).reshape(-1)
    return dict(wqkvT=wqkvT, wprojT=wprojT, wuT=wuT, wgT=wgT, wdT=wdT,
                bq=bq, bk=bk, bv_rep=bv_rep, bproj_rep=bproj_rep)


def kernel(L0_attn_query, L0_attn_key, L0_attn_value, L0_attn_Wqkv,
           L0_attn_bqkv, L0_attn_Wproj, L0_attn_bproj, L0_ffn_W_up,
           L0_ffn_W_gate, L0_ffn_W_down):
    x = np.asarray(L0_attn_query, dtype=np.float32)
    assert x.shape == (8, T, E)
    wdict = _prep_weights(L0_attn_Wqkv, L0_attn_bqkv, L0_attn_Wproj,
                          L0_attn_bproj, L0_ffn_W_up, L0_ffn_W_gate,
                          L0_ffn_W_down)

    if "nc" not in _NC_CACHE:
        _NC_CACHE["nc"] = build_nc()
    nc = _NC_CACHE["nc"]

    in_maps = []
    for c in range(8):
        m = dict(wdict)
        m["x"] = np.ascontiguousarray(x[c])
        in_maps.append(m)
    res = run_bass_kernel_spmd(nc, in_maps, core_ids=list(range(8)))
    out = np.stack([np.asarray(r["out"], dtype=np.float32)
                    for r in res.results], axis=0)
    return out


# version 29
# speedup vs baseline: 1.2013x; 1.0156x over previous
import numpy as np
import ml_dtypes
import concourse.bass as bass
import concourse.mybir as mybir
from concourse.bass_utils import run_bass_kernel_spmd
from contextlib import ExitStack

F32 = mybir.dt.float32
BF16 = mybir.dt.bfloat16
AF = mybir.ActivationFunctionType
ALU = mybir.AluOpType
AX = mybir.AxisListType
BFNP = ml_dtypes.bfloat16

T = 2048
E = 2048
H = 16
HD = 128
DEX = 8192
NDC = 16
NTC = 16
HPG = 2
NG = H // HPG
NQTG = 8
NEC = 64
NQ = 4
SQRT_HD = float(HD) ** 0.5
LN_EPS = 1e-5
Q_EPS = 1e-8

ENGINES = ("T", "V", "A", "S", "G")


class Ctr:
    __slots__ = ("h", "n", "name")

    def __init__(self, h, name):
        self.h = h
        self.n = 0
        self.name = name


class Prog:
    def __init__(self, nc):
        self.nc = nc
        self.ops = []

    def op(self, eng, fn, waits=(), inc=None, dma=False, r=(), w=()):
        assert eng in ENGINES
        for (c, v) in waits:
            assert v <= c.n, f"wait on {c.name}>={v} but only {c.n} emitted"
        incs = []
        if inc is not None:
            amt = 16 if dma else 1
            inc.n += amt
            incs.append((inc, amt))
        self.ops.append((eng, fn, tuple(waits), tuple(incs),
                         frozenset(r), frozenset(w)))

    def emit_engine(self, eng_obj, name):
        lastw = {}
        dirty_r = set()
        dirty_w = set()
        for (e, fn, waits, incs, r, w) in self.ops:
            if e != name:
                continue
            for (c, v) in waits:
                if v <= 0:
                    continue
                if lastw.get(id(c), -1) < v:
                    eng_obj.wait_ge(c.h, v)
                    lastw[id(c)] = v
            if name != "T" and \
                    ((r & dirty_w) or (w & dirty_w) or (w & dirty_r)):
                eng_obj.drain()
                dirty_r.clear()
                dirty_w.clear()
            inst = fn(eng_obj)
            for (c, amt) in incs:
                inst.then_inc(c.h, amt)
            dirty_r |= r
            dirty_w |= w


def _flat2d(ap_flat, off, p, cols):
    return ap_flat[off:off + p * cols].rearrange("(p c) -> p c", p=p)


def build_nc():
    nc = bass.Bass("TRN2", target_bir_lowering=False)

    x_h = nc.dram_tensor("x", [T, E], F32, kind="ExternalInput").ap()
    wqkv_h = nc.dram_tensor("wqkvT", [NG * 6 * 128 * 2048], F32, kind="ExternalInput").ap()
    wproj_h = nc.dram_tensor("wprojT", [16 * 128 * 2048], F32, kind="ExternalInput").ap()
    wu_h = nc.dram_tensor("wuT", [NEC * 128 * 2048], F32, kind="ExternalInput").ap()
    wg_h = nc.dram_tensor("wgT", [NEC * 128 * 2048], F32, kind="ExternalInput").ap()
    wd_h = nc.dram_tensor("wdT", [256 * 128 * 512], F32, kind="ExternalInput").ap()
    bq_h = nc.dram_tensor("bq", [128, H], F32, kind="ExternalInput").ap()
    bk_h = nc.dram_tensor("bk", [128, H], F32, kind="ExternalInput").ap()
    bv_h = nc.dram_tensor("bv_rep", [NG * 128 * (HPG * HD)], F32, kind="ExternalInput").ap()
    bp_h = nc.dram_tensor("bproj_rep", [128 * 2048], F32, kind="ExternalInput").ap()
    out_h = nc.dram_tensor("out", [T, E], F32, kind="ExternalOutput").ap()

    attv_scr = nc.dram_tensor("attv_scr", [H * NQTG * 128 * 256], BF16, kind="Internal").ap()
    s_scr = nc.dram_tensor("s_scr", [NTC * 128 * 2048], BF16, kind="Internal").ap()
    tu_scr = nc.dram_tensor("tu_scr", [NEC * 128 * 2048], BF16, kind="Internal").ap()
    tg_scr = nc.dram_tensor("tg_scr", [NEC * 128 * 2048], BF16, kind="Internal").ap()
    td_scr = nc.dram_tensor("td_scr", [256 * 128 * 512], BF16, kind="Internal").ap()
    tiny_scr = nc.dram_tensor("tiny_scr", [128], F32, kind="Internal").ap()
    tiny_scr2 = nc.dram_tensor("tiny_scr2", [128], F32, kind="Internal").ap()

    ident_dram = nc.inline_tensor(np.eye(128, dtype=BFNP), name="identconst")

    cursor = nc.sbuf_top

    def alloc(name, shape, dt):
        nonlocal cursor
        esz = {F32: 4, BF16: 2}[dt]
        nb = esz
        for s in shape[1:]:
            nb *= s
        nb = (nb + 31) // 32 * 32
        cursor -= nb
        return nc.alloc_sbuf_tensor_at(name, shape, dt, offset=cursor)

    xn_fm_t = alloc("xn_fm", [128, NDC * T], BF16)
    xn_fm = xn_fm_t[:, :]
    xn_v = xn_fm.rearrange("p (dc t) -> p dc t", dc=NDC)
    ident = alloc("ident", [128, 128], BF16)
    ones_row = alloc("ones_row", [1, 128], F32)
    ones_bf = alloc("ones_bf", [1, 128], BF16)
    bq_s = alloc("bq_s", [128, H], F32)
    bk_s = alloc("bk_s", [128, H], F32)
    tnames = ("st_sum0 st_sum1 st_nm0 st_nm1 st_a0 st_a1 st_b0 st_b1 "
              "st_var0 st_var1 st_rec0 st_rec1 st_rstd0 st_rstd1 "
              "ngm0 ngm1 nms0 nms1 rsum0 rsum1 rsr0 rsr1 "
              "pw_u pw_g pw_d t11u t11g t11d tmp11 tmp11b "
              "cu_p ncu_p cg_p ncg_p cd_p ncd_p sun_p sgd_p").split()
    tt_ = {nm: alloc(nm, [128, 1], F32) for nm in tnames}
    pmax_ = [alloc("pmax0", [128, 4], F32), alloc("pmax1", [128, 4], F32)]
    trow = alloc("trow", [1, 128], F32)
    brow = alloc("brow", [1, 128], F32)
    partials = alloc("partials", [128, 192], F32)

    A0 = cursor
    q_buf, k_buf, v_buf = [None, None], [None, None], [None, None]
    for i in (0, 1):
        q_buf[i] = alloc(f"q_buf{i}", [128, HPG * T], BF16)
        k_buf[i] = alloc(f"k_buf{i}", [128, HPG * T], BF16)
        v_buf[i] = alloc(f"v_buf{i}", [128, NTC * HPG * HD], BF16)
    a_p3 = cursor
    cursor = A0
    wproj_bf_t = alloc("wproj_bf", [128, NDC * E], BF16)
    wproj_bf = wproj_bf_t[:, :]
    a_p4 = cursor
    cursor = A0
    hidden_t = alloc("hidden_q", [128, NEC * 512], BF16)
    hidden_q = hidden_t[:, :]
    cursor = min(a_p3, a_p4, cursor)

    B0 = cursor
    x_tile, xn_tm = [None, None], [None, None]
    for i in (0, 1):
        x_tile[i] = alloc(f"x_tile{i}", [128, E], F32)
        xn_tm[i] = alloc(f"xn_tm{i}", [128, E], BF16)
    sq_scr = alloc("sq_scr", [128, E], BF16)
    b_p1 = cursor
    cursor = B0
    wstage, w_bf = [None, None], [None, None]
    for i in (0, 1):
        wstage[i] = alloc(f"wstage{i}", [128, 1024], F32)
    for i in (0, 1):
        w_bf[i] = alloc(f"w_bf{i}", [128, 2048], BF16)
    p_tm_b = [None, None]
    p_tm_b[0] = alloc("p_tm0", [128, 2 * T], BF16)[:, :]
    p_tm_b[1] = alloc("p_tm1", [128, 2 * T], BF16)[:, :]
    p_pt_t = alloc("p_pt", [128, NTC * 256], BF16)
    p_pt = p_pt_t[:, :]
    av_evac = [None, None]
    bv_s = [None, None]
    for i in (0, 1):
        av_evac[i] = alloc(f"av_evac{i}", [128, 256], BF16)
    for i in (0, 1):
        bv_s[i] = alloc(f"bv_s{i}", [128, HPG * HD], F32)
    b_p3 = cursor
    cursor = B0
    wstage4, attv_pair, s_tile = [None, None], [None, None], [None, None]
    for i in (0, 1):
        wstage4[i] = alloc(f"wstage4_{i}", [128, 2048], F32)
    for i in (0, 1):
        attv_pair[i] = alloc(f"attv_pair{i}", [128, H * 256], BF16)
    s_tile[0] = s_tile[1] = alloc("s_tile0", [128, 2048], BF16)
    bproj_row = alloc("bproj_row", [1, 2048], BF16)
    b_p4 = cursor
    cursor = B0
    tu_b, tg_b, td_b, silu_b = [None, None], [None, None], [None, None], [None, None]
    g_tile, xs_strip, ss_strip, out_strip = [None, None], [None, None], [None, None], [None, None]
    for i in (0, 1):
        tu_b[i] = alloc(f"tu_b{i}", [128, 2048], BF16)
        tg_b[i] = alloc(f"tg_b{i}", [128, 2048], BF16)
    td_b = [None, None, None, None]
    for i in (0, 1, 2, 3):
        td_b[i] = alloc(f"td_b{i}", [128, 1024], BF16)
    g_tile = [None, None, None, None]
    for i in (0, 1, 2, 3):
        g_tile[i] = alloc(f"g_tile{i}", [128, 512], BF16)
    for i in (0, 1):
        silu_b[i] = alloc(f"silu_b{i}", [128, 512], BF16)
        xs_strip[i] = alloc(f"xs_strip{i}", [128, 512], F32)
        ss_strip[i] = alloc(f"ss_strip{i}", [128, 512], BF16)
        out_strip[i] = alloc(f"out_strip{i}", [128, 512], F32)
    b_p6 = cursor
    cursor = min(b_p1, b_p3, b_p4, b_p6)

    qstage = [None, None]
    for i in (0, 1):
        qstage[i] = alloc(f"qstage{i}", [128, 2048], F32)
    abs_scr = alloc("abs_scr", [128, 2048], BF16)
    a_gt = abs_scr
    tq_out = [None, None]
    for i in (0, 1):
        tq_out[i] = alloc(f"tq_out{i}", [128, 2048], BF16)

    assert cursor > nc.sbuf_base + 1536, \
        f"SBUF overflow: cursor {cursor} base {nc.sbuf_base}"

    ps_big = nc.alloc_psum_tensor("ps_big", [128, 2048], F32)
    ps_pt = nc.alloc_psum_tensor("ps_pt", [128, 1024], BF16)
    ps_u = nc.alloc_psum_tensor("ps_u", [128, 512], F32)
    ps_g = nc.alloc_psum_tensor("ps_g", [128, 512], F32)
    ps_m = nc.alloc_psum_tensor("ps_m", [128, 512], F32)

    P = Prog(nc)
    op = P.op

    with ExitStack() as ctx:
        def sem(nm):
            return Ctr(ctx.enter_context(nc.semaphore(nm)), nm)

        QP = {q: [sem(f"d{q}{p}") for p in range(2)] for q in "SGA"}
        qix = {q: 0 for q in "SGA"}
        cT, cV, cA = sem("cT"), sem("cV"), sem("cA")
        cG = sem("cG")
        Z = (QP["S"][0], 0)

        def qdone(q):
            return tuple((c, c.n) for c in QP[q])

        def tmax(tokens):
            best = {}
            for (c, n) in tokens:
                if n > best.get(id(c), (None, -1))[1]:
                    best[id(c)] = (c, n)
            return tuple(best.values())

        def dma(q, dst, src, waits=(), r=(), w=()):
            ctr = QP[q][qix[q] % 2]
            qix[q] += 1
            prev = ctr.n
            op(q, lambda e, d=dst, s=src: e.dma_start(out=d, in_=s),
               waits=tuple(waits) + ((ctr, prev),), inc=ctr, dma=True,
               r=r, w=w)
            return (ctr, ctr.n)

        def vop(fn, waits=(), r=(), w=()):
            op("V", fn, waits=waits, inc=cV, r=r, w=w)
            return cV.n

        def aop(fn, waits=(), r=(), w=()):
            op("A", fn, waits=waits, inc=cA, r=r, w=w)
            return cA.n

        def top(fn, waits=(), inc=False, r=(), w=()):
            op("T", fn, waits=waits, inc=cT if inc else None, r=r, w=w)
            return cT.n

        def gop(fn, waits=(), r=(), w=()):
            op("G", fn, waits=waits, inc=cG, r=r, w=w)
            return cG.n

        dma("S", ident[:, :], ident_dram.ap()[:, :])
        dma("S", bq_s[:, :], bq_h)
        dma("S", bk_s[:, :], bk_h)
        P0_S = qdone("S")
        vop(lambda e: e.memset(ones_row[:, :], 1.0), w={"ones_row"})
        vop(lambda e: e.memset(ones_bf[:, :], 1.0), w={"ones_bf"})

        vdone_xt = [0, 0]
        adone_xt = [0, 0]
        pt_free_v = 0
        for tt in range(NTC):
            sl = tt % 2
            ss = tt % 2
            xt = f"xt{sl}"
            xt_ready = dma("G", x_tile[sl][:, :],
                           x_h[tt * 128:(tt + 1) * 128, :],
                           waits=[(cV, vdone_xt[sl]), (cA, adone_xt[sl])],
                           w={xt})
            vop(lambda e, sl=sl, ss=ss: e.reduce_sum(
                out=tt_[f"st_sum{ss}"][:, :], in_=x_tile[sl][:, :], axis=AX.X),
                waits=[xt_ready], r={xt}, w={f"st_sum{ss}"})
            a_sq = aop(lambda e, sl=sl, ss=ss: e.activation(
                out=sq_scr[:, :], in_=x_tile[sl][:, :], func=AF.Square,
                accum_out=tt_[f"st_a{ss}"][:, 0:1]),
                waits=[xt_ready], r={xt}, w={"sq_scr", f"st_a{ss}"})
            vop(lambda e, ss=ss: e.tensor_scalar(
                out=tt_[f"st_nm{ss}"][:, :], in0=tt_[f"st_sum{ss}"][:, :],
                scalar1=-1.0 / E, scalar2=None, op0=ALU.mult),
                r={f"st_sum{ss}"}, w={f"st_nm{ss}"})
            vop(lambda e, ss=ss: e.tensor_scalar(
                out=tt_[f"st_a{ss}"][:, :], in0=tt_[f"st_a{ss}"][:, :],
                scalar1=1.0 / E, scalar2=LN_EPS, op0=ALU.mult, op1=ALU.add),
                waits=[(cA, a_sq)], r={f"st_a{ss}"}, w={f"st_a{ss}"})
            vop(lambda e, ss=ss: e.tensor_tensor(
                out=tt_[f"st_b{ss}"][:, :], in0=tt_[f"st_nm{ss}"][:, :],
                in1=tt_[f"st_nm{ss}"][:, :], op=ALU.mult),
                r={f"st_nm{ss}"}, w={f"st_b{ss}"})
            vop(lambda e, ss=ss: e.tensor_sub(
                out=tt_[f"st_var{ss}"][:, :], in0=tt_[f"st_a{ss}"][:, :],
                in1=tt_[f"st_b{ss}"][:, :]),
                r={f"st_a{ss}", f"st_b{ss}"}, w={f"st_var{ss}"})
            v_rec = vop(lambda e, ss=ss: e.reciprocal(
                out=tt_[f"st_rec{ss}"][:, :], in_=tt_[f"st_var{ss}"][:, :]),
                r={f"st_var{ss}"}, w={f"st_rec{ss}"})
            a_rstd = aop(lambda e, ss=ss: e.activation(
                out=tt_[f"st_rstd{ss}"][:, :], in_=tt_[f"st_rec{ss}"][:, :],
                func=AF.Sqrt),
                waits=[(cV, v_rec)], r={f"st_rec{ss}"}, w={f"st_rstd{ss}"})
            v_xn = vop(lambda e, sl=sl, ss=ss: e.tensor_scalar(
                out=xn_tm[sl][:, :], in0=x_tile[sl][:, :],
                scalar1=tt_[f"st_nm{ss}"][:, 0:1],
                scalar2=tt_[f"st_rstd{ss}"][:, 0:1],
                op0=ALU.add, op1=ALU.mult),
                waits=[(cA, a_rstd)],
                r={xt, f"st_nm{ss}", f"st_rstd{ss}"}, w={f"xn_tm{sl}"})
            vdone_xt[sl] = v_xn
            adone_xt[sl] = a_sq
            for half in (0, 1):
                for i in range(8):
                    dc = half * 8 + i
                    top(lambda e, sl=sl, dc=dc, i=i: e.transpose(
                        out=ps_pt[:, i * 128:(i + 1) * 128],
                        in_=xn_tm[sl][:, dc * 128:(dc + 1) * 128],
                        identity=ident[:, :]),
                        waits=([(cV, v_xn), (cV, pt_free_v),
                                *P0_S] if i == 0 else ()),
                        inc=(i == 7), r={f"xn_tm{sl}", "ident"},
                        w={"bank4"})
                t_half = cT.n
                for j2 in (0, 1):
                    dc0 = half * 8 + j2 * 4
                    dst = xn_v[:, dc0:dc0 + 4, tt * 128:tt * 128 + 128]
                    pt_free_v = vop(
                        lambda e, dst=dst, j2=j2: e.tensor_copy(
                            out=dst, in_=ps_pt[:, j2 * 512:(j2 + 1) * 512]),
                        waits=[(cT, t_half)], r={"bank4"}, w={"xn_fm"})
        XN_V = cV.n
        P1_A = cA.n
        P1_T = cT.n
        bar13 = [(cV, XN_V), (cA, P1_A), (cT, P1_T)]

        a_absdone = [0, 0]
        q_vdone = [0, 0]
        g_wr = [Z, Z]
        qmode = {"load": "A", "store": "S"}
        tug_piece_done = {}
        tdq_piece_done = {}
        rt_free = [Z]

        def bcast(dst_pp, val11_name, tag):
            vb = vop(lambda e, val11_name=val11_name: e.tensor_scalar(
                out=brow[:, :], in0=ones_row[:, :],
                scalar1=tt_[val11_name][0:1, 0:1], scalar2=None, op0=ALU.mult),
                waits=[rt_free[0]],
                r={"ones_row", val11_name}, w={"brow"})
            d1 = dma("A", tiny_scr2.rearrange("(o p) -> o p", o=1),
                     brow[0:1, :], waits=[(cV, vb)], r={"brow"})
            d2 = dma("A", dst_pp[:, :],
                     tiny_scr2.rearrange("(p o) -> p o", p=128),
                     waits=[d1], w={tag})
            rt_free[0] = d2

        qloads = []
        qcomps = []
        qd_in = {}
        fins = {}
        srcs = {}
        dsts = {}
        wmeta = [("u", wu_h, tu_scr, 0), ("g", wg_h, tg_scr, 64),
                 ("d", wd_h, td_scr, 128)]
        for (wn, src, dst, pbase) in wmeta:
            def scale_load(c, wn=wn, src=src):
                sl = c % 2
                qd_in[(wn, c)] = dma(
                    "A", qstage[sl][:, :],
                    _flat2d(src, c * 262144, 128, 2048),
                    waits=[(cA, a_absdone[sl]), (cV, q_vdone[0]),
                           (cV, q_vdone[1])],
                    w={f"qstage{sl}"})

            def scale_comp(c, wn=wn, pbase=pbase):
                sl = c % 2
                a_absdone[sl] = aop(
                    lambda e, sl=sl, pc=pbase + c: e.activation(
                        out=abs_scr[:, :], in_=qstage[sl][:, :], func=AF.Abs,
                        accum_out=partials[:, pc:pc + 1]),
                    waits=[qd_in[(wn, c)]], r={f"qstage{sl}"},
                    w={"abs_scr", "partials"})
            for c in range(64):
                qloads.append(lambda c=c, f=scale_load: f(c))
                qcomps.append(lambda c=c, f=scale_comp: f(c))

            def finalize(wn=wn, pbase=pbase):
                v1 = vop(lambda e: e.reduce_sum(
                    out=tt_[f"pw_{wn}"][:, :],
                    in_=partials[:, pbase:pbase + 64], axis=AX.X),
                    waits=[(cA, a_absdone[0]), (cA, a_absdone[1])],
                    r={"partials"}, w={f"pw_{wn}"})
                d1 = dma("A", tiny_scr.rearrange("(p o) -> p o", p=128),
                         tt_[f"pw_{wn}"][:, :],
                         waits=[(cV, v1), rt_free[0]], r={f"pw_{wn}"})
                d2 = dma("A", trow[0:1, :],
                         tiny_scr.rearrange("(o p) -> o p", o=1),
                         waits=[d1], w={"trow"})
                rt_free[0] = d2
                vop(lambda e: e.reduce_sum(
                    out=tt_[f"t11{wn}"][0:1, :], in_=trow[:, :], axis=AX.X),
                    waits=[d2], r={"trow"}, w={f"t11{wn}"})
                vop(lambda e: e.tensor_scalar(
                    out=tt_[f"t11{wn}"][0:1, :], in0=tt_[f"t11{wn}"][0:1, :],
                    scalar1=1.0 / float(DEX * E), scalar2=Q_EPS,
                    op0=ALU.mult, op1=ALU.add),
                    r={f"t11{wn}"}, w={f"t11{wn}"})
                vop(lambda e: e.tensor_scalar(
                    out=tt_["tmp11"][0:1, :], in0=tt_[f"t11{wn}"][0:1, :],
                    scalar1=0.5, scalar2=None, op0=ALU.mult),
                    r={f"t11{wn}"}, w={"tmp11"})
                vop(lambda e: e.tensor_scalar(
                    out=tt_["tmp11b"][0:1, :], in0=tt_["tmp11"][0:1, :],
                    scalar1=-1.0, scalar2=None, op0=ALU.mult),
                    r={"tmp11"}, w={"tmp11b"})
                bcast(tt_[f"c{wn}_p"], "tmp11", f"c{wn}_p")
                bcast(tt_[f"nc{wn}_p"], "tmp11b", f"nc{wn}_p")
                if wn == "u":
                    bcast(tt_["sun_p"], "t11u", "sun_p")
                if wn == "d":
                    vop(lambda e: e.tensor_tensor(
                        out=tt_["tmp11"][0:1, :], in0=tt_["t11g"][0:1, :],
                        in1=tt_["t11d"][0:1, :], op=ALU.mult),
                        r={"t11g", "t11d"}, w={"tmp11"})
                    bcast(tt_["sgd_p"], "tmp11", "sgd_p")
                finalize.bc_done = qdone("A")
            finalize.bc_done = ()
            qloads.append(lambda: None)
            qcomps.append(finalize)
            fins[wn] = finalize
            srcs[wn] = src
            dsts[wn] = dst

        def quant_load(wn, c, sl):
            qd_in[(wn + "q", c)] = dma(
                qmode["load"], qstage[sl][:, :],
                _flat2d(srcs[wn], c * 262144, 128, 2048),
                waits=[(cV, q_vdone[sl]), (cA, a_absdone[0]),
                       (cA, a_absdone[1])],
                w={f"qstage{sl}"})

        def quant_comp(wn, c, sl):
            fin = fins[wn]
            d_in = qd_in[(wn + "q", c)]
            vop(lambda e, sl=sl, wn=wn: e.tensor_scalar(
                out=tq_out[sl][:, :], in0=qstage[sl][:, :],
                scalar1=tt_[f"nc{wn}_p"][:, 0:1], scalar2=None,
                op0=ALU.is_lt),
                waits=[d_in, *fin.bc_done, g_wr[sl]],
                r={f"qstage{sl}", f"nc{wn}_p"}, w={f"tq_out{sl}"})
            q_vdone[sl] = vop(lambda e, sl=sl, wn=wn: e.scalar_tensor_tensor(
                out=tq_out[sl][:, :], in0=qstage[sl][:, :],
                scalar=tt_[f"c{wn}_p"][:, 0:1], in1=tq_out[sl][:, :],
                op0=ALU.is_gt, op1=ALU.subtract),
                r={f"qstage{sl}", f"c{wn}_p", f"tq_out{sl}"},
                w={f"tq_out{sl}"})
            g_wr[sl] = dma(qmode["store"],
                           _flat2d(dsts[wn], c * 262144, 128, 2048),
                           tq_out[sl][:, :],
                           waits=[(cV, q_vdone[sl])], r={f"tq_out{sl}"})
            if wn == "d":
                tdq_piece_done[c] = g_wr[sl]
            else:
                tug_piece_done[(wn, c)] = g_wr[sl]

        qorder = []
        for c in range(64):
            qorder += [("u", c), ("g", c)]
        qorder += [("d", c) for c in range(64)]
        for gi, (wn, c) in enumerate(qorder):
            qloads.append(lambda wn=wn, c=c, sl=gi % 2: quant_load(wn, c, sl))
            qcomps.append(lambda wn=wn, c=c, sl=gi % 2: quant_comp(wn, c, sl))
        qload_it = iter(qloads)
        qcomp_it = iter(qcomps)
        qstate = {"loads_done": 0, "comps_done": 0,
                  "nl": len(qloads), "ncmp": len(qcomps)}

        def run_qloads(n):
            for _ in range(n):
                if qstate["loads_done"] >= qstate["comps_done"] + 2:
                    return
                stp = next(qload_it, None)
                if stp is None:
                    return
                stp()
                qstate["loads_done"] += 1

        def run_qpairs(n):
            for _ in range(n):
                run_qloads(1)
                run_qcomps(1)

        def run_qcomps(n):
            for _ in range(n):
                if qstate["comps_done"] >= qstate["loads_done"]:
                    return
                stp = next(qcomp_it, None)
                if stp is None:
                    return
                stp()
                qstate["comps_done"] += 1

        s_wstage_free = [0, 0]
        s_wbf_free = [0, 0]
        evac_ps = {0: 0, 1: 0}
        evac_ps_v = {0: 0, 1: 0}
        bv_free = [0, 0]
        groupbuf_free = [0, 0]
        attv_block_done = {}
        qcomp_per_slot = 3
        st_ = {"ptm_free_t": [0, 0], "pt_evac_v": 0, "pt_evac_a": 0,
               "ppt_free": 0, "psm_free": 0,
               "av_i": 0, "av_dma": [Z, Z], "scores_free": 0,
               "sidx": 0, "pend": None}

        def production_steps(g):
            gset = g % 2
            base = g * 6 * 128 * 2048
            piece_v = {}
            bv_ready = [Z]

            def load_bv():
                def do():
                    bv_ready[0] = dma(
                        "S", bv_s[gset][:, :],
                        _flat2d(bv_h, g * 128 * HPG * HD, 128, HPG * HD),
                        waits=bar13 + [(cV, bv_free[gset])], w={f"bv{gset}"})
                return do

            def load_piece(pi):
                def do():
                    sl = pi % 2
                    for hf in (0, 1):
                        d_in = dma(
                            "S", wstage[hf][:, :],
                            _flat2d(wqkv_h, base + pi * 262144,
                                    128, 2048)[:, hf * 1024:(hf + 1) * 1024],
                            waits=bar13 + [(cV, s_wstage_free[hf])],
                            w={f"wstage{hf}"})
                        piece_v[pi] = vop(
                            lambda e, sl=sl, hf=hf: e.tensor_copy(
                                out=w_bf[sl][:, hf * 1024:(hf + 1) * 1024],
                                in_=wstage[hf][:, :]),
                            waits=[d_in, (cT, s_wbf_free[sl])],
                            r={f"wstage{hf}"}, w={f"w_bf{sl}"})
                        s_wstage_free[hf] = piece_v[pi]
                return do

            def qk_fill(pi, which, hh, tg):
                def do():
                    sl = pi % 2
                    par = tg % 2
                    ps = ps_u if par == 0 else ps_g
                    for dc in range(NDC):
                        top(lambda e, sl=sl, dc=dc, tg=tg, ps=ps: e.matmul(
                            out=ps[:, :],
                            lhsT=w_bf[sl][:, dc * 128:(dc + 1) * 128],
                            rhs=xn_fm[:, dc * T + tg * 512:dc * T + tg * 512 + 512],
                            start=(dc == 0), stop=(dc == NDC - 1)),
                            waits=([(cV, piece_v[pi]), (cV, XN_V),
                                    (cA, evac_ps[par]),
                                    (cV, evac_ps_v[par])] if dc == 0 else ()),
                            inc=(dc == NDC - 1),
                            r={f"w_bf{pi % 2}", "xn_fm"},
                            w={f"bank{5 + par}"})
                    t_done = cT.n
                    s_wbf_free[sl] = t_done
                    dstb = q_buf if which == "q" else k_buf
                    bias = bq_s if which == "q" else bk_s
                    h_abs = g * HPG + hh
                    evac_ps[par] = aop(
                        lambda e, hh=hh, tg=tg, ps=ps, dstb=dstb,
                        bias=bias, h_abs=h_abs: e.activation(
                            out=dstb[gset][:, hh * T + tg * 512:
                                           hh * T + tg * 512 + 512],
                            in_=ps[:, :], func=AF.Identity,
                            bias=bias[:, h_abs:h_abs + 1], scale=1.0),
                        waits=[(cT, t_done), (cT, groupbuf_free[gset]),
                               *P0_S],
                        r={f"bank{5 + par}"}, w={f"qk{gset}"})
                return do

            def v_fill(tc):
                def do():
                    par = tc % 2
                    ps = ps_u if par == 0 else ps_g
                    for dc in range(NDC):
                        pi = 4 if dc < 8 else 5
                        sl = pi % 2
                        dcl = dc % 8
                        top(lambda e, sl=sl, dcl=dcl, dc=dc, tc=tc,
                            ps=ps: e.matmul(
                                out=ps[:, 0:HPG * HD],
                                lhsT=xn_fm[:, dc * T + tc * 128:
                                           dc * T + tc * 128 + 128],
                                rhs=w_bf[sl][:, dcl * (HPG * HD):
                                             (dcl + 1) * (HPG * HD)],
                                start=(dc == 0), stop=(dc == NDC - 1)),
                            waits=([(cV, piece_v[4]), (cV, piece_v[5]),
                                    (cA, evac_ps[par]),
                                    (cV, evac_ps_v[par])] if dc == 0 else ()),
                            inc=(dc == NDC - 1),
                            r={"w_bf0", "w_bf1", "xn_fm"},
                            w={f"bank{5 + par}"})
                    t_done = cT.n
                    s_wbf_free[0] = max(s_wbf_free[0], t_done)
                    s_wbf_free[1] = max(s_wbf_free[1], t_done)
                    evac_ps_v[par] = vop(
                        lambda e, tc=tc, ps=ps: e.tensor_tensor(
                            out=v_buf[gset][:, tc * (HPG * HD):
                                            (tc + 1) * (HPG * HD)],
                            in0=ps[:, 0:HPG * HD], in1=bv_s[gset][:, :],
                            op=ALU.add),
                        waits=[(cT, t_done), (cT, groupbuf_free[gset]),
                               bv_ready[0]],
                        r={f"bv{gset}", f"bank{5 + par}"}, w={f"vb{gset}"})
                return do

            steps = [load_bv(), load_piece(0), load_piece(1)]
            for tg in range(4):
                steps.append(qk_fill(0, "q", 0, tg))
            steps.append(load_piece(2))
            for tg in range(4):
                steps.append(qk_fill(1, "q", 1, tg))
            steps.append(load_piece(3))
            for tg in range(4):
                steps.append(qk_fill(2, "k", 0, tg))
            steps.append(load_piece(4))
            for tg in range(4):
                steps.append(qk_fill(3, "k", 1, tg))
            steps.append(load_piece(5))
            for tc in range(NTC):
                steps.append(v_fill(tc))
            return steps

        def softmax_qt(g, hh, qtg, sidx, qt):
            gset = g % 2
            pb = p_tm_b[sidx % 2]
            if True:
                q0 = qtg * 256 + qt * 128
                ss = qt
                for kc4 in range(4):
                    t_sc = top(lambda e, gset=gset, hh=hh, q0=q0,
                               kc4=kc4: e.matmul(
                        out=ps_big[:, kc4 * 512:(kc4 + 1) * 512],
                        lhsT=q_buf[gset][:, hh * T + q0:hh * T + q0 + 128],
                        rhs=k_buf[gset][:, hh * T + kc4 * 512:
                                        hh * T + kc4 * 512 + 512],
                        start=True, stop=True),
                        waits=([(cA, st_["scores_free"])] if kc4 == 0 else ()),
                        inc=(kc4 == 3), r={f"qk{gset}"}, w={f"bank{kc4}"})
                vop(lambda e, ss=ss: e.reduce_max(
                    out=tt_[f"ngm{ss}"][:, :], in_=ps_big[:, :], axis=AX.X,
                    negate=True),
                    waits=[(cT, t_sc)], r={"bank0", "bank1", "bank2", "bank3"}, w={f"ngm{ss}"})
                v_nms = vop(lambda e, ss=ss: e.tensor_scalar(
                    out=tt_[f"nms{ss}"][:, :], in0=tt_[f"ngm{ss}"][:, :],
                    scalar1=SQRT_HD, scalar2=None, op0=ALU.mult),
                    r={f"ngm{ss}"}, w={f"nms{ss}"})
                st_["scores_free"] = aop(lambda e, pb=pb, qt=qt,
                                         ss=ss: e.activation(
                    out=pb[:, qt * T:(qt + 1) * T], in_=ps_big[:, :],
                    func=AF.Exp, bias=tt_[f"nms{ss}"][:, 0:1], scale=SQRT_HD,
                    accum_out=tt_[f"rsum{ss}"][:, 0:1]),
                    waits=[(cV, v_nms), (cT, st_["ptm_free_t"][sidx % 2])],
                    r={f"nms{ss}", "bank0", "bank1", "bank2", "bank3"},
                    w={f"p_tm{sidx % 2}", f"rsum{ss}"})
                vop(lambda e, ss=ss: e.reciprocal(
                    out=tt_[f"rsr{ss}"][:, :], in_=tt_[f"rsum{ss}"][:, :]),
                    waits=[(cA, st_["scores_free"])],
                    r={f"rsum{ss}"}, w={f"rsr{ss}"})
                v_norm0 = vop(lambda e, pb=pb, qt=qt, ss=ss: e.tensor_scalar(
                    out=pb[:, qt * T:(qt + 1) * T],
                    in0=pb[:, qt * T:(qt + 1) * T],
                    scalar1=tt_[f"rsr{ss}"][:, 0:1], scalar2=None,
                    op0=ALU.mult),
                    r={f"rsr{ss}", f"p_tm{sidx % 2}"},
                    w={f"p_tm{sidx % 2}"})
            return v_norm0

        def attention_tail_head(ctx):
            (g, hh, qtg, sidx, v_norm) = ctx
            pb = p_tm_b[sidx % 2]
            ev_v = st_["pt_evac_v"]
            ev_a = st_["pt_evac_a"]
            for rnd in range(4):
                for s_i in range(8):
                    kc = rnd * 4 + s_i // 2
                    qt = s_i % 2
                    top(lambda e, pb=pb, kc=kc, qt=qt, s_i=s_i: e.transpose(
                        out=ps_pt[:, s_i * 128:(s_i + 1) * 128],
                        in_=pb[:, qt * T + kc * 128:qt * T + kc * 128 + 128],
                        identity=ident[:, :]),
                        waits=([(cV, v_norm), (cV, ev_v), (cA, ev_a)]
                               if s_i == 0 else ()),
                        inc=(s_i == 7), r={f"p_tm{sidx % 2}", "ident"},
                        w={"bank4"})
                t_tr = cT.n
                dstv = p_pt.rearrange("p (kc q t) -> p kc q t", kc=NTC, q=2)
                if rnd % 2 == 0:
                    ev_v = vop(lambda e, rnd=rnd, dstv=dstv: e.tensor_copy(
                        out=dstv[:, rnd * 4:(rnd + 1) * 4, :, :],
                        in_=ps_pt[:, :]),
                        waits=[(cT, t_tr), (cT, st_["ppt_free"])],
                        r={"bank4"}, w={"p_pt"})
                else:
                    ev_a = aop(lambda e, rnd=rnd, dstv=dstv: e.activation(
                        out=dstv[:, rnd * 4:(rnd + 1) * 4, :, :],
                        in_=ps_pt[:, :], func=AF.Copy),
                        waits=[(cT, t_tr), (cT, st_["ppt_free"])],
                        r={"bank4"}, w={"p_pt"})
            st_["pt_evac_v"] = ev_v
            st_["pt_evac_a"] = ev_a
            st_["ptm_free_t"][sidx % 2] = cT.n

        def attention_tail_attv(ctx):
            (g, hh, qtg, sidx, v_norm) = ctx
            gset = g % 2
            h_abs = g * HPG + hh
            ev_v = st_["pt_evac_v"]
            ev_a = st_["pt_evac_a"]
            for kc in range(NTC):
                top(lambda e, gset=gset, hh=hh, kc=kc: e.matmul(
                    out=ps_m[:, 0:256],
                    lhsT=v_buf[gset][:, kc * 256 + hh * 128:
                                     kc * 256 + hh * 128 + 128],
                    rhs=p_pt[:, kc * 256:(kc + 1) * 256],
                    start=(kc == 0), stop=(kc == NTC - 1)),
                    waits=([(cV, ev_v), (cA, ev_a),
                            (cA, st_["psm_free"])]
                           if kc == 0 else ()),
                    inc=(kc == NTC - 1), r={f"vb{gset}", "p_pt"},
                    w={"bank7"})
            t_av = cT.n
            st_["ppt_free"] = t_av
            avsl = st_["av_i"] % 2
            v_ave = aop(lambda e, avsl=avsl: e.activation(
                out=av_evac[avsl][:, :], in_=ps_m[:, 0:256], func=AF.Copy),
                waits=[(cT, t_av), st_["av_dma"][avsl]],
                r={"bank7"}, w={f"av{avsl}"})
            st_["psm_free"] = v_ave
            avdst = attv_scr.rearrange("(q p h t) -> q p h t", q=NQTG,
                                       p=128, h=H)[qtg, :, h_abs, :]
            st_["av_dma"][avsl] = dma(
                "G", avdst,
                av_evac[avsl][:, :], waits=[(cA, v_ave)], r={f"av{avsl}"})
            attv_block_done[(h_abs, qtg)] = st_["av_dma"][avsl]
            st_["av_i"] += 1
            if (hh, qtg) == (HPG - 1, NQTG - 1):
                groupbuf_free[g % 2] = cT.n

        prev_g = None
        nslots_total = NG * HPG * NQTG
        qper_slot = 3
        for g in range(NG + 1):
            steps = production_steps(g) if g < NG else []
            if prev_g is None:
                for stp in steps:
                    stp()
            else:
                slots = [(hh, qtg) for hh in range(HPG)
                         for qtg in range(NQTG)]
                per = (len(steps) + len(slots) - 2) // max(len(slots) - 1, 1)
                si = 0
                for slot_i, (hh, qtg) in enumerate(slots):
                    if slot_i > 0:
                        for _ in range(per):
                            if si < len(steps):
                                steps[si]()
                                si += 1
                    run_qloads(2)
                    softmax_qt(prev_g, hh, qtg, st_["sidx"], 0)
                    if st_["pend"] is not None:
                        attention_tail_head(st_["pend"])
                    vn1 = softmax_qt(prev_g, hh, qtg, st_["sidx"], 1)
                    ctx = (prev_g, hh, qtg, st_["sidx"], vn1)
                    st_["sidx"] += 1
                    run_qcomps(2)
                    if st_["pend"] is not None:
                        attention_tail_attv(st_["pend"])
                    st_["pend"] = ctx
                    run_qpairs(1)
                while si < len(steps):
                    steps[si]()
                    si += 1
                bv_free[prev_g % 2] = cV.n
            prev_g = g if g < NG else prev_g
        if st_["pend"] is not None:
            attention_tail_head(st_["pend"])
            attention_tail_attv(st_["pend"])
            st_["pend"] = None
        P3_T, P3_V, P3_A = cT.n, cV.n, cA.n
        P3_G = cG.n

        qmode["load"] = "S"
        qmode["store"] = "G"
        bar34 = [(cT, P3_T), (cV, P3_V), (cA, P3_A), *qdone("G")]
        bp_stage = dma("S", wstage4[0][0:1, :],
                       bp_h[0:2048].rearrange("(o c) -> o c", o=1),
                       waits=bar34, w={"wp0", "wp1", "wp2", "wp3"})
        bp_cast = vop(lambda e: e.tensor_copy(
            out=bproj_row[0:1, :], in_=wstage4[0][0:1, :]),
            waits=[bp_stage, (cT, P3_T)],
            r={"wp0", "wp1", "wp2", "wp3"}, w={"bproj_row"})
        wp_free = [bp_cast] * 4 + [0] * 4
        wp_cast = 0
        wpv_eg = []
        wpi = 0
        for eg in range(4):
            for dc in range(NDC):
                s = wpi % 8
                wpi += 1
                buf = wstage4[s // 4][:, (s % 4) * 512:(s % 4 + 1) * 512]
                d_in = dma("S", buf,
                           _flat2d(wproj_h, dc * 262144, 128, 2048)[
                               :, eg * 512:(eg + 1) * 512],
                           waits=bar34 + [(cV, wp_free[s])],
                           w={f"wp{s}"})
                wp_cast = vop(lambda e, buf=buf, dc=dc, eg=eg: e.tensor_copy(
                    out=wproj_bf[:, dc * T + eg * 512:dc * T + eg * 512 + 512],
                    in_=buf),
                    waits=[d_in, (cT, P3_T)],
                    r={f"wp{s}"}, w={f"wproj_q{eg}"})
                wp_free[s] = wp_cast
            wpv_eg.append(wp_cast)
        WP_V = wp_cast

        s_dma = [Z, Z]
        s_wr = {}
        pair_T = [P3_T, P3_T]
        psum_free_a = [P3_A, P3_A, P3_A, P3_A]
        for pair in range(NQTG):
            psl = pair % 2
            blk = tmax(attv_block_done[(h, pair)] for h in range(H))
            last_d = dma(
                "G", attv_pair[psl][:, :],
                _flat2d(attv_scr, pair * 128 * H * 256, 128, H * 256),
                waits=[*blk, (cT, pair_T[psl]), (cV, P3_V)],
                w={f"avp{psl}"})
            run_qpairs(10)
            for tcl in (0, 1):
                tc = pair * 2 + tcl
                ssl = 0
                for eg in range(4):
                    bank = (tc * 4 + eg) % 4
                    top(lambda e, eg=eg, bank=bank: e.matmul(
                        out=ps_big[:, bank * 512:(bank + 1) * 512],
                        lhsT=ones_bf[0:1, :],
                        rhs=bproj_row[0:1, eg * 512:(eg + 1) * 512],
                        start=True, stop=False),
                        waits=[last_d, (cV, wpv_eg[eg]),
                               (cA, psum_free_a[bank]), (cA, P3_A)],
                        r={"ones_bf", "bproj_row"}, w={f"bank{bank}"})
                    for dc in range(NDC):
                        top(lambda e, psl=psl, tcl=tcl, eg=eg, dc=dc,
                            bank=bank: e.matmul(
                                out=ps_big[:, bank * 512:(bank + 1) * 512],
                                lhsT=attv_pair[psl][:, dc * 256 + tcl * 128:
                                                    dc * 256 + tcl * 128 + 128],
                                rhs=wproj_bf[:, dc * T + eg * 512:
                                             dc * T + eg * 512 + 512],
                                start=False, stop=(dc == NDC - 1)),
                            inc=(dc == NDC - 1),
                            r={f"avp{psl}", f"wproj_q{eg}"}, w={f"bank{bank}"})
                    t_p = cT.n
                    psum_free_a[bank] = aop(
                        lambda e, ssl=ssl, eg=eg, bank=bank: e.activation(
                            out=s_tile[ssl][:, eg * 512:(eg + 1) * 512],
                            in_=ps_big[:, bank * 512:(bank + 1) * 512],
                            func=AF.Copy),
                        waits=[(cT, t_p), s_dma[ssl]],
                        r={f"bank{bank}"}, w={f"s_tile{ssl}"})
                s_dma[ssl] = dma(
                    "G", _flat2d(s_scr, tc * 262144, 128, 2048),
                    s_tile[ssl][:, :],
                    waits=[(cA, psum_free_a[(tc * 4 + 3) % 4])],
                    r={f"s_tile{ssl}"})
                s_wr[tc] = s_dma[ssl]
            pair_T[psl] = cT.n
        P4_T, P4_V, P4_A = cT.n, cV.n, cA.n
        P4_G = qdone("G")

        qmode["load"] = "G"
        qmode["store"] = "G"
        bar46 = [(cT, P4_T), (cV, WP_V), *P4_G]
        tu_dma = [Z, Z]
        tg_dma = [Z, Z]
        tu_cons = [0, 0]
        td_dma = [None, None, None, None]
        td_cons = [0, 0, 0, 0]
        silu_free = [0, 0]
        bankg = [(cA, psum_free_a[0]), (cA, psum_free_a[1]),
                 (cA, psum_free_a[2]), (cA, psum_free_a[3]),
                 (cA, P3_A), (cA, P3_A), (cA, P3_A), (cA, P3_A)]
        gx_dma = [Z, Z]
        gs_dma = [Z, Z]
        gt_cons = [0, 0, 0, 0]
        strip_v = [0, 0]
        out_dma = [Z, Z]
        psf32 = ps_pt[:, :].bitcast(F32)

        def bank(b):
            if b < 4:
                return ps_big[:, b * 512:(b + 1) * 512]
            return [psf32, ps_u, ps_g, ps_m][b - 4][:, 0:512] \
                if b > 4 else psf32[:, 0:512]

        while qstate["comps_done"] < qstate["ncmp"] and False:
            pass
        first6 = True
        for Q in range(NQ):
            for ec in range(NEC):
                sl = ec % 2
                run_qpairs(2)
                tu_dma[sl] = dma("S", tu_b[sl][:, :],
                                 _flat2d(tu_scr, ec * 262144, 128, 2048),
                                 waits=(bar46 if first6 else []) +
                                 [tug_piece_done[("u", ec)],
                                  (cT, tu_cons[sl])],
                                 w={f"tu{sl}"})
                tg_dma[sl] = dma("A", tg_b[sl][:, :],
                                 _flat2d(tg_scr, ec * 262144, 128, 2048),
                                 waits=(bar46 if first6 else []) +
                                 [tug_piece_done[("g", ec)],
                                  (cT, tu_cons[sl])],
                                 w={f"tg{sl}"})
                first6 = False
                par = ec % 2
                upb = ps_big[:, par * 512:(par + 1) * 512]
                gtb = ps_big[:, (2 + par) * 512:(3 + par) * 512]
                for dc in range(NDC):
                    top(lambda e, sl=sl, dc=dc, Q=Q, upb=upb: e.matmul(
                        out=upb,
                        lhsT=tu_b[sl][:, dc * 128:(dc + 1) * 128],
                        rhs=xn_fm[:, dc * T + Q * 512:dc * T + Q * 512 + 512],
                        start=(dc == 0), stop=(dc == NDC - 1)),
                        waits=([tu_dma[sl], bankg[par]] if dc == 0 else ()),
                        inc=(dc == NDC - 1), r={f"tu{sl}", "xn_fm"},
                        w={f"bank{par}"})
                t_up = cT.n
                for dc in range(NDC):
                    top(lambda e, sl=sl, dc=dc, Q=Q, gtb=gtb: e.matmul(
                        out=gtb,
                        lhsT=tg_b[sl][:, dc * 128:(dc + 1) * 128],
                        rhs=xn_fm[:, dc * T + Q * 512:dc * T + Q * 512 + 512],
                        start=(dc == 0), stop=(dc == NDC - 1)),
                        waits=([tg_dma[sl], bankg[2 + par]] if dc == 0
                               else ()),
                        inc=(dc == NDC - 1), r={f"tg{sl}", "xn_fm"},
                        w={f"bank{2 + par}"})
                t_gt = cT.n
                tu_cons[sl] = t_gt
                ssl = ec % 2
                a_silu = aop(lambda e, ssl=ssl, upb=upb: e.activation(
                    out=silu_b[ssl][:, :], in_=upb, func=AF.Silu,
                    bias=0.0, scale=tt_["sun_p"][:, 0:1]),
                    waits=[(cT, t_up), (cV, silu_free[ssl])],
                    r={"sun_p", f"bank{par}"}, w={f"silu{ssl}"})
                bankg[par] = (cA, a_silu)
                silu_free[ssl] = vop(
                    lambda e, ssl=ssl, ec=ec, gtb=gtb: e.scalar_tensor_tensor(
                        out=hidden_q[:, ec * 512:(ec + 1) * 512],
                        in0=gtb, scalar=0.0, in1=silu_b[ssl][:, :],
                        op0=ALU.add, op1=ALU.mult),
                    waits=[(cT, t_gt), (cA, a_silu)],
                    r={f"silu{ssl}", f"bank{2 + par}"}, w={"hidden_q"})
                bankg[2 + par] = (cV, silu_free[ssl])
            HID_V = cV.n
            for dmh in range(2):
                for ec in range(NEC):
                    sl = ec % 4
                    if ec % 8 == 0:
                        run_qloads(1)
                        run_qcomps(1)
                    qq = "S" if sl % 2 == 0 else "A"
                    td_dma[sl] = dma(
                        qq, td_b[sl][:, :],
                        _flat2d(td_scr, ec * 262144, 128, 2048)[
                            :, dmh * 1024:(dmh + 1) * 1024],
                        waits=[tdq_piece_done[ec],
                               (cT, td_cons[sl]), (cT, P4_T)],
                        w={f"td{sl}"})
                    for tc in range(4):
                        for d2 in range(2):
                            b = tc * 2 + d2
                            wl = []
                            if tc == 0 and d2 == 0:
                                wl.append(td_dma[sl])
                            if ec == 0:
                                wl += [(cV, HID_V), bankg[b]]
                            top(lambda e, sl=sl, ec=ec, tc=tc, d2=d2,
                                b=b: e.matmul(
                                    out=bank(b),
                                    lhsT=hidden_q[:, ec * 512 + tc * 128:
                                                  ec * 512 + tc * 128 + 128],
                                    rhs=td_b[sl][:, d2 * 512:(d2 + 1) * 512],
                                    start=(ec == 0), stop=(ec == NEC - 1)),
                                waits=wl,
                                inc=(b == 7), r={f"td{sl}", "hidden_q"})
                    td_cons[sl] = cT.n
                t_dn = cT.n
                for tc in range(4):
                    for d2 in range(2):
                        b = tc * 2 + d2
                        dmg = dmh * 2 + d2
                        gsl = b % 2
                        tglob = Q * 4 + tc
                        gx_dma[gsl] = dma(
                            "G", xs_strip[gsl][:, :],
                            x_h[tglob * 128:(tglob + 1) * 128,
                                dmg * 512:(dmg + 1) * 512],
                            waits=[(cV, strip_v[gsl]), (cT, P4_T)],
                            w={f"xs{gsl}"})
                        gs_dma[gsl] = dma(
                            "S", ss_strip[gsl][:, :],
                            _flat2d(s_scr, tglob * 262144, 128, 2048)[
                                :, dmg * 512:(dmg + 1) * 512],
                            waits=[(cV, strip_v[gsl]), s_wr[tglob],
                                   (cT, P4_T)],
                            w={f"ssb{gsl}"})
                        g4 = b % 4
                        a_g = aop(lambda e, g4=g4, b=b: e.activation(
                            out=g_tile[g4][:, :],
                            in_=bank(b),
                            func=AF.Silu, bias=0.0,
                            scale=tt_["sgd_p"][:, 0:1]),
                            waits=[(cT, t_dn), (cV, gt_cons[g4])],
                            r={"sgd_p", f"bank{b}"}, w={f"g{g4}"})
                        bankg[b] = (cA, a_g)
                        v_m = vop(lambda e, gsl=gsl, g4=g4: e.tensor_tensor(
                            out=out_strip[gsl][:, :],
                            in0=ss_strip[gsl][:, :],
                            in1=g_tile[g4][:, :], op=ALU.mult),
                            waits=[(cA, a_g), gs_dma[gsl],
                                   out_dma[gsl]],
                            r={f"ssb{gsl}", f"g{g4}"}, w={f"os{gsl}"})
                        gt_cons[g4] = v_m
                        v_o = vop(lambda e, gsl=gsl: e.tensor_tensor(
                            out=out_strip[gsl][:, :],
                            in0=out_strip[gsl][:, :],
                            in1=xs_strip[gsl][:, :], op=ALU.add),
                            waits=[gx_dma[gsl]],
                            r={f"os{gsl}", f"xs{gsl}"}, w={f"os{gsl}"})
                        strip_v[gsl] = v_o
                        out_dma[gsl] = dma(
                            "G", out_h[tglob * 128:(tglob + 1) * 128,
                                       dmg * 512:(dmg + 1) * 512],
                            out_strip[gsl][:, :],
                            waits=[(cV, v_o)], r={f"os{gsl}"})

        op("G", lambda e: e.nop(),
           waits=list(qdone("G")) + list(qdone("S")) + list(qdone("A")))

        with nc.Block() as block:
            block.tensor(lambda e: P.emit_engine(e, "T"))
            block.vector(lambda e: P.emit_engine(e, "V"))
            block.scalar(lambda e: P.emit_engine(e, "A"))
            block.sync(lambda e: P.emit_engine(e, "S"))
            block.gpsimd(lambda e: P.emit_engine(e, "G"))

    return nc


_NC_CACHE = {}


def _prep_weights(Wqkv, bqkv, Wproj, bproj, Wu, Wg, Wd):
    Wqkv = np.ascontiguousarray(np.asarray(Wqkv, dtype=np.float32))
    bqkv = np.ascontiguousarray(np.asarray(bqkv, dtype=np.float32))
    Wproj = np.ascontiguousarray(np.asarray(Wproj, dtype=np.float32))
    bproj = np.ascontiguousarray(np.asarray(bproj, dtype=np.float32))
    Wu = np.ascontiguousarray(np.asarray(Wu, dtype=np.float32))
    Wg = np.ascontiguousarray(np.asarray(Wg, dtype=np.float32))
    Wd = np.ascontiguousarray(np.asarray(Wd, dtype=np.float32))

    pieces = []
    for g in range(NG):
        hs = [g * HPG + i for i in range(HPG)]
        for h in hs:
            blk = Wqkv[h * 384:h * 384 + 128, :]
            pieces.append(blk.reshape(128, 16, 128).transpose(2, 1, 0)
                          .reshape(128, 2048))
        for h in hs:
            blk = Wqkv[h * 384 + 128:h * 384 + 256, :]
            pieces.append(blk.reshape(128, 16, 128).transpose(2, 1, 0)
                          .reshape(128, 2048))
        vrows = np.concatenate([Wqkv[h * 384 + 256:h * 384 + 384, :]
                                for h in hs], axis=0)
        vt = vrows.reshape(HPG * 128, 16, 128).transpose(2, 1, 0)
        pieces.append(np.ascontiguousarray(vt[:, :8, :]).reshape(128, 2048))
        pieces.append(np.ascontiguousarray(vt[:, 8:, :]).reshape(128, 2048))
    wqkvT = np.ascontiguousarray(np.stack(pieces)).reshape(-1)

    wprojT = np.ascontiguousarray(Wproj.T).reshape(-1)
    wuT = np.ascontiguousarray(
        Wu.reshape(NEC, 128, 16, 128).transpose(0, 3, 2, 1)).reshape(-1)
    wgT = np.ascontiguousarray(
        Wg.reshape(NEC, 128, 16, 128).transpose(0, 3, 2, 1)).reshape(-1)
    wdT = np.ascontiguousarray(
        Wd.reshape(2048, 64, 128).transpose(1, 2, 0)).reshape(-1)

    bq = np.ascontiguousarray(
        np.stack([bqkv[h * 384:h * 384 + 128] for h in range(H)], axis=1))
    bk = np.ascontiguousarray(
        np.stack([bqkv[h * 384 + 128:h * 384 + 256] for h in range(H)],
                 axis=1))
    bv_row = np.concatenate([bqkv[h * 384 + 256:h * 384 + 384]
                             for h in range(H)])
    bv_rep = np.ascontiguousarray(
        np.broadcast_to(bv_row.reshape(NG, HPG * 128)[:, None, :],
                        (NG, 128, HPG * 128))).reshape(-1)
    bproj_rep = np.ascontiguousarray(
        np.broadcast_to(bproj[None, :], (128, E))).reshape(-1)
    return dict(wqkvT=wqkvT, wprojT=wprojT, wuT=wuT, wgT=wgT, wdT=wdT,
                bq=bq, bk=bk, bv_rep=bv_rep, bproj_rep=bproj_rep)


def kernel(L0_attn_query, L0_attn_key, L0_attn_value, L0_attn_Wqkv,
           L0_attn_bqkv, L0_attn_Wproj, L0_attn_bproj, L0_ffn_W_up,
           L0_ffn_W_gate, L0_ffn_W_down):
    x = np.asarray(L0_attn_query, dtype=np.float32)
    assert x.shape == (8, T, E)
    wdict = _prep_weights(L0_attn_Wqkv, L0_attn_bqkv, L0_attn_Wproj,
                          L0_attn_bproj, L0_ffn_W_up, L0_ffn_W_gate,
                          L0_ffn_W_down)

    if "nc" not in _NC_CACHE:
        _NC_CACHE["nc"] = build_nc()
    nc = _NC_CACHE["nc"]

    in_maps = []
    for c in range(8):
        m = dict(wdict)
        m["x"] = np.ascontiguousarray(x[c])
        in_maps.append(m)
    res = run_bass_kernel_spmd(nc, in_maps, core_ids=list(range(8)))
    out = np.stack([np.asarray(r["out"], dtype=np.float32)
                    for r in res.results], axis=0)
    return out


# version 30
# speedup vs baseline: 1.2039x; 1.0021x over previous
import numpy as np
import ml_dtypes
import concourse.bass as bass
import concourse.mybir as mybir
from concourse.bass_utils import run_bass_kernel_spmd
from contextlib import ExitStack

F32 = mybir.dt.float32
BF16 = mybir.dt.bfloat16
AF = mybir.ActivationFunctionType
ALU = mybir.AluOpType
AX = mybir.AxisListType
BFNP = ml_dtypes.bfloat16

T = 2048
E = 2048
H = 16
HD = 128
DEX = 8192
NDC = 16
NTC = 16
HPG = 2
NG = H // HPG
NQTG = 8
NEC = 64
NQ = 4
SQRT_HD = float(HD) ** 0.5
LN_EPS = 1e-5
Q_EPS = 1e-8

ENGINES = ("T", "V", "A", "S", "G")


class Ctr:
    __slots__ = ("h", "n", "name")

    def __init__(self, h, name):
        self.h = h
        self.n = 0
        self.name = name


class Prog:
    def __init__(self, nc):
        self.nc = nc
        self.ops = []

    def op(self, eng, fn, waits=(), inc=None, dma=False, r=(), w=()):
        assert eng in ENGINES
        for (c, v) in waits:
            assert v <= c.n, f"wait on {c.name}>={v} but only {c.n} emitted"
        incs = []
        if inc is not None:
            amt = 16 if dma else 1
            inc.n += amt
            incs.append((inc, amt))
        self.ops.append((eng, fn, tuple(waits), tuple(incs),
                         frozenset(r), frozenset(w)))

    def emit_engine(self, eng_obj, name):
        lastw = {}
        dirty_r = set()
        dirty_w = set()
        for (e, fn, waits, incs, r, w) in self.ops:
            if e != name:
                continue
            for (c, v) in waits:
                if v <= 0:
                    continue
                if lastw.get(id(c), -1) < v:
                    eng_obj.wait_ge(c.h, v)
                    lastw[id(c)] = v
            if name != "T" and \
                    ((r & dirty_w) or (w & dirty_w) or (w & dirty_r)):
                eng_obj.drain()
                dirty_r.clear()
                dirty_w.clear()
            inst = fn(eng_obj)
            for (c, amt) in incs:
                inst.then_inc(c.h, amt)
            dirty_r |= r
            dirty_w |= w


def _flat2d(ap_flat, off, p, cols):
    return ap_flat[off:off + p * cols].rearrange("(p c) -> p c", p=p)


def build_nc():
    nc = bass.Bass("TRN2", target_bir_lowering=False)

    x_h = nc.dram_tensor("x", [T, E], F32, kind="ExternalInput").ap()
    wqkv_h = nc.dram_tensor("wqkvT", [NG * 6 * 128 * 2048], F32, kind="ExternalInput").ap()
    wproj_h = nc.dram_tensor("wprojT", [16 * 128 * 2048], F32, kind="ExternalInput").ap()
    wu_h = nc.dram_tensor("wuT", [NEC * 128 * 2048], F32, kind="ExternalInput").ap()
    wg_h = nc.dram_tensor("wgT", [NEC * 128 * 2048], F32, kind="ExternalInput").ap()
    wd_h = nc.dram_tensor("wdT", [256 * 128 * 512], F32, kind="ExternalInput").ap()
    bq_h = nc.dram_tensor("bq", [128, H], F32, kind="ExternalInput").ap()
    bk_h = nc.dram_tensor("bk", [128, H], F32, kind="ExternalInput").ap()
    bv_h = nc.dram_tensor("bv_rep", [NG * 128 * (HPG * HD)], F32, kind="ExternalInput").ap()
    bp_h = nc.dram_tensor("bproj_rep", [128 * 2048], F32, kind="ExternalInput").ap()
    out_h = nc.dram_tensor("out", [T, E], F32, kind="ExternalOutput").ap()

    attv_scr = nc.dram_tensor("attv_scr", [H * NQTG * 128 * 256], BF16, kind="Internal").ap()
    s_scr = nc.dram_tensor("s_scr", [NTC * 128 * 2048], BF16, kind="Internal").ap()
    tu_scr = nc.dram_tensor("tu_scr", [NEC * 128 * 2048], BF16, kind="Internal").ap()
    tg_scr = nc.dram_tensor("tg_scr", [NEC * 128 * 2048], BF16, kind="Internal").ap()
    td_scr = nc.dram_tensor("td_scr", [256 * 128 * 512], BF16, kind="Internal").ap()
    tiny_scr = nc.dram_tensor("tiny_scr", [128], F32, kind="Internal").ap()
    tiny_scr2 = nc.dram_tensor("tiny_scr2", [128], F32, kind="Internal").ap()

    ident_dram = nc.inline_tensor(np.eye(128, dtype=BFNP), name="identconst")

    cursor = nc.sbuf_top

    def alloc(name, shape, dt):
        nonlocal cursor
        esz = {F32: 4, BF16: 2}[dt]
        nb = esz
        for s in shape[1:]:
            nb *= s
        nb = (nb + 31) // 32 * 32
        cursor -= nb
        return nc.alloc_sbuf_tensor_at(name, shape, dt, offset=cursor)

    xn_fm_t = alloc("xn_fm", [128, NDC * T], BF16)
    xn_fm = xn_fm_t[:, :]
    xn_v = xn_fm.rearrange("p (dc t) -> p dc t", dc=NDC)
    ident = alloc("ident", [128, 128], BF16)
    ones_row = alloc("ones_row", [1, 128], F32)
    ones_bf = alloc("ones_bf", [1, 128], BF16)
    bq_s = alloc("bq_s", [128, H], F32)
    bk_s = alloc("bk_s", [128, H], F32)
    tnames = ("st_sum0 st_sum1 st_nm0 st_nm1 st_a0 st_a1 st_b0 st_b1 "
              "st_var0 st_var1 st_rec0 st_rec1 st_rstd0 st_rstd1 "
              "ngm0 ngm1 nms0 nms1 rsum0 rsum1 rsr0 rsr1 "
              "pw_u pw_g pw_d t11u t11g t11d tmp11 tmp11b "
              "cu_p ncu_p cg_p ncg_p cd_p ncd_p sun_p sgd_p").split()
    tt_ = {nm: alloc(nm, [128, 1], F32) for nm in tnames}
    pmax_ = [alloc("pmax0", [128, 4], F32), alloc("pmax1", [128, 4], F32)]
    trow = alloc("trow", [1, 128], F32)
    brow = alloc("brow", [1, 128], F32)
    partials = alloc("partials", [128, 192], F32)

    A0 = cursor
    q_buf, k_buf, v_buf = [None, None], [None, None], [None, None]
    for i in (0, 1):
        q_buf[i] = alloc(f"q_buf{i}", [128, HPG * T], BF16)
        k_buf[i] = alloc(f"k_buf{i}", [128, HPG * T], BF16)
        v_buf[i] = alloc(f"v_buf{i}", [128, NTC * HPG * HD], BF16)
    a_p3 = cursor
    cursor = A0
    wproj_bf_t = alloc("wproj_bf", [128, NDC * E], BF16)
    wproj_bf = wproj_bf_t[:, :]
    a_p4 = cursor
    cursor = A0
    hidden_t = alloc("hidden_q", [128, NEC * 512], BF16)
    hidden_q = hidden_t[:, :]
    cursor = min(a_p3, a_p4, cursor)

    B0 = cursor
    x_tile, xn_tm = [None, None], [None, None]
    for i in (0, 1):
        x_tile[i] = alloc(f"x_tile{i}", [128, E], F32)
        xn_tm[i] = alloc(f"xn_tm{i}", [128, E], BF16)
    sq_scr = alloc("sq_scr", [128, E], BF16)
    b_p1 = cursor
    cursor = B0
    wstage, w_bf = [None, None], [None, None]
    for i in (0, 1):
        wstage[i] = alloc(f"wstage{i}", [128, 1024], F32)
    for i in (0, 1):
        w_bf[i] = alloc(f"w_bf{i}", [128, 2048], BF16)
    p_tm_b = [None, None]
    p_tm_b[0] = alloc("p_tm0", [128, 2 * T], BF16)[:, :]
    p_tm_b[1] = alloc("p_tm1", [128, 2 * T], BF16)[:, :]
    p_pt_t = alloc("p_pt", [128, NTC * 256], BF16)
    p_pt = p_pt_t[:, :]
    av_evac = [None, None]
    bv_s = [None, None]
    for i in (0, 1):
        av_evac[i] = alloc(f"av_evac{i}", [128, 256], BF16)
    for i in (0, 1):
        bv_s[i] = alloc(f"bv_s{i}", [128, HPG * HD], F32)
    b_p3 = cursor
    cursor = B0
    wstage4, attv_pair, s_tile = [None, None], [None, None], [None, None]
    for i in (0, 1):
        wstage4[i] = alloc(f"wstage4_{i}", [128, 2048], F32)
    for i in (0, 1):
        attv_pair[i] = alloc(f"attv_pair{i}", [128, H * 256], BF16)
    s_tile[0] = s_tile[1] = alloc("s_tile0", [128, 2048], BF16)
    bproj_row = alloc("bproj_row", [1, 2048], BF16)
    b_p4 = cursor
    cursor = B0
    tu_b, tg_b, td_b, silu_b = [None, None], [None, None], [None, None], [None, None]
    g_tile, xs_strip, ss_strip, out_strip = [None, None], [None, None], [None, None], [None, None]
    for i in (0, 1):
        tu_b[i] = alloc(f"tu_b{i}", [128, 2048], BF16)
        tg_b[i] = alloc(f"tg_b{i}", [128, 2048], BF16)
    td_b = [None, None, None, None]
    for i in (0, 1, 2, 3):
        td_b[i] = alloc(f"td_b{i}", [128, 1024], BF16)
    g_tile = [None, None, None, None]
    for i in (0, 1, 2, 3):
        g_tile[i] = alloc(f"g_tile{i}", [128, 512], BF16)
    for i in (0, 1):
        silu_b[i] = alloc(f"silu_b{i}", [128, 512], BF16)
        xs_strip[i] = alloc(f"xs_strip{i}", [128, 512], F32)
        ss_strip[i] = alloc(f"ss_strip{i}", [128, 512], BF16)
        out_strip[i] = alloc(f"out_strip{i}", [128, 512], F32)
    b_p6 = cursor
    cursor = min(b_p1, b_p3, b_p4, b_p6)

    qstage = [None, None]
    for i in (0, 1):
        qstage[i] = alloc(f"qstage{i}", [128, 2048], F32)
    abs_scr = alloc("abs_scr", [128, 2048], BF16)
    a_gt = abs_scr
    tq_out = [None, None]
    for i in (0, 1):
        tq_out[i] = alloc(f"tq_out{i}", [128, 2048], BF16)

    assert cursor > nc.sbuf_base + 1536, \
        f"SBUF overflow: cursor {cursor} base {nc.sbuf_base}"

    ps_big = nc.alloc_psum_tensor("ps_big", [128, 2048], F32)
    ps_pt = nc.alloc_psum_tensor("ps_pt", [128, 1024], BF16)
    ps_u = nc.alloc_psum_tensor("ps_u", [128, 512], F32)
    ps_g = nc.alloc_psum_tensor("ps_g", [128, 512], F32)
    ps_m = nc.alloc_psum_tensor("ps_m", [128, 512], F32)

    P = Prog(nc)
    op = P.op

    with ExitStack() as ctx:
        def sem(nm):
            return Ctr(ctx.enter_context(nc.semaphore(nm)), nm)

        QP = {q: [sem(f"d{q}{p}") for p in range(2)] for q in "SGA"}
        qix = {q: 0 for q in "SGA"}
        cT, cV, cA = sem("cT"), sem("cV"), sem("cA")
        cG = sem("cG")
        Z = (QP["S"][0], 0)

        def qdone(q):
            return tuple((c, c.n) for c in QP[q])

        def tmax(tokens):
            best = {}
            for (c, n) in tokens:
                if n > best.get(id(c), (None, -1))[1]:
                    best[id(c)] = (c, n)
            return tuple(best.values())

        def dma(q, dst, src, waits=(), r=(), w=()):
            ctr = QP[q][qix[q] % 2]
            qix[q] += 1
            prev = ctr.n
            op(q, lambda e, d=dst, s=src: e.dma_start(out=d, in_=s),
               waits=tuple(waits) + ((ctr, prev),), inc=ctr, dma=True,
               r=r, w=w)
            return (ctr, ctr.n)

        def vop(fn, waits=(), r=(), w=()):
            op("V", fn, waits=waits, inc=cV, r=r, w=w)
            return cV.n

        def aop(fn, waits=(), r=(), w=()):
            op("A", fn, waits=waits, inc=cA, r=r, w=w)
            return cA.n

        def top(fn, waits=(), inc=False, r=(), w=()):
            op("T", fn, waits=waits, inc=cT if inc else None, r=r, w=w)
            return cT.n

        def gop(fn, waits=(), r=(), w=()):
            op("G", fn, waits=waits, inc=cG, r=r, w=w)
            return cG.n

        dma("S", ident[:, :], ident_dram.ap()[:, :])
        dma("S", bq_s[:, :], bq_h)
        dma("S", bk_s[:, :], bk_h)
        P0_S = qdone("S")
        vop(lambda e: e.memset(ones_row[:, :], 1.0), w={"ones_row"})
        vop(lambda e: e.memset(ones_bf[:, :], 1.0), w={"ones_bf"})

        vdone_xt = [0, 0]
        adone_xt = [0, 0]
        pt_free_v = 0
        for tt in range(NTC):
            sl = tt % 2
            ss = tt % 2
            xt = f"xt{sl}"
            xt_ready = dma("G", x_tile[sl][:, :],
                           x_h[tt * 128:(tt + 1) * 128, :],
                           waits=[(cV, vdone_xt[sl]), (cA, adone_xt[sl])],
                           w={xt})
            vop(lambda e, sl=sl, ss=ss: e.reduce_sum(
                out=tt_[f"st_sum{ss}"][:, :], in_=x_tile[sl][:, :], axis=AX.X),
                waits=[xt_ready], r={xt}, w={f"st_sum{ss}"})
            a_sq = aop(lambda e, sl=sl, ss=ss: e.activation(
                out=sq_scr[:, :], in_=x_tile[sl][:, :], func=AF.Square,
                accum_out=tt_[f"st_a{ss}"][:, 0:1]),
                waits=[xt_ready], r={xt}, w={"sq_scr", f"st_a{ss}"})
            vop(lambda e, ss=ss: e.tensor_scalar(
                out=tt_[f"st_nm{ss}"][:, :], in0=tt_[f"st_sum{ss}"][:, :],
                scalar1=-1.0 / E, scalar2=None, op0=ALU.mult),
                r={f"st_sum{ss}"}, w={f"st_nm{ss}"})
            vop(lambda e, ss=ss: e.tensor_scalar(
                out=tt_[f"st_a{ss}"][:, :], in0=tt_[f"st_a{ss}"][:, :],
                scalar1=1.0 / E, scalar2=LN_EPS, op0=ALU.mult, op1=ALU.add),
                waits=[(cA, a_sq)], r={f"st_a{ss}"}, w={f"st_a{ss}"})
            vop(lambda e, ss=ss: e.tensor_tensor(
                out=tt_[f"st_b{ss}"][:, :], in0=tt_[f"st_nm{ss}"][:, :],
                in1=tt_[f"st_nm{ss}"][:, :], op=ALU.mult),
                r={f"st_nm{ss}"}, w={f"st_b{ss}"})
            vop(lambda e, ss=ss: e.tensor_sub(
                out=tt_[f"st_var{ss}"][:, :], in0=tt_[f"st_a{ss}"][:, :],
                in1=tt_[f"st_b{ss}"][:, :]),
                r={f"st_a{ss}", f"st_b{ss}"}, w={f"st_var{ss}"})
            v_rec = vop(lambda e, ss=ss: e.reciprocal(
                out=tt_[f"st_rec{ss}"][:, :], in_=tt_[f"st_var{ss}"][:, :]),
                r={f"st_var{ss}"}, w={f"st_rec{ss}"})
            a_rstd = aop(lambda e, ss=ss: e.activation(
                out=tt_[f"st_rstd{ss}"][:, :], in_=tt_[f"st_rec{ss}"][:, :],
                func=AF.Sqrt),
                waits=[(cV, v_rec)], r={f"st_rec{ss}"}, w={f"st_rstd{ss}"})
            v_xn = vop(lambda e, sl=sl, ss=ss: e.tensor_scalar(
                out=xn_tm[sl][:, :], in0=x_tile[sl][:, :],
                scalar1=tt_[f"st_nm{ss}"][:, 0:1],
                scalar2=tt_[f"st_rstd{ss}"][:, 0:1],
                op0=ALU.add, op1=ALU.mult),
                waits=[(cA, a_rstd)],
                r={xt, f"st_nm{ss}", f"st_rstd{ss}"}, w={f"xn_tm{sl}"})
            vdone_xt[sl] = v_xn
            adone_xt[sl] = a_sq
            for half in (0, 1):
                for i in range(8):
                    dc = half * 8 + i
                    top(lambda e, sl=sl, dc=dc, i=i: e.transpose(
                        out=ps_pt[:, i * 128:(i + 1) * 128],
                        in_=xn_tm[sl][:, dc * 128:(dc + 1) * 128],
                        identity=ident[:, :]),
                        waits=([(cV, v_xn), (cV, pt_free_v),
                                *P0_S] if i == 0 else ()),
                        inc=(i == 7), r={f"xn_tm{sl}", "ident"},
                        w={"bank4"})
                t_half = cT.n
                for j2 in (0, 1):
                    dc0 = half * 8 + j2 * 4
                    dst = xn_v[:, dc0:dc0 + 4, tt * 128:tt * 128 + 128]
                    pt_free_v = vop(
                        lambda e, dst=dst, j2=j2: e.tensor_copy(
                            out=dst, in_=ps_pt[:, j2 * 512:(j2 + 1) * 512]),
                        waits=[(cT, t_half)], r={"bank4"}, w={"xn_fm"})
        XN_V = cV.n
        P1_A = cA.n
        P1_T = cT.n
        bar13 = [(cV, XN_V), (cA, P1_A), (cT, P1_T)]

        a_absdone = [0, 0]
        q_vdone = [0, 0]
        g_wr = [Z, Z]
        qmode = {"load": "A", "store": "S"}
        tug_piece_done = {}
        tdq_piece_done = {}
        rt_free = [Z]

        def bcast(dst_pp, val11_name, tag):
            vb = vop(lambda e, val11_name=val11_name: e.tensor_scalar(
                out=brow[:, :], in0=ones_row[:, :],
                scalar1=tt_[val11_name][0:1, 0:1], scalar2=None, op0=ALU.mult),
                waits=[rt_free[0]],
                r={"ones_row", val11_name}, w={"brow"})
            d1 = dma("A", tiny_scr2.rearrange("(o p) -> o p", o=1),
                     brow[0:1, :], waits=[(cV, vb)], r={"brow"})
            d2 = dma("A", dst_pp[:, :],
                     tiny_scr2.rearrange("(p o) -> p o", p=128),
                     waits=[d1], w={tag})
            rt_free[0] = d2

        qloads = []
        qcomps = []
        qd_in = {}
        fins = {}
        srcs = {}
        dsts = {}
        wmeta = [("u", wu_h, tu_scr, 0), ("g", wg_h, tg_scr, 64),
                 ("d", wd_h, td_scr, 128)]
        for (wn, src, dst, pbase) in wmeta:
            def scale_load(c, wn=wn, src=src):
                sl = c % 2
                qd_in[(wn, c)] = dma(
                    "A", qstage[sl][:, :],
                    _flat2d(src, c * 262144, 128, 2048),
                    waits=[(cA, a_absdone[sl]), (cV, q_vdone[0]),
                           (cV, q_vdone[1])],
                    w={f"qstage{sl}"})

            def scale_comp(c, wn=wn, pbase=pbase):
                sl = c % 2
                a_absdone[sl] = aop(
                    lambda e, sl=sl, pc=pbase + c: e.activation(
                        out=abs_scr[:, :], in_=qstage[sl][:, :], func=AF.Abs,
                        accum_out=partials[:, pc:pc + 1]),
                    waits=[qd_in[(wn, c)]], r={f"qstage{sl}"},
                    w={"abs_scr", "partials"})
            for c in range(64):
                qloads.append(lambda c=c, f=scale_load: f(c))
                qcomps.append(lambda c=c, f=scale_comp: f(c))

            fin = {"bc_done": (), "d2": None}

            def fin_p1(wn=wn, pbase=pbase, fin=fin):
                v1 = vop(lambda e: e.reduce_sum(
                    out=tt_[f"pw_{wn}"][:, :],
                    in_=partials[:, pbase:pbase + 64], axis=AX.X),
                    waits=[(cA, a_absdone[0]), (cA, a_absdone[1])],
                    r={"partials"}, w={f"pw_{wn}"})
                d1 = dma("A", tiny_scr.rearrange("(p o) -> p o", p=128),
                         tt_[f"pw_{wn}"][:, :],
                         waits=[(cV, v1), rt_free[0]], r={f"pw_{wn}"})
                d2 = dma("A", trow[0:1, :],
                         tiny_scr.rearrange("(o p) -> o p", o=1),
                         waits=[d1], w={"trow"})
                rt_free[0] = d2
                fin["d2"] = d2

            def fin_p2(wn=wn, fin=fin):
                vop(lambda e: e.reduce_sum(
                    out=tt_[f"t11{wn}"][0:1, :], in_=trow[:, :], axis=AX.X),
                    waits=[fin["d2"]], r={"trow"}, w={f"t11{wn}"})
                vop(lambda e: e.tensor_scalar(
                    out=tt_[f"t11{wn}"][0:1, :], in0=tt_[f"t11{wn}"][0:1, :],
                    scalar1=1.0 / float(DEX * E), scalar2=Q_EPS,
                    op0=ALU.mult, op1=ALU.add),
                    r={f"t11{wn}"}, w={f"t11{wn}"})
                vop(lambda e: e.tensor_scalar(
                    out=tt_[f"tmp11"][0:1, :], in0=tt_[f"t11{wn}"][0:1, :],
                    scalar1=0.5, scalar2=None, op0=ALU.mult),
                    r={f"t11{wn}"}, w={"tmp11"})
                vop(lambda e: e.tensor_scalar(
                    out=tt_["tmp11b"][0:1, :], in0=tt_["tmp11"][0:1, :],
                    scalar1=-1.0, scalar2=None, op0=ALU.mult),
                    r={"tmp11"}, w={"tmp11b"})

            def fin_p3(wn=wn):
                bcast(tt_[f"c{wn}_p"], "tmp11", f"c{wn}_p")

            def fin_p4(wn=wn):
                bcast(tt_[f"nc{wn}_p"], "tmp11b", f"nc{wn}_p")
                if wn == "d":
                    vop(lambda e: e.tensor_tensor(
                        out=tt_["tmp11"][0:1, :], in0=tt_["t11g"][0:1, :],
                        in1=tt_["t11d"][0:1, :], op=ALU.mult),
                        r={"t11g", "t11d"}, w={"tmp11"})

            def fin_p5(wn=wn, fin=fin):
                if wn == "u":
                    bcast(tt_["sun_p"], "t11u", "sun_p")
                if wn == "d":
                    bcast(tt_["sgd_p"], "tmp11", "sgd_p")
                fin["bc_done"] = qdone("A")

            for part in (fin_p1, fin_p2, fin_p3, fin_p4, fin_p5):
                qloads.append(lambda: None)
                qcomps.append(part)
            fins[wn] = fin
            srcs[wn] = src
            dsts[wn] = dst

        def quant_load(wn, c, sl):
            qd_in[(wn + "q", c)] = dma(
                qmode["load"], qstage[sl][:, :],
                _flat2d(srcs[wn], c * 262144, 128, 2048),
                waits=[(cV, q_vdone[sl]), (cA, a_absdone[0]),
                       (cA, a_absdone[1])],
                w={f"qstage{sl}"})

        def quant_comp(wn, c, sl):
            fin = fins[wn]
            d_in = qd_in[(wn + "q", c)]
            vop(lambda e, sl=sl, wn=wn: e.tensor_scalar(
                out=tq_out[sl][:, :], in0=qstage[sl][:, :],
                scalar1=tt_[f"nc{wn}_p"][:, 0:1], scalar2=None,
                op0=ALU.is_lt),
                waits=[d_in, *fin["bc_done"], g_wr[sl]],
                r={f"qstage{sl}", f"nc{wn}_p"}, w={f"tq_out{sl}"})
            q_vdone[sl] = vop(lambda e, sl=sl, wn=wn: e.scalar_tensor_tensor(
                out=tq_out[sl][:, :], in0=qstage[sl][:, :],
                scalar=tt_[f"c{wn}_p"][:, 0:1], in1=tq_out[sl][:, :],
                op0=ALU.is_gt, op1=ALU.subtract),
                r={f"qstage{sl}", f"c{wn}_p", f"tq_out{sl}"},
                w={f"tq_out{sl}"})
            g_wr[sl] = dma(qmode["store"],
                           _flat2d(dsts[wn], c * 262144, 128, 2048),
                           tq_out[sl][:, :],
                           waits=[(cV, q_vdone[sl])], r={f"tq_out{sl}"})
            if wn == "d":
                tdq_piece_done[c] = g_wr[sl]
            else:
                tug_piece_done[(wn, c)] = g_wr[sl]

        qorder = []
        for c in range(64):
            qorder += [("u", c), ("g", c)]
        qorder += [("d", c) for c in range(64)]
        for gi, (wn, c) in enumerate(qorder):
            qloads.append(lambda wn=wn, c=c, sl=gi % 2: quant_load(wn, c, sl))
            qcomps.append(lambda wn=wn, c=c, sl=gi % 2: quant_comp(wn, c, sl))
        qload_it = iter(qloads)
        qcomp_it = iter(qcomps)
        qstate = {"loads_done": 0, "comps_done": 0,
                  "nl": len(qloads), "ncmp": len(qcomps)}

        def run_qloads(n):
            for _ in range(n):
                if qstate["loads_done"] >= qstate["comps_done"] + 2:
                    return
                stp = next(qload_it, None)
                if stp is None:
                    return
                stp()
                qstate["loads_done"] += 1

        def run_qpairs(n):
            for _ in range(n):
                run_qloads(1)
                run_qcomps(1)

        def run_qcomps(n):
            for _ in range(n):
                if qstate["comps_done"] >= qstate["loads_done"]:
                    return
                stp = next(qcomp_it, None)
                if stp is None:
                    return
                stp()
                qstate["comps_done"] += 1

        s_wstage_free = [0, 0]
        s_wbf_free = [0, 0]
        evac_ps = {0: 0, 1: 0}
        evac_ps_v = {0: 0, 1: 0}
        bv_free = [0, 0]
        groupbuf_free = [0, 0]
        attv_block_done = {}
        qcomp_per_slot = 3
        st_ = {"ptm_free_t": [0, 0], "pt_evac_v": 0, "pt_evac_a": 0,
               "ppt_free": 0, "psm_free": 0,
               "av_i": 0, "av_dma": [Z, Z], "scores_free": 0,
               "sidx": 0, "pend": None}

        def production_steps(g):
            gset = g % 2
            base = g * 6 * 128 * 2048
            piece_v = {}
            bv_ready = [Z]

            def load_bv():
                def do():
                    bv_ready[0] = dma(
                        "S", bv_s[gset][:, :],
                        _flat2d(bv_h, g * 128 * HPG * HD, 128, HPG * HD),
                        waits=bar13 + [(cV, bv_free[gset])], w={f"bv{gset}"})
                return do

            def load_piece(pi):
                def do():
                    sl = pi % 2
                    for hf in (0, 1):
                        d_in = dma(
                            "S", wstage[hf][:, :],
                            _flat2d(wqkv_h, base + pi * 262144,
                                    128, 2048)[:, hf * 1024:(hf + 1) * 1024],
                            waits=bar13 + [(cV, s_wstage_free[hf])],
                            w={f"wstage{hf}"})
                        piece_v[pi] = vop(
                            lambda e, sl=sl, hf=hf: e.tensor_copy(
                                out=w_bf[sl][:, hf * 1024:(hf + 1) * 1024],
                                in_=wstage[hf][:, :]),
                            waits=[d_in, (cT, s_wbf_free[sl])],
                            r={f"wstage{hf}"}, w={f"w_bf{sl}"})
                        s_wstage_free[hf] = piece_v[pi]
                return do

            def qk_fill(pi, which, hh, tg):
                def do():
                    sl = pi % 2
                    par = tg % 2
                    ps = ps_u if par == 0 else ps_g
                    for dc in range(NDC):
                        top(lambda e, sl=sl, dc=dc, tg=tg, ps=ps: e.matmul(
                            out=ps[:, :],
                            lhsT=w_bf[sl][:, dc * 128:(dc + 1) * 128],
                            rhs=xn_fm[:, dc * T + tg * 512:dc * T + tg * 512 + 512],
                            start=(dc == 0), stop=(dc == NDC - 1)),
                            waits=([(cV, piece_v[pi]), (cV, XN_V),
                                    (cA, evac_ps[par]),
                                    (cV, evac_ps_v[par])] if dc == 0 else ()),
                            inc=(dc == NDC - 1),
                            r={f"w_bf{pi % 2}", "xn_fm"},
                            w={f"bank{5 + par}"})
                    t_done = cT.n
                    s_wbf_free[sl] = t_done
                    dstb = q_buf if which == "q" else k_buf
                    bias = bq_s if which == "q" else bk_s
                    h_abs = g * HPG + hh
                    evac_ps[par] = aop(
                        lambda e, hh=hh, tg=tg, ps=ps, dstb=dstb,
                        bias=bias, h_abs=h_abs: e.activation(
                            out=dstb[gset][:, hh * T + tg * 512:
                                           hh * T + tg * 512 + 512],
                            in_=ps[:, :], func=AF.Identity,
                            bias=bias[:, h_abs:h_abs + 1], scale=1.0),
                        waits=[(cT, t_done), (cT, groupbuf_free[gset]),
                               *P0_S],
                        r={f"bank{5 + par}"}, w={f"qk{gset}"})
                return do

            def v_fill(tc):
                def do():
                    par = tc % 2
                    ps = ps_u if par == 0 else ps_g
                    for dc in range(NDC):
                        pi = 4 if dc < 8 else 5
                        sl = pi % 2
                        dcl = dc % 8
                        top(lambda e, sl=sl, dcl=dcl, dc=dc, tc=tc,
                            ps=ps: e.matmul(
                                out=ps[:, 0:HPG * HD],
                                lhsT=xn_fm[:, dc * T + tc * 128:
                                           dc * T + tc * 128 + 128],
                                rhs=w_bf[sl][:, dcl * (HPG * HD):
                                             (dcl + 1) * (HPG * HD)],
                                start=(dc == 0), stop=(dc == NDC - 1)),
                            waits=([(cV, piece_v[4]), (cV, piece_v[5]),
                                    (cA, evac_ps[par]),
                                    (cV, evac_ps_v[par])] if dc == 0 else ()),
                            inc=(dc == NDC - 1),
                            r={"w_bf0", "w_bf1", "xn_fm"},
                            w={f"bank{5 + par}"})
                    t_done = cT.n
                    s_wbf_free[0] = max(s_wbf_free[0], t_done)
                    s_wbf_free[1] = max(s_wbf_free[1], t_done)
                    evac_ps_v[par] = vop(
                        lambda e, tc=tc, ps=ps: e.tensor_tensor(
                            out=v_buf[gset][:, tc * (HPG * HD):
                                            (tc + 1) * (HPG * HD)],
                            in0=ps[:, 0:HPG * HD], in1=bv_s[gset][:, :],
                            op=ALU.add),
                        waits=[(cT, t_done), (cT, groupbuf_free[gset]),
                               bv_ready[0]],
                        r={f"bv{gset}", f"bank{5 + par}"}, w={f"vb{gset}"})
                return do

            steps = [load_bv(), load_piece(0), load_piece(1)]
            for tg in range(4):
                steps.append(qk_fill(0, "q", 0, tg))
            steps.append(load_piece(2))
            for tg in range(4):
                steps.append(qk_fill(1, "q", 1, tg))
            steps.append(load_piece(3))
            for tg in range(4):
                steps.append(qk_fill(2, "k", 0, tg))
            steps.append(load_piece(4))
            for tg in range(4):
                steps.append(qk_fill(3, "k", 1, tg))
            steps.append(load_piece(5))
            for tc in range(NTC):
                steps.append(v_fill(tc))
            return steps

        def softmax_qt(g, hh, qtg, sidx, qt):
            gset = g % 2
            pb = p_tm_b[sidx % 2]
            if True:
                q0 = qtg * 256 + qt * 128
                ss = qt
                for kc4 in range(4):
                    t_sc = top(lambda e, gset=gset, hh=hh, q0=q0,
                               kc4=kc4: e.matmul(
                        out=ps_big[:, kc4 * 512:(kc4 + 1) * 512],
                        lhsT=q_buf[gset][:, hh * T + q0:hh * T + q0 + 128],
                        rhs=k_buf[gset][:, hh * T + kc4 * 512:
                                        hh * T + kc4 * 512 + 512],
                        start=True, stop=True),
                        waits=([(cA, st_["scores_free"])] if kc4 == 0 else ()),
                        inc=(kc4 == 3), r={f"qk{gset}"}, w={f"bank{kc4}"})
                vop(lambda e, ss=ss: e.reduce_max(
                    out=tt_[f"ngm{ss}"][:, :], in_=ps_big[:, :], axis=AX.X,
                    negate=True),
                    waits=[(cT, t_sc)], r={"bank0", "bank1", "bank2", "bank3"}, w={f"ngm{ss}"})
                v_nms = vop(lambda e, ss=ss: e.tensor_scalar(
                    out=tt_[f"nms{ss}"][:, :], in0=tt_[f"ngm{ss}"][:, :],
                    scalar1=SQRT_HD, scalar2=None, op0=ALU.mult),
                    r={f"ngm{ss}"}, w={f"nms{ss}"})
                st_["scores_free"] = aop(lambda e, pb=pb, qt=qt,
                                         ss=ss: e.activation(
                    out=pb[:, qt * T:(qt + 1) * T], in_=ps_big[:, :],
                    func=AF.Exp, bias=tt_[f"nms{ss}"][:, 0:1], scale=SQRT_HD,
                    accum_out=tt_[f"rsum{ss}"][:, 0:1]),
                    waits=[(cV, v_nms), (cT, st_["ptm_free_t"][sidx % 2])],
                    r={f"nms{ss}", "bank0", "bank1", "bank2", "bank3"},
                    w={f"p_tm{sidx % 2}", f"rsum{ss}"})
                vop(lambda e, ss=ss: e.reciprocal(
                    out=tt_[f"rsr{ss}"][:, :], in_=tt_[f"rsum{ss}"][:, :]),
                    waits=[(cA, st_["scores_free"])],
                    r={f"rsum{ss}"}, w={f"rsr{ss}"})
                v_norm0 = vop(lambda e, pb=pb, qt=qt, ss=ss: e.tensor_scalar(
                    out=pb[:, qt * T:(qt + 1) * T],
                    in0=pb[:, qt * T:(qt + 1) * T],
                    scalar1=tt_[f"rsr{ss}"][:, 0:1], scalar2=None,
                    op0=ALU.mult),
                    r={f"rsr{ss}", f"p_tm{sidx % 2}"},
                    w={f"p_tm{sidx % 2}"})
            return v_norm0

        def attention_tail_head(ctx):
            (g, hh, qtg, sidx, v_norm) = ctx
            pb = p_tm_b[sidx % 2]
            ev_v = st_["pt_evac_v"]
            ev_a = st_["pt_evac_a"]
            for rnd in range(4):
                for s_i in range(8):
                    kc = rnd * 4 + s_i // 2
                    qt = s_i % 2
                    top(lambda e, pb=pb, kc=kc, qt=qt, s_i=s_i: e.transpose(
                        out=ps_pt[:, s_i * 128:(s_i + 1) * 128],
                        in_=pb[:, qt * T + kc * 128:qt * T + kc * 128 + 128],
                        identity=ident[:, :]),
                        waits=([(cV, v_norm), (cV, ev_v), (cA, ev_a)]
                               if s_i == 0 else ()),
                        inc=(s_i == 7), r={f"p_tm{sidx % 2}", "ident"},
                        w={"bank4"})
                t_tr = cT.n
                dstv = p_pt.rearrange("p (kc q t) -> p kc q t", kc=NTC, q=2)
                if rnd % 2 == 0:
                    ev_v = vop(lambda e, rnd=rnd, dstv=dstv: e.tensor_copy(
                        out=dstv[:, rnd * 4:(rnd + 1) * 4, :, :],
                        in_=ps_pt[:, :]),
                        waits=[(cT, t_tr), (cT, st_["ppt_free"])],
                        r={"bank4"}, w={"p_pt"})
                else:
                    ev_a = aop(lambda e, rnd=rnd, dstv=dstv: e.activation(
                        out=dstv[:, rnd * 4:(rnd + 1) * 4, :, :],
                        in_=ps_pt[:, :], func=AF.Copy),
                        waits=[(cT, t_tr), (cT, st_["ppt_free"])],
                        r={"bank4"}, w={"p_pt"})
            st_["pt_evac_v"] = ev_v
            st_["pt_evac_a"] = ev_a
            st_["ptm_free_t"][sidx % 2] = cT.n

        def attention_tail_attv(ctx):
            (g, hh, qtg, sidx, v_norm) = ctx
            gset = g % 2
            h_abs = g * HPG + hh
            ev_v = st_["pt_evac_v"]
            ev_a = st_["pt_evac_a"]
            for kc in range(NTC):
                top(lambda e, gset=gset, hh=hh, kc=kc: e.matmul(
                    out=ps_m[:, 0:256],
                    lhsT=v_buf[gset][:, kc * 256 + hh * 128:
                                     kc * 256 + hh * 128 + 128],
                    rhs=p_pt[:, kc * 256:(kc + 1) * 256],
                    start=(kc == 0), stop=(kc == NTC - 1)),
                    waits=([(cV, ev_v), (cA, ev_a),
                            (cA, st_["psm_free"])]
                           if kc == 0 else ()),
                    inc=(kc == NTC - 1), r={f"vb{gset}", "p_pt"},
                    w={"bank7"})
            t_av = cT.n
            st_["ppt_free"] = t_av
            avsl = st_["av_i"] % 2
            v_ave = aop(lambda e, avsl=avsl: e.activation(
                out=av_evac[avsl][:, :], in_=ps_m[:, 0:256], func=AF.Copy),
                waits=[(cT, t_av), st_["av_dma"][avsl]],
                r={"bank7"}, w={f"av{avsl}"})
            st_["psm_free"] = v_ave
            avdst = attv_scr.rearrange("(q p h t) -> q p h t", q=NQTG,
                                       p=128, h=H)[qtg, :, h_abs, :]
            st_["av_dma"][avsl] = dma(
                "G", avdst,
                av_evac[avsl][:, :], waits=[(cA, v_ave)], r={f"av{avsl}"})
            attv_block_done[(h_abs, qtg)] = st_["av_dma"][avsl]
            st_["av_i"] += 1
            if (hh, qtg) == (HPG - 1, NQTG - 1):
                groupbuf_free[g % 2] = cT.n

        prev_g = None
        nslots_total = NG * HPG * NQTG
        qper_slot = 3
        for g in range(NG + 1):
            steps = production_steps(g) if g < NG else []
            if prev_g is None:
                for stp in steps:
                    stp()
            else:
                slots = [(hh, qtg) for hh in range(HPG)
                         for qtg in range(NQTG)]
                per = (len(steps) + len(slots) - 2) // max(len(slots) - 1, 1)
                si = 0
                for slot_i, (hh, qtg) in enumerate(slots):
                    if slot_i > 0:
                        for _ in range(per):
                            if si < len(steps):
                                steps[si]()
                                si += 1
                    run_qloads(2)
                    softmax_qt(prev_g, hh, qtg, st_["sidx"], 0)
                    if st_["pend"] is not None:
                        attention_tail_head(st_["pend"])
                    vn1 = softmax_qt(prev_g, hh, qtg, st_["sidx"], 1)
                    ctx = (prev_g, hh, qtg, st_["sidx"], vn1)
                    st_["sidx"] += 1
                    run_qcomps(2)
                    if st_["pend"] is not None:
                        attention_tail_attv(st_["pend"])
                    st_["pend"] = ctx
                    run_qpairs(1)
                while si < len(steps):
                    steps[si]()
                    si += 1
                bv_free[prev_g % 2] = cV.n
            prev_g = g if g < NG else prev_g
        if st_["pend"] is not None:
            attention_tail_head(st_["pend"])
            attention_tail_attv(st_["pend"])
            st_["pend"] = None
        P3_T, P3_V, P3_A = cT.n, cV.n, cA.n
        P3_G = cG.n

        qmode["load"] = "S"
        qmode["store"] = "G"
        bar34 = [(cT, P3_T), (cV, P3_V), (cA, P3_A), *qdone("G")]
        bp_stage = dma("S", wstage4[0][0:1, :],
                       bp_h[0:2048].rearrange("(o c) -> o c", o=1),
                       waits=bar34, w={"wp0", "wp1", "wp2", "wp3"})
        bp_cast = vop(lambda e: e.tensor_copy(
            out=bproj_row[0:1, :], in_=wstage4[0][0:1, :]),
            waits=[bp_stage, (cT, P3_T)],
            r={"wp0", "wp1", "wp2", "wp3"}, w={"bproj_row"})
        wp_free = [bp_cast] * 4 + [0] * 4
        wp_cast = 0
        wpv_eg = []
        wpi = 0
        for eg in range(4):
            for dc in range(NDC):
                s = wpi % 8
                wpi += 1
                buf = wstage4[s // 4][:, (s % 4) * 512:(s % 4 + 1) * 512]
                d_in = dma("S", buf,
                           _flat2d(wproj_h, dc * 262144, 128, 2048)[
                               :, eg * 512:(eg + 1) * 512],
                           waits=bar34 + [(cV, wp_free[s])],
                           w={f"wp{s}"})
                wp_cast = vop(lambda e, buf=buf, dc=dc, eg=eg: e.tensor_copy(
                    out=wproj_bf[:, dc * T + eg * 512:dc * T + eg * 512 + 512],
                    in_=buf),
                    waits=[d_in, (cT, P3_T)],
                    r={f"wp{s}"}, w={f"wproj_q{eg}"})
                wp_free[s] = wp_cast
            wpv_eg.append(wp_cast)
        WP_V = wp_cast

        s_dma = [Z, Z]
        s_wr = {}
        pair_T = [P3_T, P3_T]
        psum_free_a = [P3_A, P3_A, P3_A, P3_A]
        for pair in range(NQTG):
            psl = pair % 2
            blk = tmax(attv_block_done[(h, pair)] for h in range(H))
            last_d = dma(
                "G", attv_pair[psl][:, :],
                _flat2d(attv_scr, pair * 128 * H * 256, 128, H * 256),
                waits=[*blk, (cT, pair_T[psl]), (cV, P3_V)],
                w={f"avp{psl}"})
            run_qpairs(10)
            for tcl in (0, 1):
                tc = pair * 2 + tcl
                ssl = 0
                for eg in range(4):
                    bank = (tc * 4 + eg) % 4
                    top(lambda e, eg=eg, bank=bank: e.matmul(
                        out=ps_big[:, bank * 512:(bank + 1) * 512],
                        lhsT=ones_bf[0:1, :],
                        rhs=bproj_row[0:1, eg * 512:(eg + 1) * 512],
                        start=True, stop=False),
                        waits=[last_d, (cV, wpv_eg[eg]),
                               (cA, psum_free_a[bank]), (cA, P3_A)],
                        r={"ones_bf", "bproj_row"}, w={f"bank{bank}"})
                    for dc in range(NDC):
                        top(lambda e, psl=psl, tcl=tcl, eg=eg, dc=dc,
                            bank=bank: e.matmul(
                                out=ps_big[:, bank * 512:(bank + 1) * 512],
                                lhsT=attv_pair[psl][:, dc * 256 + tcl * 128:
                                                    dc * 256 + tcl * 128 + 128],
                                rhs=wproj_bf[:, dc * T + eg * 512:
                                             dc * T + eg * 512 + 512],
                                start=False, stop=(dc == NDC - 1)),
                            inc=(dc == NDC - 1),
                            r={f"avp{psl}", f"wproj_q{eg}"}, w={f"bank{bank}"})
                    t_p = cT.n
                    psum_free_a[bank] = aop(
                        lambda e, ssl=ssl, eg=eg, bank=bank: e.activation(
                            out=s_tile[ssl][:, eg * 512:(eg + 1) * 512],
                            in_=ps_big[:, bank * 512:(bank + 1) * 512],
                            func=AF.Copy),
                        waits=[(cT, t_p), s_dma[ssl]],
                        r={f"bank{bank}"}, w={f"s_tile{ssl}"})
                s_dma[ssl] = dma(
                    "G", _flat2d(s_scr, tc * 262144, 128, 2048),
                    s_tile[ssl][:, :],
                    waits=[(cA, psum_free_a[(tc * 4 + 3) % 4])],
                    r={f"s_tile{ssl}"})
                s_wr[tc] = s_dma[ssl]
            pair_T[psl] = cT.n
        P4_T, P4_V, P4_A = cT.n, cV.n, cA.n
        P4_G = qdone("G")

        qmode["load"] = "G"
        qmode["store"] = "G"
        bar46 = [(cT, P4_T), (cV, WP_V), *P4_G]
        tu_dma = [Z, Z]
        tg_dma = [Z, Z]
        tu_cons = [0, 0]
        td_dma = [None, None, None, None]
        td_cons = [0, 0, 0, 0]
        silu_free = [0, 0]
        bankg = [(cA, psum_free_a[0]), (cA, psum_free_a[1]),
                 (cA, psum_free_a[2]), (cA, psum_free_a[3]),
                 (cA, P3_A), (cA, P3_A), (cA, P3_A), (cA, P3_A)]
        gx_dma = [Z, Z]
        gs_dma = [Z, Z]
        gt_cons = [0, 0, 0, 0]
        strip_v = [0, 0]
        out_dma = [Z, Z]
        psf32 = ps_pt[:, :].bitcast(F32)

        def bank(b):
            if b < 4:
                return ps_big[:, b * 512:(b + 1) * 512]
            return [psf32, ps_u, ps_g, ps_m][b - 4][:, 0:512] \
                if b > 4 else psf32[:, 0:512]

        while qstate["comps_done"] < qstate["ncmp"] and False:
            pass
        first6 = True
        for Q in range(NQ):
            for ec in range(NEC):
                sl = ec % 2
                run_qpairs(2)
                tu_dma[sl] = dma("S", tu_b[sl][:, :],
                                 _flat2d(tu_scr, ec * 262144, 128, 2048),
                                 waits=(bar46 if first6 else []) +
                                 [tug_piece_done[("u", ec)],
                                  (cT, tu_cons[sl])],
                                 w={f"tu{sl}"})
                tg_dma[sl] = dma("A", tg_b[sl][:, :],
                                 _flat2d(tg_scr, ec * 262144, 128, 2048),
                                 waits=(bar46 if first6 else []) +
                                 [tug_piece_done[("g", ec)],
                                  (cT, tu_cons[sl])],
                                 w={f"tg{sl}"})
                first6 = False
                par = ec % 2
                upb = ps_big[:, par * 512:(par + 1) * 512]
                gtb = ps_big[:, (2 + par) * 512:(3 + par) * 512]
                for dc in range(NDC):
                    top(lambda e, sl=sl, dc=dc, Q=Q, upb=upb: e.matmul(
                        out=upb,
                        lhsT=tu_b[sl][:, dc * 128:(dc + 1) * 128],
                        rhs=xn_fm[:, dc * T + Q * 512:dc * T + Q * 512 + 512],
                        start=(dc == 0), stop=(dc == NDC - 1)),
                        waits=([tu_dma[sl], bankg[par]] if dc == 0 else ()),
                        inc=(dc == NDC - 1), r={f"tu{sl}", "xn_fm"},
                        w={f"bank{par}"})
                t_up = cT.n
                for dc in range(NDC):
                    top(lambda e, sl=sl, dc=dc, Q=Q, gtb=gtb: e.matmul(
                        out=gtb,
                        lhsT=tg_b[sl][:, dc * 128:(dc + 1) * 128],
                        rhs=xn_fm[:, dc * T + Q * 512:dc * T + Q * 512 + 512],
                        start=(dc == 0), stop=(dc == NDC - 1)),
                        waits=([tg_dma[sl], bankg[2 + par]] if dc == 0
                               else ()),
                        inc=(dc == NDC - 1), r={f"tg{sl}", "xn_fm"},
                        w={f"bank{2 + par}"})
                t_gt = cT.n
                tu_cons[sl] = t_gt
                ssl = ec % 2
                a_silu = aop(lambda e, ssl=ssl, upb=upb: e.activation(
                    out=silu_b[ssl][:, :], in_=upb, func=AF.Silu,
                    bias=0.0, scale=tt_["sun_p"][:, 0:1]),
                    waits=[(cT, t_up), (cV, silu_free[ssl])],
                    r={"sun_p", f"bank{par}"}, w={f"silu{ssl}"})
                bankg[par] = (cA, a_silu)
                silu_free[ssl] = vop(
                    lambda e, ssl=ssl, ec=ec, gtb=gtb: e.scalar_tensor_tensor(
                        out=hidden_q[:, ec * 512:(ec + 1) * 512],
                        in0=gtb, scalar=0.0, in1=silu_b[ssl][:, :],
                        op0=ALU.add, op1=ALU.mult),
                    waits=[(cT, t_gt), (cA, a_silu)],
                    r={f"silu{ssl}", f"bank{2 + par}"}, w={"hidden_q"})
                bankg[2 + par] = (cV, silu_free[ssl])
            HID_V = cV.n
            for dmh in range(2):
                for ec in range(NEC):
                    sl = ec % 4
                    if ec % 8 == 0:
                        run_qloads(1)
                        run_qcomps(1)
                    qq = "S" if sl % 2 == 0 else "A"
                    td_dma[sl] = dma(
                        qq, td_b[sl][:, :],
                        _flat2d(td_scr, ec * 262144, 128, 2048)[
                            :, dmh * 1024:(dmh + 1) * 1024],
                        waits=[tdq_piece_done[ec],
                               (cT, td_cons[sl]), (cT, P4_T)],
                        w={f"td{sl}"})
                    for tc in range(4):
                        for d2 in range(2):
                            b = tc * 2 + d2
                            wl = []
                            if tc == 0 and d2 == 0:
                                wl.append(td_dma[sl])
                            if ec == 0:
                                wl += [(cV, HID_V), bankg[b]]
                            top(lambda e, sl=sl, ec=ec, tc=tc, d2=d2,
                                b=b: e.matmul(
                                    out=bank(b),
                                    lhsT=hidden_q[:, ec * 512 + tc * 128:
                                                  ec * 512 + tc * 128 + 128],
                                    rhs=td_b[sl][:, d2 * 512:(d2 + 1) * 512],
                                    start=(ec == 0), stop=(ec == NEC - 1)),
                                waits=wl,
                                inc=(b == 7), r={f"td{sl}", "hidden_q"})
                    td_cons[sl] = cT.n
                t_dn = cT.n
                for tc in range(4):
                    for d2 in range(2):
                        b = tc * 2 + d2
                        dmg = dmh * 2 + d2
                        gsl = b % 2
                        tglob = Q * 4 + tc
                        gx_dma[gsl] = dma(
                            "G", xs_strip[gsl][:, :],
                            x_h[tglob * 128:(tglob + 1) * 128,
                                dmg * 512:(dmg + 1) * 512],
                            waits=[(cV, strip_v[gsl]), (cT, P4_T)],
                            w={f"xs{gsl}"})
                        gs_dma[gsl] = dma(
                            "S", ss_strip[gsl][:, :],
                            _flat2d(s_scr, tglob * 262144, 128, 2048)[
                                :, dmg * 512:(dmg + 1) * 512],
                            waits=[(cV, strip_v[gsl]), s_wr[tglob],
                                   (cT, P4_T)],
                            w={f"ssb{gsl}"})
                        g4 = b % 4
                        a_g = aop(lambda e, g4=g4, b=b: e.activation(
                            out=g_tile[g4][:, :],
                            in_=bank(b),
                            func=AF.Silu, bias=0.0,
                            scale=tt_["sgd_p"][:, 0:1]),
                            waits=[(cT, t_dn), (cV, gt_cons[g4])],
                            r={"sgd_p", f"bank{b}"}, w={f"g{g4}"})
                        bankg[b] = (cA, a_g)
                        v_m = vop(lambda e, gsl=gsl, g4=g4: e.tensor_tensor(
                            out=out_strip[gsl][:, :],
                            in0=ss_strip[gsl][:, :],
                            in1=g_tile[g4][:, :], op=ALU.mult),
                            waits=[(cA, a_g), gs_dma[gsl],
                                   out_dma[gsl]],
                            r={f"ssb{gsl}", f"g{g4}"}, w={f"os{gsl}"})
                        gt_cons[g4] = v_m
                        v_o = vop(lambda e, gsl=gsl: e.tensor_tensor(
                            out=out_strip[gsl][:, :],
                            in0=out_strip[gsl][:, :],
                            in1=xs_strip[gsl][:, :], op=ALU.add),
                            waits=[gx_dma[gsl]],
                            r={f"os{gsl}", f"xs{gsl}"}, w={f"os{gsl}"})
                        strip_v[gsl] = v_o
                        out_dma[gsl] = dma(
                            "G", out_h[tglob * 128:(tglob + 1) * 128,
                                       dmg * 512:(dmg + 1) * 512],
                            out_strip[gsl][:, :],
                            waits=[(cV, v_o)], r={f"os{gsl}"})

        op("G", lambda e: e.nop(),
           waits=list(qdone("G")) + list(qdone("S")) + list(qdone("A")))

        with nc.Block() as block:
            block.tensor(lambda e: P.emit_engine(e, "T"))
            block.vector(lambda e: P.emit_engine(e, "V"))
            block.scalar(lambda e: P.emit_engine(e, "A"))
            block.sync(lambda e: P.emit_engine(e, "S"))
            block.gpsimd(lambda e: P.emit_engine(e, "G"))

    return nc


_NC_CACHE = {}


def _prep_weights(Wqkv, bqkv, Wproj, bproj, Wu, Wg, Wd):
    Wqkv = np.ascontiguousarray(np.asarray(Wqkv, dtype=np.float32))
    bqkv = np.ascontiguousarray(np.asarray(bqkv, dtype=np.float32))
    Wproj = np.ascontiguousarray(np.asarray(Wproj, dtype=np.float32))
    bproj = np.ascontiguousarray(np.asarray(bproj, dtype=np.float32))
    Wu = np.ascontiguousarray(np.asarray(Wu, dtype=np.float32))
    Wg = np.ascontiguousarray(np.asarray(Wg, dtype=np.float32))
    Wd = np.ascontiguousarray(np.asarray(Wd, dtype=np.float32))

    pieces = []
    for g in range(NG):
        hs = [g * HPG + i for i in range(HPG)]
        for h in hs:
            blk = Wqkv[h * 384:h * 384 + 128, :]
            pieces.append(blk.reshape(128, 16, 128).transpose(2, 1, 0)
                          .reshape(128, 2048))
        for h in hs:
            blk = Wqkv[h * 384 + 128:h * 384 + 256, :]
            pieces.append(blk.reshape(128, 16, 128).transpose(2, 1, 0)
                          .reshape(128, 2048))
        vrows = np.concatenate([Wqkv[h * 384 + 256:h * 384 + 384, :]
                                for h in hs], axis=0)
        vt = vrows.reshape(HPG * 128, 16, 128).transpose(2, 1, 0)
        pieces.append(np.ascontiguousarray(vt[:, :8, :]).reshape(128, 2048))
        pieces.append(np.ascontiguousarray(vt[:, 8:, :]).reshape(128, 2048))
    wqkvT = np.ascontiguousarray(np.stack(pieces)).reshape(-1)

    wprojT = np.ascontiguousarray(Wproj.T).reshape(-1)
    wuT = np.ascontiguousarray(
        Wu.reshape(NEC, 128, 16, 128).transpose(0, 3, 2, 1)).reshape(-1)
    wgT = np.ascontiguousarray(
        Wg.reshape(NEC, 128, 16, 128).transpose(0, 3, 2, 1)).reshape(-1)
    wdT = np.ascontiguousarray(
        Wd.reshape(2048, 64, 128).transpose(1, 2, 0)).reshape(-1)

    bq = np.ascontiguousarray(
        np.stack([bqkv[h * 384:h * 384 + 128] for h in range(H)], axis=1))
    bk = np.ascontiguousarray(
        np.stack([bqkv[h * 384 + 128:h * 384 + 256] for h in range(H)],
                 axis=1))
    bv_row = np.concatenate([bqkv[h * 384 + 256:h * 384 + 384]
                             for h in range(H)])
    bv_rep = np.ascontiguousarray(
        np.broadcast_to(bv_row.reshape(NG, HPG * 128)[:, None, :],
                        (NG, 128, HPG * 128))).reshape(-1)
    bproj_rep = np.ascontiguousarray(
        np.broadcast_to(bproj[None, :], (128, E))).reshape(-1)
    return dict(wqkvT=wqkvT, wprojT=wprojT, wuT=wuT, wgT=wgT, wdT=wdT,
                bq=bq, bk=bk, bv_rep=bv_rep, bproj_rep=bproj_rep)


def kernel(L0_attn_query, L0_attn_key, L0_attn_value, L0_attn_Wqkv,
           L0_attn_bqkv, L0_attn_Wproj, L0_attn_bproj, L0_ffn_W_up,
           L0_ffn_W_gate, L0_ffn_W_down):
    x = np.asarray(L0_attn_query, dtype=np.float32)
    assert x.shape == (8, T, E)
    wdict = _prep_weights(L0_attn_Wqkv, L0_attn_bqkv, L0_attn_Wproj,
                          L0_attn_bproj, L0_ffn_W_up, L0_ffn_W_gate,
                          L0_ffn_W_down)

    if "nc" not in _NC_CACHE:
        _NC_CACHE["nc"] = build_nc()
    nc = _NC_CACHE["nc"]

    in_maps = []
    for c in range(8):
        m = dict(wdict)
        m["x"] = np.ascontiguousarray(x[c])
        in_maps.append(m)
    res = run_bass_kernel_spmd(nc, in_maps, core_ids=list(range(8)))
    out = np.stack([np.asarray(r["out"], dtype=np.float32)
                    for r in res.results], axis=0)
    return out
